# version 11
# baseline (speedup 1.0000x reference)
import numpy as np
from contextlib import ExitStack

import concourse.bass as bass
import concourse.mybir as mybir

F32 = mybir.dt.float32
BF16 = mybir.dt.bfloat16
I32 = mybir.dt.int32
ALU = mybir.AluOpType
ACT = mybir.ActivationFunctionType
AX = mybir.AxisListType





ENGINES = ("sync", "tensor", "vector", "scalar", "gpsimd")


class Trk:
    def __init__(self, nc, ctx):
        self.nc = nc
        self.eng = {e: getattr(nc, e) for e in ENGINES}
        self.sem = {}
        self.cnt = {}
        keys = ([(e, "c") for e in ENGINES] + [(e, "d") for e in ENGINES]
                + [("gpsimd", "x")])
        for k in keys:
            self.sem[k] = ctx.enter_context(nc.semaphore(f"trk_{k[0]}_{k[1]}"))
            self.cnt[k] = 0
        self.last_w = {}
        self.readers = {}
        self.seen = {e: {k: 0 for k in keys} for e in ENGINES}
        self.flushed = {}
        self.nwaits = 0
        self.nrelays = 0
        self.nops = 0

    def _wait(self, e, pk, val):
        if val <= 0 or self.seen[e][pk] >= val:
            return
        self.eng[e].wait_ge(self.sem[pk], val)
        self.seen[e][pk] = val
        self.nwaits += 1

    def _relay_k(self, pk, v):
        fl = self.flushed.setdefault(pk, (0, 0))
        if fl[0] >= v:
            return fl[1]
        pe = pk[0]
        total = self.cnt[pk]
        self.eng[pe].wait_ge(self.sem[pk], total)
        self.seen[pe][pk] = total
        self.cnt[(pe, "c")] += 1
        cv = self.cnt[(pe, "c")]
        self.eng[pe].sem_inc(self.sem[(pe, "c")], 1)
        self.flushed[pk] = (total, cv)
        self.nrelays += 1
        return cv

    def emit(self, e, fn, reads=(), writes=(), dma=False, cc=False, inc_extra=None):
        need = {}

        def add(pk, v):
            need[pk] = max(need.get(pk, 0), v)

        for r in reads:
            lw = self.last_w.get(r)
            if lw:
                add(*lw)
        for w in writes:
            lw = self.last_w.get(w)
            if lw:
                add(*lw)
            for pk, v in self.readers.get(w, {}).items():
                add(pk, v)
        for pk, v in sorted(need.items(), key=lambda kv: str(kv[0])):
            pe, kind = pk
            if kind in ("d", "x"):
                if pe == e:
                    self._wait(e, pk, self.cnt[pk])
                else:
                    cv = self._relay_k(pk, v)
                    self._wait(e, (pe, "c"), cv)
            else:
                self._wait(e, pk, v)
        inst = fn(self.eng[e])
        k = (e, "d") if dma else (("gpsimd", "x") if cc else (e, "c"))
        step = 16 if dma else 1
        self.cnt[k] += step
        inst.then_inc(self.sem[k], step)
        if inc_extra is not None:
            inst.then_inc(*inc_extra)
        myv = self.cnt[k]
        for w in writes:
            self.last_w[w] = (k, myv)
            self.readers[w] = {}
        for r in reads:
            self.readers.setdefault(r, {})[k] = myv
        self.nops += 1
        return inst

    def snapshot(self):
        return dict(self.cnt)

    def wait_snapshot(self, e, snap):
        for pk, v in snap.items():
            pe, kind = pk
            if kind in ("d", "x") and pe != e:
                cv = self._relay_k(pk, v)
                self._wait(e, (pe, "c"), cv)
            else:
                self._wait(e, pk, v)


def full_cfg():
    return dict(TC=2048, D=1024, H=16, HD=64, C=64, DFF=4096, C2=2048,
                DQ=3072, NST=16, DTR=64, DW=64, DA=64, DG=128, NCORES=8)


def tiny_cfg():
    return dict(TC=256, D=256, H=4, HD=64, C=64, DFF=512, C2=512,
                DQ=384, NST=16, DTR=64, DW=64, DA=64, DG=128, NCORES=8)


def tiny8_cfg():
    return dict(TC=128, D=1024, H=16, HD=64, C=64, DFF=512, C2=2048,
                DQ=384, NST=16, DTR=64, DW=64, DA=64, DG=128, NCORES=8)


class St:
    pass


def segs(total, step, start=0):
    out = []
    a = start
    while a < total:
        out.append((a, min(a + step, total)))
        a += step
    return out


def build(cfg, taps=(), upto="all"):
    TC = cfg["TC"]; D = cfg["D"]; NPB = D // 128
    DFF = cfg["DFF"]; NFB = DFF // 128
    XW = TC + 2
    HW = ((XW + 7) // 8) * 8
    nc = bass.Bass("TRN2", target_bir_lowering=False, debug=False)

    s = St()
    s.cfg = cfg; s.nc = nc; s.taps = set(taps); s.upto = upto
    s.NPB = NPB; s.NFB = NFB; s.XW = XW; s.HW = HW
    s.TC = TC; s.D = D; s.DFF = DFF

    def din(name, shape, dt=F32):
        ap = nc.dram_tensor(name, list(shape), dt, kind="ExternalInput").ap()
        setattr(s, name, ap)
        return ap
    s.din = din

    din("xT", (D, XW))
    din("vmix", (128, 6 * NPB))
    din("vw0", (128, NPB)); din("va0", (128, NPB))
    din("vkk", (128, NPB)); din("vka", (128, NPB))
    din("vlnw", (128, NPB)); din("vlnb", (128, NPB)); din("vrk", (128, NPB))
    din("f0w1t", (D, DFF)); din("f0w2t", (DFF, D))
    din("f0cw", (128, 3 * NFB))
    din("f0cb", (128, NFB))
    din("ident", (128, 128))
    din("wr", (D, D)); din("wk", (D, D)); din("wv", (D, D)); din("wo", (D, D))
    din("lw1", (D, cfg["DW"])); din("lw2", (cfg["DW"], D))
    din("la1", (D, cfg["DA"])); din("la2", (cfg["DA"], D))
    din("lg1", (D, cfg["DG"])); din("lg2", (cfg["DG"], D))
    din("hsum2", (128, 2))
    din("hexp2", (2, 128))
    din("rstpat", (128, TC))
    din("pfxm64", (128, cfg["NCORES"]))
    din("mskLR", (128, 2 * cfg["NCORES"]))
    din("psum64", (128, 72))
    DSL = D // cfg["NCORES"]
    s.DSL = DSL
    din("vD", (128, 1))
    din("alog_lanes", (128, max(DSL * 16 // 128, 1)))
    din("cw1", (128, 4 * (cfg["C2"] // 128)))
    din("cb1", (128, cfg["C2"] // 128))
    din("xprojt", (cfg["C2"], 96))
    din("dtprojt", (cfg["DTR"], D))
    din("vdtb", (128, NPB))
    din("qg", (D, cfg["DQ"]), I32)
    din("qu", (D, cfg["DQ"]), I32)
    din("qd", (cfg["DQ"], D), I32)
    C = cfg["C"]
    din("m_stril", (128, C))
    din("m_tril", (128, C))
    s.out = nc.dram_tensor("out", [D, TC], F32, kind="ExternalOutput").ap()

    s.tap_out = {}

    def dtap(name, shape, dt=F32):
        if name in s.taps and name not in s.tap_out:
            s.tap_out[name] = nc.dram_tensor("tap_" + name, list(shape), dt,
                                             kind="ExternalOutput").ap()
        return s.tap_out.get(name)
    s.dtap = dtap

    def dscr(name, shape, dt):
        ap = nc.dram_tensor(name, list(shape), dt).ap()
        setattr(s, name, ap)
        return ap
    s.dscr = dscr

    H = cfg["H"]
    NCH = TC // cfg["C"]
    s.H = H; s.NCH = NCH; s.C = cfg["C"]
    NCR = cfg["NCORES"]
    s.NCR = NCR
    dscr("s_r", (D, TC), BF16); dscr("s_kmod", (D, TC), BF16)
    dscr("s_v", (D, TC), BF16); dscr("s_kap", (D, TC), BF16)
    dscr("s_ab", (D, TC), BF16); dscr("s_sz", (D, TC), BF16)
    dscr("s_g", (D, TC), BF16); dscr("s_lnp", (D, TC), F32)
    dscr("s_srk", (2 * NPB, TC), F32)
    dscr("s_yg", (D, TC), BF16)
    C2_ = cfg["C2"]
    dscr("s_xn1", (D, TC + 6), BF16)
    dscr("s_xc", (2 * C2_, TC), BF16)
    dscr("agH_in", (D, 6), F32)
    dscr("agH_out", (NCR * D, 6), F32)
    DSL_ = D // NCR
    dscr("a2aM_in", (NCR * 4 * DSL_, TC), BF16)
    dscr("a2aM_out", (NCR * 4 * DSL_, TC), BF16)
    dscr("agBC_in", (64, TC), BF16)
    dscr("agBC_out", (NCR * 64, TC), BF16)
    dscr("s_u", (2 * DSL_, NCR * TC), BF16)
    dscr("a2aY_in", (NCR * DSL_, TC), BF16)
    dscr("a2aY_out", (NCR * DSL_, TC), BF16)
    dscr("s_hq", (cfg["DQ"], TC), BF16)
    dscr("s_ybr", (DSL_, NCR * TC), BF16)
    C_ = cfg["C"]
    dscr("s_pack", (NCH * NPB * 128, 384), BF16)
    dscr("agE_in", (NPB * 64, 128), F32)
    dscr("agE_out", (NCR * NPB * 64, 128), F32)
    dscr("agD_in", (NPB * 128, 64), F32)
    dscr("agD_out", (NCR * NPB * 128, 64), F32)
    dscr("s_sh", (DFF, TC), BF16)
    dscr("s_tern2", (DFF, D), BF16)

    with ExitStack() as ctx:
        def sbuf(name, shape, dt):
            t_ = ctx.enter_context(nc.sbuf_tensor(name, list(shape), dt))
            setattr(s, "b_" + name, t_)
            return t_
        s.sbuf = sbuf

        KK0 = NPB * 512
        SQ0 = 2 * NPB * 512
        s.KK0, s.SQ0 = KK0, SQ0
        s.LERP0 = SQ0 + 512
        s.EX0 = SQ0 + 1024
        SEGW = min(2048, cfg['NCORES'] * TC)
        FG0 = min(3, NFB)
        s.FG0 = FG0
        WCOLS = max(FG0 * HW + 3072, SQ0 + 1536, 2048 + TC // 2 + 64,
                    2 * SEGW + 576, 4096, NPB * 512 + 4096)
        sbuf("x1", (128, NPB * XW), BF16)
        sbuf("xn0", (128, NPB * XW), BF16)
        sbuf("W", (128, WCOLS), F32)
        sbuf("W2", (128, max(NPB * XW, NFB * 512)), BF16)
        sbuf("W2b", (128, 6144), BF16)
        sbuf("wt", (128, 8192), BF16)
        s.b_wlr = s.b_wt
        sbuf("two", (2, 2048), F32)
        sbuf("bd", (128, 16 * 128), BF16)
        sbuf("pfx", (128, cfg["NCORES"]), F32)
        sbuf("kt", (128, 16 * 64), BF16)
        sbuf("st8", (128, 12 * 64), F32)
        sbuf("ec", (128, 10 * 64), BF16)
        sbuf("pv", (128, 384), F32)
        sbuf("rows", (1, 5 * 512), F32)
        sbuf("rowsq", (128, XW), BF16)
        sbuf("tok", (128, 128), F32)
        sbuf("cstI", (128, 128), F32)
        sbuf("cstIb", (128, 128), BF16)
        sbuf("onesr", (1, 128), F32)
        sbuf("onesrb", (1, 128), BF16)
        sbuf("rq1", (1, 512), BF16)
        sbuf("onesc", (128, 1), F32)
        sbuf("onescb", (128, 1), BF16)
        sbuf("iq", (128, 512), I32)
        sbuf("hs2", (128, 2), F32)
        sbuf("hx2", (2, 128), F32)
        sbuf("rst", (128, TC), BF16)
        sbuf("mstr", (128, cfg["C"]), F32)
        sbuf("mtr", (128, cfg["C"]), F32)
        sbuf("msk", (128, 2 * cfg["NCORES"]), F32)
        sbuf("ps64", (128, 72), BF16)
        sbuf("alog", (128, 16), F32)
        s.psum = [ctx.enter_context(nc.psum_tensor(f"ps{i}", [128, 512], F32))
                  for i in range(8)]
        s.psumb = [s.psum[6].bitcast(BF16), s.psum[7].bitcast(BF16)]
        s.pbt_i = 0

        t = Trk(nc, ctx)
        s.t = t
        s.block = ctx.enter_context(nc.Block())

        ph_loads(s)
        ph_ln0(s)
        if upto not in ("ln0", "ffn"):
            ph_rwkv_proj(s)
            ph_rwkv_chunks(s)
            ph_rwkv_post(s)
        if upto not in ("ln0", "rwkv"):
            ph_ffn0(s)
        if upto == "all":
            ph_ln1(s)
            ph_mamba_prep(s)
            ph_scan(s)
            ph_ffn1_final(s)
        else:
            ph_finish(s)

    return nc


def rk(buf, c0, c1):
    return [(buf, i) for i in range(c0 // 512, (c1 + 511) // 512)]


class Pb:
    def __init__(self, s, banks=range(8)):
        self.s = s
        self.banks = list(banks)
        self.i = 0

    def take(self):
        b = self.banks[self.i % len(self.banks)]
        self.i += 1
        return self.s.psum[b], [("psum", b)]


def bW(s, c0, c1):
    return s.b_W[:, c0:c1], rk("W", c0, c1)


def bW2(s, c0, c1):
    return s.b_W2[:, c0:c1], rk("W2", c0, c1)


def bW2b(s, c0, c1):
    return s.b_W2b[:, c0:c1], rk("W2b", c0, c1)


def ph_loads(s):
    t = s.t
    NPB = s.NPB; XW = s.XW
    for j in range(NPB):
        t.emit("gpsimd", lambda e, j=j: e.dma_start(
            out=s.b_x1[:, j * XW:(j + 1) * XW],
            in_=s.xT[j * 128:(j + 1) * 128, :]),
            reads=[("xT",)], writes=rk("x1", j * XW, (j + 1) * XW), dma=True)
    col = 0
    s.pvcol = {}
    for name in ["vmix", "vw0", "va0", "vkk", "vka", "vlnw", "vlnb", "vrk",
                 "f0cw", "f0cb", "vdtb", "cw1", "cb1", "vD"]:
        ap = getattr(s, name)
        w = ap.shape[1]
        t.emit("sync", lambda e, ap=ap, col=col, w=w: e.dma_start(
            out=s.b_pv[:, col:col + w], in_=ap), reads=[(name,)],
            writes=[("pv", name)], dma=True)
        s.pvcol[name] = col
        col += w
    s.pv_free = col
    t.emit("sync", lambda e: e.dma_start(out=s.b_cstI[:], in_=s.ident),
           reads=[("ident",)], writes=[("cstI",)], dma=True)
    t.emit("gpsimd", lambda e: e.dma_start(out=s.b_cstIb[:], in_=s.ident),
           reads=[("ident",)], writes=[("cstIb",)], dma=True)
    t.emit("sync", lambda e: e.dma_start(out=s.b_hs2[:], in_=s.hsum2),
           reads=[("hsum2",)], writes=[("hsum2c",)], dma=True)
    t.emit("sync", lambda e: e.dma_start(out=s.b_hx2[:], in_=s.hexp2),
           reads=[("hexp2",)], writes=[("hexp2c",)], dma=True)
    t.emit("gpsimd", lambda e: e.dma_start(out=s.b_rst[:], in_=s.rstpat),
           reads=[("rstpat",)], writes=[("rst",)], dma=True)
    t.emit("sync", lambda e: e.dma_start(out=s.b_mstr[:], in_=s.m_stril),
           reads=[("m_stril",)], writes=[("mstr",)], dma=True)
    t.emit("sync", lambda e: e.dma_start(out=s.b_mtr[:], in_=s.m_tril),
           reads=[("m_tril",)], writes=[("mtr",)], dma=True)
    s.hs2_ap = lambda: s.b_hs2[:, 0:2]
    s.hexp2_ap = lambda: s.b_hx2[0:2, :]
    t.emit("vector", lambda e: e.memset(s.b_tok[:], 0.0), writes=[("tok", c) for c in range(64)])
    t.emit("vector", lambda e: e.memset(s.b_onesr[:], 1.0), writes=[("onesr",)])
    t.emit("vector", lambda e: e.memset(s.b_onesrb[:], 1.0), writes=[("onesrb",)])
    t.emit("vector", lambda e: e.memset(s.b_onesc[:], 1.0), writes=[("onesc",)])
    t.emit("vector", lambda e: e.memset(s.b_onescb[:], 1.0), writes=[("onescb",)])
    if s.upto == "all":
        t.emit("sync", lambda e: e.dma_start(out=s.b_msk[:], in_=s.mskLR),
               reads=[("mskLR",)], writes=[("mskc",)], dma=True)
        t.emit("gpsimd", lambda e: e.dma_start(out=s.b_ps64[:], in_=s.psum64),
               reads=[("psum64",)], writes=[("ps64c",)], dma=True)
        t.emit("sync", lambda e: e.dma_start(out=s.b_alog[:, 0:s.alog_lanes.shape[1]],
                                             in_=s.alog_lanes),
               reads=[("alog_lanes",)], writes=[("alogc",)], dma=True)


def pvc(s, name, j=0):
    c = s.pvcol[name] + j
    return s.b_pv[:, c:c + 1]


def pv_alloc(s, name, w=1):
    s.pvcol[name] = s.pv_free
    s.pv_free += w
    assert s.pv_free <= 384
    return s.pvcol[name]



def len_W(s):
    return s.b_W.shape[1]


def bcast_sbuf(s, pb, row_ap, row_reg, width, c_dst, bf=False, obase=0):
    t = s.t
    p, r = pb.take()
    if bf:
        t.emit("vector", lambda e, row_ap=row_ap, width=width: e.tensor_copy(
            s.b_rq1[0:1, 0:width], row_ap), reads=list(row_reg), writes=[("rq1",)])
        row_ap = s.b_rq1[0:1, 0:width]
        row_reg = [("rq1",)]
        ones, okey = s.b_onesrb[0:1, 0:128], ("onesrb",)
    else:
        ones, okey = s.b_onesr[0:1, 0:128], ("onesr",)
    t.emit("tensor", lambda e, row_ap=row_ap: e.matmul(p[:, 0:width], ones,
                                        row_ap, start=True, stop=True),
           reads=[okey] + list(row_reg), writes=r)
    dst, keys = bW(s, c_dst, c_dst + width)
    t.emit("scalar", lambda e, dst=dst, p=p, width=width: e.copy(dst, p[:, 0:width]),
           reads=r, writes=keys)
    return dst, keys


def bcast_row(s, pb, row_ap, row_reg, width, bf=False, obase=0):
    t = s.t
    p, r = pb.take()
    if bf:
        t.emit("vector", lambda e, row_ap=row_ap, width=width: e.tensor_copy(
            s.b_rq1[0:1, 0:width], row_ap), reads=list(row_reg), writes=[("rq1",)])
        row_ap = s.b_rq1[0:1, 0:width]
        row_reg = [("rq1",)]
        ones, okey = s.b_onesrb[0:1, 0:128], ("onesrb",)
    else:
        ones, okey = s.b_onesr[0:1, 0:128], ("onesr",)
    t.emit("tensor", lambda e, row_ap=row_ap: e.matmul(p[:, 0:width], ones,
                                        row_ap, start=True, stop=True),
           reads=[okey] + list(row_reg), writes=r)
    return p, r


def ph_ln0(s):
    ln_channels(s, srcb="x1", dstb="xn0", eps=1e-5)
    tap_blocks(s, "xn0", s.b_xn0, "xn0", s.NPB, s.XW, BF16)


def ln_channels(s, srcb, dstb, eps):
    t = s.t
    NPB = s.NPB; XW = s.XW; D = s.D
    src = getattr(s, "b_" + srcb)
    dst = getattr(s, "b_" + dstb)
    pb = Pb(s)
    for (t0, t1) in segs(XW, 512):
        W = t1 - t0
        p1, r1 = pb.take()
        p2, r2 = pb.take()
        for j in range(NPB):
            t.emit("scalar", lambda e, j=j: e.activation(
                out=s.b_W[:, j * 512:j * 512 + W],
                in_=src[:, j * XW + t0:j * XW + t1], func=ACT.Square),
                reads=rk(srcb, j * XW + t0, j * XW + t1),
                writes=rk("W", j * 512, j * 512 + W))
        for j in range(NPB):
            t.emit("tensor", lambda e, j=j: e.matmul(
                p1[0:1, 0:W], s.b_onescb[:, 0:1],
                src[:, j * XW + t0:j * XW + t1],
                start=(j == 0), stop=(j == NPB - 1)),
                reads=[("onescb",)] + rk(srcb, j * XW + t0, j * XW + t1), writes=r1)
        for j in range(NPB):
            t.emit("tensor", lambda e, j=j: e.matmul(
                p2[0:1, 0:W], s.b_onesc[:, 0:1],
                s.b_W[:, j * 512:j * 512 + W],
                start=(j == 0), stop=(j == NPB - 1)),
                reads=[("onesc",)] + rk("W", j * 512, j * 512 + W), writes=r2)
        ln_rows(s, p1, r1, p2, r2, t0, W, D, eps)
        p3, r3 = bcast_row(s, pb, s.b_rows[0:1, 512:512 + W], [("rows", 3)], W)
        p4, r4 = bcast_row(s, pb, s.b_rows[0:1, 1024:1024 + W], [("rows", 4)], W)
        for j in range(NPB):
            t.emit("vector", lambda e, j=j: e.tensor_tensor(
                s.b_W[:, j * 512:j * 512 + W],
                src[:, j * XW + t0:j * XW + t1], p3[:, 0:W], ALU.mult),
                reads=rk(srcb, j * XW + t0, j * XW + t1) + r3,
                writes=rk("W", j * 512, j * 512 + W))
            t.emit("vector", lambda e, j=j: e.tensor_tensor(
                dst[:, j * XW + t0:j * XW + t1],
                s.b_W[:, j * 512:j * 512 + W], p4[:, 0:W], ALU.add),
                reads=rk("W", j * 512, j * 512 + W) + r4,
                writes=rk(dstb, j * XW + t0, j * XW + t1))


def ln_rows(s, p_sum, r_sum, p_sum2, r_sum2, t0, W, N, eps):
    t = s.t
    R = s.b_rows
    t1 = t0 + W
    LNS = {0: 0, 3: 1, 4: 2, 5: 3, 6: 4}

    def rr(row):
        return R[0:1, LNS[row] * 512:LNS[row] * 512 + W]

    def er(row):
        return [("rows", row)]

    t.emit("vector", lambda e: e.tensor_scalar(
        rr(0), p_sum[0:1, 0:W], 1.0 / N, None, ALU.mult),
        reads=r_sum, writes=er(0))
    t.emit("vector", lambda e: e.tensor_scalar(
        rr(5), p_sum2[0:1, 0:W], 1.0 / N, None, ALU.mult),
        reads=r_sum2, writes=er(5))
    t.emit("vector", lambda e: e.tensor_tensor(
        rr(6), rr(0), rr(0), ALU.mult),
        reads=er(0), writes=er(6))
    t.emit("vector", lambda e: e.tensor_tensor(
        rr(5), rr(5), rr(6), ALU.subtract),
        reads=er(5) + er(6), writes=er(5))
    t.emit("vector", lambda e: e.tensor_scalar(
        rr(5), rr(5), eps, None, ALU.add),
        reads=er(5), writes=er(5))
    t.emit("vector", lambda e: e.reciprocal(rr(6), rr(5)),
           reads=er(5), writes=er(6))
    t.emit("scalar", lambda e: e.activation(
        out=rr(3), in_=rr(6), func=ACT.Sqrt),
        reads=er(6), writes=er(3))
    t.emit("vector", lambda e: e.tensor_tensor(
        rr(6), rr(3), rr(3), ALU.mult),
        reads=er(3), writes=er(6))
    t.emit("vector", lambda e: e.tensor_tensor(
        rr(6), rr(6), rr(5), ALU.mult),
        reads=er(5) + er(6), writes=er(6))
    t.emit("vector", lambda e: e.tensor_scalar(
        rr(6), rr(6), -0.5, 1.5, ALU.mult, ALU.add),
        reads=er(6), writes=er(6))
    t.emit("vector", lambda e: e.tensor_tensor(
        rr(3), rr(3), rr(6), ALU.mult),
        reads=er(6) + er(3), writes=er(3))
    t.emit("vector", lambda e: e.tensor_tensor(
        rr(4), rr(0), rr(3), ALU.mult),
        reads=er(0) + er(3), writes=er(4))
    t.emit("vector", lambda e: e.tensor_scalar(
        rr(4), rr(4), -1.0, None, ALU.mult),
        reads=er(4), writes=er(4))


def tap_blocks(s, name, buf, bufname, nblk, W, dt):
    if name not in s.taps:
        return
    ap = s.dtap(name, (nblk * 128, W), dt)
    t = s.t
    for j in range(nblk):
        t.emit("gpsimd", lambda e, j=j: e.dma_start(
            out=ap[j * 128:(j + 1) * 128, :], in_=buf[:, j * W:(j + 1) * W]),
            reads=rk(bufname, j * W, (j + 1) * W),
            writes=[("tap" + name, j)], dma=True)


def weight_mean_cols(s, w_ap, wname, K, M, pb, ws_name):
    t = s.t
    KB = K // 128
    first = True
    CW = 2048
    for j in range(KB):
        for (m0, m1) in segs(M, CW):
            Wd = m1 - m0
            a1, k1 = bW(s, 0, Wd)
            t.emit("sync", lambda e, j=j, m0=m0, a1=a1, Wd=Wd: e.dma_start(
                out=a1, in_=w_ap[j * 128:(j + 1) * 128, m0:m0 + Wd]),
                reads=[(wname,)], writes=k1, dma=True)
            c2 = 2048
            a2, k2 = bW(s, c2, c2 + Wd)
            t.emit("scalar", lambda e, a1=a1, a2=a2: e.activation(
                out=a2, in_=a1, func=ACT.Abs), reads=k1, writes=k2)
            t.emit("vector", lambda e, a2=a2: e.tensor_reduce(
                s.b_tok[:, 0:1], a2, AX.X, ALU.add),
                reads=k2, writes=[("tok", 0)])
            if first:
                t.emit("vector", lambda e: e.tensor_copy(s.b_tok[:, 1:2], s.b_tok[:, 0:1]),
                       reads=[("tok", 0)], writes=[("tok", 1)])
                first = False
            else:
                t.emit("vector", lambda e: e.tensor_tensor(
                    s.b_tok[:, 1:2], s.b_tok[:, 1:2], s.b_tok[:, 0:1], ALU.add),
                    reads=[("tok", 0), ("tok", 1)], writes=[("tok", 1)])
    p, r = pb.take()
    t.emit("tensor", lambda e: e.matmul(p[0:1, 0:1], s.b_tok[:, 1:2],
                                        s.b_onesc[:, 0:1], start=True, stop=True),
           reads=[("tok", 1), ("onesc",)], writes=r)
    t.emit("vector", lambda e: e.tensor_scalar(
        s.b_tok[0:1, 2:3], p[0:1, 0:1], 1.0 / (K * M), 1e-5, ALU.mult, ALU.max),
        reads=r, writes=[("tok", 2)])
    p2, r2 = pb.take()
    t.emit("tensor", lambda e: e.matmul(p2[:, 0:1], s.b_onesr[0:1, 0:128],
                                        s.b_tok[0:1, 2:3], start=True, stop=True),
           reads=[("tok", 2), ("onesr",)], writes=r2)
    cw = pv_alloc(s, ws_name, 3)
    s.pvcol[ws_name + "_thr"] = cw + 1
    s.pvcol[ws_name + "_nthr"] = cw + 2
    t.emit("vector", lambda e: e.tensor_copy(s.b_pv[:, cw:cw + 1], p2[:, 0:1]),
           reads=r2, writes=[("pv", ws_name)])
    t.emit("vector", lambda e: e.tensor_scalar(
        s.b_pv[:, cw + 1:cw + 2], p2[:, 0:1], 0.5, None, ALU.mult),
        reads=r2, writes=[("pv", ws_name + "_thr")])
    t.emit("vector", lambda e: e.tensor_scalar(
        s.b_pv[:, cw + 2:cw + 3], p2[:, 0:1], -0.5, None, ALU.mult),
        reads=r2, writes=[("pv", ws_name + "_nthr")])


def tern_block(s, w_ap, wname, j, m0, m1, dst_ap, dst_keys, ws_name, c_ld, c_ge):
    t = s.t
    Wd = m1 - m0
    a1, k1 = bW(s, c_ld, c_ld + Wd)
    t.emit("sync", lambda e, a1=a1: e.dma_start(
        out=a1, in_=w_ap[j * 128:(j + 1) * 128, m0:m1]),
        reads=[(wname,)], writes=k1, dma=True)
    a2, k2 = bW(s, c_ge, c_ge + Wd)
    t.emit("vector", lambda e, a1=a1, a2=a2: e.tensor_scalar(
        a2, a1, pvc(s, ws_name + "_thr"), None, ALU.is_ge),
        reads=k1 + [("pv", ws_name + "_thr")], writes=k2)
    t.emit("vector", lambda e, a1=a1: e.tensor_scalar(
        a1, a1, pvc(s, ws_name + "_nthr"), None, ALU.is_le),
        reads=k1 + [("pv", ws_name + "_nthr")], writes=k1)
    t.emit("vector", lambda e, a1=a1, a2=a2: e.tensor_tensor(
        dst_ap, a2, a1, ALU.subtract),
        reads=k1 + k2, writes=dst_keys)


def quant_rows(s, srcfn, nblk, pb, W_total, srow_row, orow_row, abs_c0,
               oscale=None):
    t = s.t
    for (a, b) in segs(W_total, 512):
        Wd = b - a
        acc, kacc = bW(s, abs_c0 + a, abs_c0 + b)
        ap0, k0 = srcfn(0, a, b)
        t.emit("scalar", lambda e, acc=acc, ap0=ap0: e.activation(
            out=acc, in_=ap0, func=ACT.Abs), reads=k0, writes=kacc)
        for j in range(1, nblk):
            apj, kj = srcfn(j, a, b)
            tmpa, tmpk = bW(s, len_W(s) - 512, len_W(s) - 512 + (b - a))
            t.emit("scalar", lambda e, tmpa=tmpa, apj=apj: e.activation(
                out=tmpa, in_=apj, func=ACT.Abs), reads=kj, writes=tmpk)
            t.emit("vector", lambda e, acc=acc, tmpa=tmpa: e.tensor_tensor(
                acc, acc, tmpa, ALU.max), reads=tmpk + kacc, writes=kacc)
    quant_rows_from_abs(s, pb, W_total, srow_row, orow_row, abs_c0, oscale)


def quant_rows_from_abs(s, pb, W_total, srow_row, orow_row, abs_c0, oscale=None):
    t = s.t
    nb = (W_total + 127) // 128
    assert nb <= 20
    for tb in range(nb):
        a = tb * 128
        Wd = min(128, W_total - a)
        p, r = pb.take()
        src, ksrc = bW(s, abs_c0 + a, abs_c0 + a + Wd)
        t.emit("tensor", lambda e, p=p, src=src, Wd=Wd: e.transpose(
            p[0:Wd, 0:128], src, s.b_cstI[:]),
            reads=ksrc + [("cstI",)], writes=r)
        t.emit("vector", lambda e, p=p, Wd=Wd, tb=tb: e.reduce_max(
            s.b_tok[0:Wd, 20 + tb:21 + tb], p[0:Wd, 0:128], AX.X),
            reads=r, writes=[("tok", 20 + tb)])
    for tb in range(nb):
        t.emit("vector", lambda e, tb=tb: e.tensor_scalar(
            s.b_tok[:, 20 + tb:21 + tb], s.b_tok[:, 20 + tb:21 + tb], 1e-5,
            None, ALU.max),
            reads=[("tok", 20 + tb)], writes=[("tok", 20 + tb)])
        t.emit("vector", lambda e, tb=tb: e.reciprocal(
            s.b_tok[:, 40 + tb:41 + tb], s.b_tok[:, 20 + tb:21 + tb]),
            reads=[("tok", 20 + tb)], writes=[("tok", 40 + tb)])
        t.emit("vector", lambda e, tb=tb: e.tensor_scalar(
            s.b_tok[:, 40 + tb:41 + tb], s.b_tok[:, 40 + tb:41 + tb], 127.0,
            None, ALU.mult),
            reads=[("tok", 40 + tb)], writes=[("tok", 40 + tb)])
        if oscale is not None:
            t.emit("vector", lambda e, tb=tb: e.tensor_scalar(
                s.b_tok[:, 20 + tb:21 + tb], s.b_tok[:, 20 + tb:21 + tb],
                pvc(s, oscale), None, ALU.mult),
                reads=[("tok", 20 + tb), ("pv", oscale)], writes=[("tok", 20 + tb)])
        t.emit("vector", lambda e, tb=tb: e.tensor_scalar(
            s.b_tok[:, 20 + tb:21 + tb], s.b_tok[:, 20 + tb:21 + tb],
            1.0 / 127.0, None, ALU.mult),
            reads=[("tok", 20 + tb)], writes=[("tok", 20 + tb)])
    for (cols, dstrow) in ((40, srow_row), (20, orow_row)):
        for tb in range(nb):
            a = tb * 128
            Wd = min(128, W_total - a)
            t.emit("gpsimd", lambda e, tb=tb, a=a, Wd=Wd, dstrow=dstrow, cols=cols:
                   e.dma_start(
                       out=s.b_rowsq[32 * (dstrow - 1):32 * (dstrow - 1) + 1,
                                     a:a + Wd],
                       in_=s.b_tok[0:Wd, cols + tb:cols + tb + 1]),
                   reads=[("tok", cols + tb)], writes=[("rowsq", dstrow)], dma=True)


def quant_apply(s, srcfn, dstfn, nblk, pb, W_total, srow_row, c_tmp):
    t = s.t
    for (a, b) in segs(W_total, 512):
        Wd = b - a
        ps, rs = bcast_row(s, pb,
                           s.b_rowsq[32 * (srow_row - 1):32 * (srow_row - 1) + 1, a:b],
                           [("rowsq", srow_row)], Wd, bf=True)
        for j in range(nblk):
            sap, skeys = srcfn(j, a, b)
            dap, dkeys = dstfn(j, a, b)
            t1a, t1k = bW(s, c_tmp, c_tmp + Wd)
            t.emit("vector", lambda e, t1a=t1a, sap=sap, ps=ps, Wd=Wd: e.tensor_tensor(
                t1a, sap, ps[:, 0:Wd], ALU.mult), reads=skeys + rs, writes=t1k)
            t.emit("vector", lambda e, t1a=t1a: e.tensor_scalar(
                t1a, t1a, 128.5, None, ALU.add), reads=t1k, writes=t1k)
            t.emit("vector", lambda e, t1a=t1a, Wd=Wd: e.tensor_copy(
                s.b_iq[:, 0:Wd], t1a), reads=t1k, writes=[("iq",)])
            t.emit("vector", lambda e, dap=dap, Wd=Wd: e.tensor_scalar(
                dap, s.b_iq[:, 0:Wd], -128.0, None, ALU.add),
                reads=[("iq",)], writes=dkeys)


def ph_ffn0(s):
    t = s.t
    NPB, NFB = s.NPB, s.NFB
    TC = s.TC; XW = s.XW; HW = s.HW; D = s.D; DFF = s.DFF
    pb = Pb(s)

    weight_mean_cols(s, s.f0w1t, "f0w1t", D, DFF, pb, "ws1")
    weight_mean_cols(s, s.f0w2t, "f0w2t", DFF, D, pb, "ws2")

    quant_rows(s, lambda j, a, b: bxn0(s, j, a, b), NPB, pb, XW,
               srow_row=1, orow_row=2, abs_c0=0)
    quant_apply(s, lambda j, a, b: bxn0(s, j, a, b),
                lambda j, a, b: bW2(s, j * XW + a, j * XW + b),
                NPB, pb, XW, srow_row=1, c_tmp=0)
    tap_blocks(s, "xq1", s.b_W2, "W2", NPB, XW, BF16)

    for j in range(NFB):
        a, k = bW2b(s, 0, D)
        tern_block(s, s.f0w2t, "f0w2t", j, 0, D, a, k, "ws2", c_ld=0, c_ge=1024)
        t.emit("gpsimd", lambda e, j=j, a=a: e.dma_start(
            out=s.s_tern2[j * 128:(j + 1) * 128, :], in_=a),
            reads=k, writes=[("s_tern2", j)], dma=True)

    FG = s.FG0
    FGH = FG * HW
    for fg in range(0, NFB, FG):
        gpar = (fg // FG) % 2
        nfi = min(FG, NFB - fg)
        for j in range(NPB):
            tern_block(s, s.f0w1t, "f0w1t", j, fg * 128, (fg + nfi) * 128,
                       s.b_wt[:, j * 1024 + gpar * 512:j * 1024 + gpar * 512 + nfi * 128],
                       [("wtg", j, gpar), ("wt", j)], "ws1",
                       c_ld=FGH + 1024 + (j % 2) * 512,
                       c_ge=FGH + 2048 + (j % 2) * 512)
        for (a, b) in segs(XW, 512):
            Wd = b - a
            repa, repk = bcast_sbuf(s, pb, s.b_rowsq[32:33, a:b],
                                    [("rowsq", 2)], Wd, FGH, bf=True)
            for fl in range(nfi):
                p, rp = pb.take()
                for j in range(NPB):
                    t.emit("tensor", lambda e, j=j, fl=fl, a=a, Wd=Wd, p=p,
                           gpar=gpar: e.matmul(
                        p[:, 0:Wd],
                        s.b_wt[:, j * 1024 + gpar * 512 + fl * 128:
                               j * 1024 + gpar * 512 + fl * 128 + 128],
                        s.b_W2[:, j * XW + a:j * XW + a + Wd],
                        start=(j == 0), stop=(j == NPB - 1)),
                        reads=[("wtg", j, gpar)] + rk("W2", j * XW + a, j * XW + b),
                        writes=rp)
                ha, hk = bW(s, fl * HW + a, fl * HW + a + Wd)
                t.emit("vector", lambda e, ha=ha, p=p, repa=repa, Wd=Wd:
                       e.scalar_tensor_tensor(ha, p[:, 0:Wd], pvc(s, "ws1"),
                                              repa, ALU.mult, ALU.mult),
                       reads=rp + repk + [("pv", "ws1")], writes=hk)
        for fl in range(nfi):
            fi = fg + fl
            base = fl * HW
            for (a, b) in segs(TC, 512):
                Wd = b - a
                cpar = ((fl * 4) + a // 512) % 2
                hsrc, hkeys = bW(s, base + a, base + a + Wd + 2)
                cva, cvk = bW(s, FGH + cpar * 512, FGH + cpar * 512 + Wd)
                t.emit("vector", lambda e, cva=cva, base=base, a=a, Wd=Wd, fi=fi:
                       e.tensor_scalar(cva, s.b_W[:, base + a:base + a + Wd],
                                       pvc(s, "f0cw", 3 * fi + 0), None, ALU.mult),
                       reads=hkeys + [("pv", "f0cw")], writes=cvk)
                for tap in (1, 2):
                    t.emit("vector", lambda e, cva=cva, base=base, a=a, Wd=Wd,
                           fi=fi, tap=tap: e.scalar_tensor_tensor(
                               cva, s.b_W[:, base + a + tap:base + a + tap + Wd],
                               pvc(s, "f0cw", 3 * fi + tap), cva, ALU.mult, ALU.add),
                           reads=hkeys + cvk + [("pv", "f0cw")], writes=cvk)
                sha, shk = bW2b(s, 2048 + a % 2048, 2048 + a % 2048 + Wd)
                t.emit("scalar", lambda e, sha=sha, cva=cva, fi=fi: e.activation(
                    out=sha, in_=cva, func=ACT.Silu, bias=pvc(s, "f0cb", fi),
                    scale=1.0),
                    reads=cvk + [("pv", "f0cb")], writes=shk)
                t.emit("gpsimd", lambda e, sha=sha, fi=fi, a=a, Wd=Wd: e.dma_start(
                    out=s.s_sh[fi * 128:(fi + 1) * 128, a:a + Wd], in_=sha),
                    reads=shk, writes=[("s_sh", fi)], dma=True)
                acc, kacc = bW2b(s, 4096 + a, 4096 + b)
                if fi == 0:
                    t.emit("scalar", lambda e, acc=acc, sha=sha: e.activation(
                        out=acc, in_=sha, func=ACT.Abs), reads=shk, writes=kacc)
                else:
                    tmpa, tmpk = bW2b(s, (a // 512 % 2) * 512, (a // 512 % 2) * 512 + Wd)
                    t.emit("scalar", lambda e, tmpa=tmpa, sha=sha: e.activation(
                        out=tmpa, in_=sha, func=ACT.Abs), reads=shk, writes=tmpk)
                    t.emit("vector", lambda e, acc=acc, tmpa=tmpa: e.tensor_tensor(
                        acc, acc, tmpa, ALU.max), reads=tmpk + kacc, writes=kacc)

    for (a, b) in segs(TC, 512):
        aw, kw = bW(s, a, b)
        ab2, kb2 = bW2b(s, 4096 + a, 4096 + b)
        t.emit("vector", lambda e, aw=aw, ab2=ab2: e.tensor_copy(aw, ab2),
               reads=kb2, writes=kw)
    quant_rows_from_abs(s, pb, TC, srow_row=3, orow_row=4, abs_c0=0, oscale="ws2")

    for (a, b) in segs(TC, 512):
        Wd = b - a
        srepa, srepk = bcast_sbuf(s, pb, s.b_rowsq[64:65, a:b],
                                  [("rowsq", 3)], Wd, FGH, bf=True)
        for j in range(NFB):
            t.emit("sync", lambda e, j=j, a=a, Wd=Wd: e.dma_start(
                out=s.b_W2b[:, 0:Wd], in_=s.s_sh[j * 128:(j + 1) * 128, a:a + Wd]),
                reads=[("s_sh", j)], writes=rk("W2b", 0, Wd), dma=True)
            t1a, t1k = bW(s, 0, Wd)
            t.emit("vector", lambda e, t1a=t1a, srepa=srepa, Wd=Wd: e.tensor_tensor(
                t1a, s.b_W2b[:, 0:Wd], srepa, ALU.mult),
                reads=rk("W2b", 0, Wd) + srepk, writes=t1k)
            t.emit("vector", lambda e, t1a=t1a: e.tensor_scalar(
                t1a, t1a, 128.5, None, ALU.add), reads=t1k, writes=t1k)
            t.emit("vector", lambda e, t1a=t1a, Wd=Wd: e.tensor_copy(
                s.b_iq[:, 0:Wd], t1a), reads=t1k, writes=[("iq",)])
            t.emit("vector", lambda e, j=j, Wd=Wd: e.tensor_scalar(
                s.b_W2[:, j * 512:j * 512 + Wd], s.b_iq[:, 0:Wd], -128.0,
                None, ALU.add),
                reads=[("iq",)], writes=rk("W2", j * 512, j * 512 + Wd))
        orepa, orepk = bcast_sbuf(s, pb, s.b_rowsq[96:97, a:b],
                                  [("rowsq", 4)], Wd, FGH + 512, bf=True)
        pouts = [(s.psum[i], [("psum", i)]) for i in range(NPB)]
        for j in range(NFB):
            t.emit("sync", lambda e, j=j: e.dma_start(
                out=s.b_wt[:, (j % 8) * 1024:(j % 8) * 1024 + D],
                in_=s.s_tern2[j * 128:(j + 1) * 128, :]),
                reads=[("s_tern2", j)], writes=[("wt", j % 8)], dma=True)
            for oi in range(NPB):
                p, rp = pouts[oi]
                t.emit("tensor", lambda e, j=j, oi=oi, p=p, Wd=Wd: e.matmul(
                    p[:, 0:Wd],
                    s.b_wt[:, (j % 8) * 1024 + oi * 128:(j % 8) * 1024 + oi * 128 + 128],
                    s.b_W2[:, j * 512:j * 512 + Wd],
                    start=(j == 0), stop=(j == NFB - 1)),
                    reads=[("wt", j % 8)] + rk("W2", j * 512, j * 512 + Wd),
                    writes=rp)
        for oi in range(NPB):
            p, rp = pouts[oi]
            t1a, t1k = bW(s, 0, Wd)
            t.emit("vector", lambda e, t1a=t1a, p=p, orepa=orepa, Wd=Wd:
                   e.tensor_tensor(t1a, p[:, 0:Wd], orepa, ALU.mult),
                   reads=rp + orepk, writes=t1k)
            xa = s.b_x1[:, oi * XW + 1 + a:oi * XW + 1 + b]
            xkeys = rk("x1", oi * XW + 1 + a, oi * XW + 1 + b)
            t.emit("vector", lambda e, xa=xa, t1a=t1a: e.tensor_tensor(
                xa, xa, t1a, ALU.add), reads=t1k + xkeys, writes=xkeys)
    tap_blocks(s, "x1f", s.b_x1, "x1", NPB, XW, BF16)


def bxn0(s, j, a, b):
    XW = s.XW
    return s.b_xn0[:, j * XW + a:j * XW + b], rk("xn0", j * XW + a, j * XW + b)


def ph_finish(s):
    t = s.t
    TC = s.TC; XW = s.XW
    for j in range(s.NPB):
        t.emit("gpsimd", lambda e, j=j: e.dma_start(
            out=s.out[j * 128:(j + 1) * 128, :],
            in_=s.b_x1[:, j * XW + 1:j * XW + 1 + TC]),
            reads=rk("x1", j * XW + 1, j * XW + 1 + TC),
            writes=[("outw", j)], dma=True)
    t.eng["gpsimd"].wait_ge(t.sem[("gpsimd", "d")], t.cnt[("gpsimd", "d")])
    t.eng["sync"].wait_ge(t.sem[("sync", "d")], t.cnt[("sync", "d")])


def host_layout(cfg, inputs, core):
    TC = cfg["TC"]; D = cfg["D"]; NPB = D // 128
    NC = cfg["NCORES"]
    x = np.asarray(inputs["L0_attn_x"])[0]
    T = x.shape[0]
    assert T == TC * NC
    t0 = core * TC
    xe = np.zeros((TC + 2, D), np.float32)
    xe[1:TC + 1] = x[t0:t0 + TC]
    if t0 > 0:
        xe[0] = x[t0 - 1]
    if t0 + TC < T:
        xe[TC + 1] = x[t0 + TC]
    m = {}
    m["xT"] = np.ascontiguousarray(xe.T)

    def v128(vec):
        v = np.asarray(vec, np.float32).reshape(-1)
        return np.ascontiguousarray(v.reshape(-1, 128).T)
    m["vmix"] = np.concatenate([v128(inputs["L0_attn_x_" + nm])
                                for nm in ["r", "w", "k", "v", "a", "g"]], axis=1)
    m["vw0"] = v128(inputs["L0_attn_w0"])
    m["va0"] = v128(inputs["L0_attn_a0"])
    m["vkk"] = v128(inputs["L0_attn_k_k"])
    m["vka"] = v128(inputs["L0_attn_k_a"])
    m["vlnw"] = v128(inputs["L0_attn_ln_w"])
    m["vlnb"] = v128(inputs["L0_attn_ln_b"])
    m["vrk"] = v128(inputs["L0_attn_r_k"].reshape(-1))
    m["f0w1t"] = np.ascontiguousarray(np.asarray(inputs["L0_ffn_w1"], np.float32).T)
    m["f0w2t"] = np.ascontiguousarray(np.asarray(inputs["L0_ffn_w2"], np.float32).T)
    cw = np.asarray(inputs["L0_ffn_conv_w"], np.float32)[:, 0, :]
    NFB = cfg["DFF"] // 128
    m["f0cw"] = np.ascontiguousarray(
        cw.reshape(NFB, 128, 3).transpose(1, 0, 2).reshape(128, NFB * 3)
    )
    m["f0cb"] = v128(inputs["L0_ffn_conv_b"])
    m["ident"] = np.eye(128, dtype=np.float32)
    C = cfg["C"]
    m["wr"] = np.asarray(inputs["L0_attn_W_r"], np.float32)
    m["wk"] = np.asarray(inputs["L0_attn_W_k"], np.float32)
    m["wv"] = np.asarray(inputs["L0_attn_W_v"], np.float32)
    m["wo"] = np.asarray(inputs["L0_attn_W_o"], np.float32)
    m["lw1"] = np.asarray(inputs["L0_attn_w1"], np.float32)
    m["lw2"] = np.asarray(inputs["L0_attn_w2"], np.float32)
    m["la1"] = np.asarray(inputs["L0_attn_a1"], np.float32)
    m["la2"] = np.asarray(inputs["L0_attn_a2"], np.float32)
    m["lg1"] = np.asarray(inputs["L0_attn_g1"], np.float32)
    m["lg2"] = np.asarray(inputs["L0_attn_g2"], np.float32)
    hs2 = np.zeros((128, 2), np.float32)
    hs2[0:64, 0] = 1.0
    hs2[64:128, 1] = 1.0
    m["hsum2"] = hs2
    hx2 = np.zeros((2, 128), np.float32)
    hx2[0, 0:64] = 1.0
    hx2[1, 64:128] = 1.0
    m["hexp2"] = hx2
    rst = np.ones((128, TC), np.float32)
    rst[:, ::C] = 0.0
    m["rstpat"] = rst
    pfx = np.zeros((128, NC), np.float32)
    pfx[:, 0:core] = 1.0
    m["pfxm64"] = pfx
    stril = (np.arange(C)[:, None] < np.arange(C)[None, :]).astype(np.float32)
    tril = (np.arange(C)[:, None] <= np.arange(C)[None, :]).astype(np.float32)
    ms_ = np.zeros((128, C), np.float32)
    ms_[0:C] = stril
    ms_[C:2 * C] = stril
    m["m_stril"] = ms_
    mt_ = np.zeros((128, C), np.float32)
    mt_[0:C] = tril
    mt_[C:2 * C] = tril
    m["m_tril"] = mt_
    C2 = cfg["C2"]; NQ = C2 // 128
    DSL = D // NC
    cw = np.asarray(inputs["L1_attn_conv_w"], np.float32)[:, 0, :]
    m["cw1"] = np.ascontiguousarray(
        cw.reshape(NQ, 128, 4).transpose(1, 0, 2).reshape(128, NQ * 4))
    m["cb1"] = v128(inputs["L1_attn_conv_b"])
    m["xprojt"] = np.ascontiguousarray(np.asarray(inputs["L1_attn_x_proj_w"], np.float32).T)
    m["dtprojt"] = np.ascontiguousarray(np.asarray(inputs["L1_attn_dt_proj_w"], np.float32).T)
    m["vdtb"] = v128(inputs["L1_attn_dt_proj_b"])
    al = np.asarray(inputs["L1_attn_A_log"], np.float32)[core * DSL:(core + 1) * DSL]
    NTI = max(DSL * 16 // 128, 1)
    m["alog_lanes"] = np.ascontiguousarray(
        np.stack([al[8 * ti:8 * ti + 8, :].reshape(-1) for ti in range(NTI)], axis=1))
    vD = np.zeros((128, 1), np.float32)
    vD[0:DSL, 0] = np.asarray(inputs["L1_attn_D"], np.float32)[core * DSL:(core + 1) * DSL]
    m["vD"] = vD
    msk = np.zeros((128, 2 * NC), np.float32)
    if core - 1 >= 0:
        msk[:, core - 1] = 1.0
    if core + 1 < NC:
        msk[:, NC + core + 1] = 1.0
    m["mskLR"] = msk
    ps = np.zeros((128, 72), np.float32)
    for p in range(128):
        ps[p, p // 2] = 0.5
        ps[p, 64 + p // 16] = 1.0
    m["psum64"] = ps
    m["qg"] = np.ascontiguousarray(np.asarray(inputs["L1_ffn_w_gate"]).T).astype(np.int32)
    m["qu"] = np.ascontiguousarray(np.asarray(inputs["L1_ffn_w_up"]).T).astype(np.int32)
    m["qd"] = np.ascontiguousarray(np.asarray(inputs["L1_ffn_w_down"]).T).astype(np.int32)
    return m


def run_tiny_test():
    pass


def lerp_d(s, j, t0, W, c_d):
    t = s.t
    XW = s.XW
    da, dk = bW(s, c_d, c_d + W)
    t.emit("vector", lambda e, da=da: e.tensor_tensor(
        da, s.b_xn0[:, j * XW + t0:j * XW + t0 + W],
        s.b_xn0[:, j * XW + t0 + 1:j * XW + t0 + 1 + W], ALU.subtract),
        reads=rk("xn0", j * XW + t0, j * XW + t0 + 1 + W), writes=dk)
    return da, dk


def lerp_apply(s, j, t0, W, mix_idx, da, dk, dst_ap, dst_keys):
    t = s.t
    XW = s.XW
    t.emit("vector", lambda e: e.scalar_tensor_tensor(
        dst_ap, da, pvc(s, "vmix", mix_idx * s.NPB + j),
        s.b_xn0[:, j * XW + t0 + 1:j * XW + t0 + 1 + W], ALU.mult, ALU.add),
        reads=dk + rk("xn0", j * XW + t0 + 1, j * XW + t0 + 1 + W) + [("pv", "vmix")],
        writes=dst_keys)


def load_bigw(s, w_ap, wname):
    t = s.t
    D = s.D
    NPB = s.NPB
    t.emit("gpsimd", lambda e: e.dma_start(
        out=s.b_wt[:, 0:NPB * D],
        in_=w_ap.rearrange("(ks p) m -> p ks m", p=128)),
        reads=[(wname,)],
        writes=[("wt", j) for j in range(8)] +
        [("wlr", n) for n in ("lw1", "la1", "lg1", "lw2", "la2", "lg2")], dma=True)


def mm_proj(s, pb, rhsfn, out_cb, width, kdim=None, wslot=None):
    t = s.t
    D = s.D
    NPB = s.NPB
    outs = []
    for mj in range(NPB):
        p, rp = pb.take()
        for ks in range(NPB):
            ap, keys = rhsfn(ks)
            t.emit("tensor", lambda e, p=p, mj=mj, ks=ks, ap=ap: e.matmul(
                p[:, 0:width], s.b_wt[:, ks * D + mj * 128:ks * D + mj * 128 + 128],
                ap, start=(ks == 0), stop=(ks == NPB - 1)),
                reads=[("wt", ks)] + keys, writes=rp)
        outs.append((p, rp))
    return outs


def ph_rwkv_proj(s):
    t = s.t
    NPB = s.NPB
    TC = s.TC; XW = s.XW; D = s.D
    H = s.H
    pb = Pb(s)
    DW, DA, DG = s.cfg["DW"], s.cfg["DA"], s.cfg["DG"]
    lr = [("lw1", 0, (D, DW), "ks"), ("la1", 512, (D, DA), "ks"),
          ("lg1", 1024, (D, DG), "ks"), ("lw2", 2048, (DW, D), "flat"),
          ("la2", 3072, (DA, D), "flat"), ("lg2", 4096, (DG, D), "flat")]
    s.lroff = {}
    for nm, off, shape, kind in lr:
        ap = getattr(s, nm)
        s.lroff[nm] = off
        if kind == "ks":
            t.emit("gpsimd", lambda e, ap=ap, off=off, w=shape[1]: e.dma_start(
                out=s.b_wlr[:, off:off + NPB * w],
                in_=ap.rearrange("(ks p) m -> p ks m", p=128)),
                reads=[(nm,)], writes=[("wlr", nm)], dma=True)
        else:
            t.emit("gpsimd", lambda e, ap=ap, off=off: e.dma_start(
                out=s.b_wlr[:, off:off + shape[1]][0:shape[0], :], in_=ap),
                reads=[(nm,)], writes=[("wlr", nm)], dma=True)
    pv_alloc(s, "omka", NPB)
    for j in range(NPB):
        t.emit("vector", lambda e, j=j: e.tensor_scalar(
            pvc(s, "omka", j), pvc(s, "vka", j), -1.0, 1.0, ALU.mult, ALU.add),
            reads=[("pv", "vka")], writes=[("pv", "omka")])

    MIX = {"r": 0, "w": 1, "k": 2, "v": 3, "a": 4, "g": 5}

    def lowrank(mix, w1n, wdim, w2n, bias, sink):
        for (t0, t1) in segs(TC, 512):
            W = t1 - t0
            p1, r1 = pb.take()
            for j in range(NPB):
                da, dk = lerp_d(s, j, t0, W, s.LERP0)
                xa, xk = bW2b(s, 0, W)
                lerp_apply(s, j, t0, W, mix, da, dk, xa, xk)
                t.emit("tensor", lambda e, p1=p1, j=j, xa=xa, W=W: e.matmul(
                    p1[0:wdim, 0:W],
                    s.b_wlr[:, s.lroff[w1n] + j * wdim:s.lroff[w1n] + (j + 1) * wdim],
                    xa, start=(j == 0), stop=(j == NPB - 1)),
                    reads=[("wlr", w1n)] + xk, writes=r1)
            lowa, lowk = bW2b(s, 1024, 1024 + W)
            fn1 = ACT.Tanh if w1n == "lw1" else (ACT.Sigmoid if w1n == "lg1" else ACT.Copy)
            t.emit("scalar", lambda e, lowa=lowa, p1=p1, W=W, fn1=fn1: e.activation(
                out=lowa[0:wdim, :], in_=p1[0:wdim, 0:W], func=fn1),
                reads=r1, writes=lowk)
            for mj in range(NPB):
                p2, r2 = pb.take()
                t.emit("tensor", lambda e, p2=p2, mj=mj, lowa=lowa, W=W: e.matmul(
                    p2[:, 0:W],
                    s.b_wlr[:, s.lroff[w2n] + mj * 128:s.lroff[w2n] + (mj + 1) * 128][0:wdim, :],
                    lowa[0:wdim, :], start=True, stop=True),
                    reads=[("wlr", w2n)] + lowk, writes=r2)
                sink(mj, t0, W, p2, r2)

    def sink_a(mj, t0, W, p2, r2):
        dst, dkeys = bW2(s, mj * TC + t0, mj * TC + t0 + W)
        t.emit("scalar", lambda e, dst=dst, p2=p2, W=W, mj=mj: e.activation(
            out=dst, in_=p2[:, 0:W], func=ACT.Sigmoid, bias=pvc(s, "va0", mj),
            scale=1.0),
            reads=r2 + [("pv", "va0")], writes=dkeys)

    def sink_sz(mj, t0, W, p2, r2):
        dst, dkeys = bW2b(s, 2048, 2048 + W)
        t.emit("scalar", lambda e, dst=dst, p2=p2, W=W, mj=mj: e.activation(
            out=dst, in_=p2[:, 0:W], func=ACT.Sigmoid, bias=pvc(s, "vw0", mj),
            scale=1.0),
            reads=r2 + [("pv", "vw0")], writes=dkeys)
        t.emit("gpsimd", lambda e, dst=dst, mj=mj, t0=t0, W=W: e.dma_start(
            out=s.s_sz[mj * 128:(mj + 1) * 128, t0:t0 + W], in_=dst),
            reads=dkeys, writes=[("s_sz", mj)], dma=True)

    def sink_g(mj, t0, W, p2, r2):
        dst, dkeys = bW2b(s, 2048, 2048 + W)
        t.emit("vector", lambda e, dst=dst, p2=p2, W=W: e.tensor_copy(
            dst, p2[:, 0:W]), reads=r2, writes=dkeys)
        t.emit("gpsimd", lambda e, dst=dst, mj=mj, t0=t0, W=W: e.dma_start(
            out=s.s_g[mj * 128:(mj + 1) * 128, t0:t0 + W], in_=dst),
            reads=dkeys, writes=[("s_g", mj)], dma=True)

    lowrank(MIX["a"], "la1", s.cfg["DA"], "la2", "va0", sink_a)
    lowrank(MIX["w"], "lw1", s.cfg["DW"], "lw2", "vw0", sink_sz)
    lowrank(MIX["g"], "lg1", s.cfg["DG"], "lg2", None, sink_g)

    load_bigw(s, s.wk, "wk")
    for (t0, t1) in segs(TC, 512):
        W = t1 - t0
        for j in range(NPB):
            da, dk = lerp_d(s, j, t0, W, s.LERP0)
            xa, xk_ = bW2b(s, j * 512, j * 512 + W)
            lerp_apply(s, j, t0, W, MIX["k"], da, dk, xa, xk_)
        outs = mm_proj(s, pb, lambda ks: bW2b(s, ks * 512, ks * 512 + W), None, W)
        for mj in range(NPB):
            p, rp = outs[mj]
            ka_, kk_ = bW(s, mj * 512, mj * 512 + W)
            t.emit("vector", lambda e, ka_=ka_, p=p, W=W: e.tensor_copy(
                ka_, p[:, 0:W]), reads=rp, writes=kk_)
        for mj in range(NPB):
            ka_, kk_ = bW(s, mj * 512, mj * 512 + W)
            kka, kkk = bW(s, s.KK0 + mj * 512, s.KK0 + mj * 512 + W)
            t.emit("vector", lambda e, kka=kka, ka_=ka_, mj=mj: e.tensor_scalar(
                kka, ka_, pvc(s, "vkk", mj), None, ALU.mult),
                reads=kk_ + [("pv", "vkk")], writes=kkk)
            sqa, sqk = bW(s, s.SQ0, s.SQ0 + W)
            t.emit("scalar", lambda e, sqa=sqa, kka=kka: e.activation(
                out=sqa, in_=kka, func=ACT.Square), reads=kkk, writes=sqk)
            pss, rss = pb.take()
            t.emit("tensor", lambda e, pss=pss, sqa=sqa, W=W: e.matmul(
                pss[0:2, 0:W], s.b_hsum2[:, 0:2] if False else s.b_pv[:, 0:0] if False else s.hs2_ap(), sqa,
                start=True, stop=True),
                reads=[("hsum2c",)] + sqk, writes=rss)
            tw = s.b_two
            t.emit("vector", lambda e, pss=pss, W=W: e.tensor_scalar(
                tw[:, 0:W], pss[0:2, 0:W], 1e-24, None, ALU.max),
                reads=rss, writes=[("two", 0)])
            t.emit("vector", lambda e, W=W: e.reciprocal(tw[:, 512:512 + W], tw[:, 0:W]),
                   reads=[("two", 0)], writes=[("two", 1)])
            t.emit("scalar", lambda e, W=W: e.activation(
                out=tw[:, 1024:1024 + W], in_=tw[:, 512:512 + W], func=ACT.Sqrt),
                reads=[("two", 1)], writes=[("two", 2)])
            t.emit("vector", lambda e, W=W: e.tensor_tensor(
                tw[:, 1536:1536 + W], tw[:, 1024:1024 + W], tw[:, 1024:1024 + W],
                ALU.mult), reads=[("two", 2)], writes=[("two", 3)])
            t.emit("vector", lambda e, W=W: e.tensor_tensor(
                tw[:, 1536:1536 + W], tw[:, 1536:1536 + W], tw[:, 0:W], ALU.mult),
                reads=[("two", 0), ("two", 3)], writes=[("two", 3)])
            t.emit("vector", lambda e, W=W: e.tensor_scalar(
                tw[:, 1536:1536 + W], tw[:, 1536:1536 + W], -0.5, 1.5, ALU.mult,
                ALU.add), reads=[("two", 3)], writes=[("two", 3)])
            t.emit("vector", lambda e, W=W: e.tensor_tensor(
                tw[:, 1024:1024 + W], tw[:, 1024:1024 + W], tw[:, 1536:1536 + W],
                ALU.mult), reads=[("two", 2), ("two", 3)], writes=[("two", 2)])
            pbr, rbr = pb.take()
            t.emit("tensor", lambda e, pbr=pbr, W=W: e.matmul(
                pbr[:, 0:W], s.hexp2_ap(), tw[0:2, 1024:1024 + W],
                start=True, stop=True),
                reads=[("hexp2c",), ("two", 2)], writes=rbr)
            kapa, kapk = bW2b(s, 0, W)
            t.emit("vector", lambda e, kapa=kapa, kka=kka, pbr=pbr, W=W:
                   e.tensor_tensor(kapa, kka, pbr[:, 0:W], ALU.mult),
                   reads=kkk + rbr, writes=kapk)
            t.emit("gpsimd", lambda e, kapa=kapa, mj=mj, t0=t0, W=W: e.dma_start(
                out=s.s_kap[mj * 128:(mj + 1) * 128, t0:t0 + W], in_=kapa),
                reads=kapk, writes=[("s_kap", mj)], dma=True)
            aba, abk = bW2b(s, 1024, 1024 + W)
            t.emit("vector", lambda e, aba=aba, kapa=kapa, mj=mj, t0=t0, W=W:
                   e.tensor_tensor(aba, kapa, s.b_W2[:, mj * TC + t0:mj * TC + t0 + W],
                                   ALU.mult),
                   reads=kapk + rk("W2", mj * TC + t0, mj * TC + t0 + W), writes=abk)
            t.emit("gpsimd", lambda e, aba=aba, mj=mj, t0=t0, W=W: e.dma_start(
                out=s.s_ab[mj * 128:(mj + 1) * 128, t0:t0 + W], in_=aba),
                reads=abk, writes=[("s_ab", mj)], dma=True)
            ma, mk = bW(s, s.SQ0, s.SQ0 + W)
            t.emit("vector", lambda e, ma=ma, mj=mj, t0=t0, W=W: e.tensor_scalar(
                ma, s.b_W2[:, mj * TC + t0:mj * TC + t0 + W], pvc(s, "vka", mj),
                pvc(s, "omka", mj), ALU.mult, ALU.add),
                reads=rk("W2", mj * TC + t0, mj * TC + t0 + W) +
                [("pv", "vka"), ("pv", "omka")], writes=mk)
            kma, kmk = bW2b(s, 0, W)
            t.emit("vector", lambda e, kma=kma, ka_=ka_, ma=ma: e.tensor_tensor(
                kma, ka_, ma, ALU.mult), reads=kk_ + mk, writes=kmk)
            t.emit("gpsimd", lambda e, kma=kma, mj=mj, t0=t0, W=W: e.dma_start(
                out=s.s_kmod[mj * 128:(mj + 1) * 128, t0:t0 + W], in_=kma),
                reads=kmk, writes=[("s_kmod", mj)], dma=True)

    load_bigw(s, s.wr, "wr")
    for (t0, t1) in segs(TC, 512):
        W = t1 - t0
        for j in range(NPB):
            da, dk = lerp_d(s, j, t0, W, s.LERP0)
            xa, xk_ = bW2b(s, j * 512, j * 512 + W)
            lerp_apply(s, j, t0, W, MIX["r"], da, dk, xa, xk_)
        outs = mm_proj(s, pb, lambda ks: bW2b(s, ks * 512, ks * 512 + W), None, W)
        for mj in range(NPB):
            p, rp = outs[mj]
            ra, rk_ = bW(s, mj * 512, mj * 512 + W)
            t.emit("vector", lambda e, ra=ra, p=p, W=W: e.tensor_copy(
                ra, p[:, 0:W]), reads=rp, writes=rk_)
            rb, rbk = bW2b(s, 0, W)
            t.emit("vector", lambda e, rb=rb, ra=ra: e.tensor_copy(rb, ra),
                   reads=rk_, writes=rbk)
            t.emit("gpsimd", lambda e, rb=rb, mj=mj, t0=t0, W=W: e.dma_start(
                out=s.s_r[mj * 128:(mj + 1) * 128, t0:t0 + W], in_=rb),
                reads=rbk, writes=[("s_r", mj)], dma=True)
        for mj in range(NPB):
            ra, rk_ = bW(s, mj * 512, mj * 512 + W)
            kma, kmk = bW2b(s, 1024, 1024 + W)
            t.emit("sync", lambda e, kma=kma, mj=mj, t0=t0, W=W: e.dma_start(
                out=kma, in_=s.s_kmod[mj * 128:(mj + 1) * 128, t0:t0 + W]),
                reads=[("s_kmod", mj)], writes=kmk, dma=True)
            rka, rkk = bW(s, s.SQ0, s.SQ0 + W)
            t.emit("vector", lambda e, rka=rka, ra=ra, kma=kma, mj=mj:
                   e.scalar_tensor_tensor(rka, ra, pvc(s, "vrk", mj), kma,
                                          ALU.mult, ALU.mult),
                   reads=rk_ + kmk + [("pv", "vrk")], writes=rkk)
            pss, rss = pb.take()
            t.emit("tensor", lambda e, pss=pss, rka=rka, W=W: e.matmul(
                pss[0:2, 0:W], s.hs2_ap(), rka, start=True, stop=True),
                reads=[("hsum2c",)] + rkk, writes=rss)
            t.emit("vector", lambda e, pss=pss, W=W: e.tensor_copy(
                s.b_two[:, 0:W], pss[0:2, 0:W]), reads=rss, writes=[("two", 0)])
            t.emit("gpsimd", lambda e, mj=mj, t0=t0, W=W: e.dma_start(
                out=s.s_srk[mj * 2:(mj + 1) * 2, t0:t0 + W], in_=s.b_two[:, 0:W]),
                reads=[("two", 0)], writes=[("s_srk", mj)], dma=True)

    load_bigw(s, s.wv, "wv")
    for (t0, t1) in segs(TC, 512):
        W = t1 - t0
        for j in range(NPB):
            da, dk = lerp_d(s, j, t0, W, s.LERP0)
            xa, xk_ = bW2b(s, j * 512, j * 512 + W)
            lerp_apply(s, j, t0, W, MIX["v"], da, dk, xa, xk_)
        outs = mm_proj(s, pb, lambda ks: bW2b(s, ks * 512, ks * 512 + W), None, W)
        for mj in range(NPB):
            p, rp = outs[mj]
            va, vk = bW2b(s, 0, W)
            t.emit("vector", lambda e, va=va, p=p, W=W: e.tensor_copy(
                va, p[:, 0:W]), reads=rp, writes=vk)
            t.emit("gpsimd", lambda e, va=va, mj=mj, t0=t0, W=W: e.dma_start(
                out=s.s_v[mj * 128:(mj + 1) * 128, t0:t0 + W], in_=va),
                reads=vk, writes=[("s_v", mj)], dma=True)

    for j in range(NPB):
        sza, szk = bW2b(s, 0, TC)
        t.emit("sync", lambda e, sza=sza, j=j: e.dma_start(
            out=sza, in_=s.s_sz[j * 128:(j + 1) * 128, :]),
            reads=[("s_sz", j)], writes=szk, dma=True)
        lpa, lpk = bW(s, 0, TC)
        t.emit("vector", lambda e, lpa=lpa, sza=sza: e.tensor_tensor_scan(
            lpa, s.b_rst[:], sza, 0.0, ALU.mult, ALU.add),
            reads=szk + [("rst",)], writes=lpk)
        t.emit("gpsimd", lambda e, lpa=lpa, j=j: e.dma_start(
            out=s.s_lnp[j * 128:(j + 1) * 128, :], in_=lpa),
            reads=lpk, writes=[("s_lnp", j)], dma=True)


def bdsl(s, i, w=128):
    return s.b_bd[:, i * 128:i * 128 + w], [("bd", i)]


def ktsl(s, i, w=64):
    return s.b_kt[:, i * 64:i * 64 + w], [("kt", i)]


def st8sl(s, i, w=64):
    return s.b_st8[:, i * 64:i * 64 + w], [("st8", i)]


EXPF = 0.60653065971263342


def ph_rwkv_chunks(s):
    t = s.t
    NPB = s.NPB; TC = s.TC; C = s.C
    s.cu_par = 0
    pb = Pb(s, range(6))
    t.emit("vector", lambda e: e.memset(s.b_bd[:], 0.0),
           writes=[("bd", i) for i in range(16)])
    t.emit("vector", lambda e: e.memset(s.b_kt[:], 0.0),
           writes=[("kt", i) for i in range(16)])
    HHALF = TC // 2
    for j in range(NPB):
        t.emit("vector", lambda e: e.tensor_copy(s.b_ec[0:64, 0:64],
                                                 s.b_cstI[0:64, 0:64]),
               reads=[("cstI",)], writes=[("ecE", 0)])
        t.emit("vector", lambda e: e.tensor_copy(s.b_ec[64:128, 0:64],
                                                 s.b_cstI[64:128, 64:128]),
               reads=[("cstI",)], writes=[("ecE", 0)])
        t.emit("vector", lambda e: e.memset(s.b_ec[:, 64:128], 0.0),
               writes=[("ecD", 0)])
        cur = 0
        for half in range(2):
            h0 = half * HHALF
            for nm, off in [("s_kmod", 0), ("s_ab", 1024), ("s_kap", 2048),
                            ("s_r", 3072), ("s_v", 4096)]:
                ap = getattr(s, nm)
                t.emit("sync", lambda e, ap=ap, off=off, j=j, h0=h0: e.dma_start(
                    out=s.b_W2b[:, off:off + HHALF],
                    in_=ap[j * 128:(j + 1) * 128, h0:h0 + HHALF]),
                    reads=[(nm, j)], writes=rk("W2b", off, off + HHALF), dma=True)
            t.emit("sync", lambda e, j=j, h0=h0: e.dma_start(
                out=s.b_W[:, 0:HHALF],
                in_=s.s_lnp[j * 128:(j + 1) * 128, h0:h0 + HHALF]),
                reads=[("s_lnp", j)], writes=rk("W", 0, HHALF), dma=True)
            t.emit("gpsimd", lambda e, j=j, h0=h0: e.dma_start(
                out=s.b_W[:, 2048:2048 + HHALF],
                in_=s.s_sz[j * 128:(j + 1) * 128, h0:h0 + HHALF]),
                reads=[("s_sz", j)], writes=rk("W", 2048, 2048 + HHALF), dma=True)
            for cl in range(HHALF // C):
                c = half * (HHALF // C) + cl
                cur = chunk_unit(s, pb, j, c, cl, cur)
        ecE = s.b_ec[:, cur * 128:cur * 128 + 64]
        ecD = s.b_ec[:, cur * 128 + 64:cur * 128 + 128]
        ef, efk = st8sl(s, 9, 64)
        t.emit("vector", lambda e, ecE=ecE, ef=ef: e.tensor_copy(ef, ecE),
               reads=[("ecE", cur)], writes=efk)
        p, rp = pb.take()
        t.emit("tensor", lambda e, p=p, ef=ef: e.transpose(
            p[0:64, 0:128], ef, s.b_cstI[:]),
            reads=efk + [("cstI",)], writes=rp)
        ea, ek = bW(s, s.EX0, s.EX0 + 128)
        t.emit("vector", lambda e, p=p, ea=ea: e.tensor_copy(ea[0:64, :], p[0:64, 0:128]),
               reads=rp, writes=ek)
        t.emit("gpsimd", lambda e, ea=ea, j=j: e.dma_start(
            out=s.agE_in[j * 64:(j + 1) * 64, :], in_=ea[0:64, :]),
            reads=ek, writes=[("agE_in", j)], dma=True)
        t.emit("gpsimd", lambda e, ecD=ecD, j=j: e.dma_start(
            out=s.agD_in[j * 128:(j + 1) * 128, :], in_=ecD),
            reads=[("ecD", cur)], writes=[("agD_in", j)], dma=True)


def chunk_unit(s, pb, j, c, cl, cur):
    t = s.t
    C = s.C; C2 = 2 * C; HD = s.cfg["HD"]
    a0 = cl * C
    up = s.cu_par
    s.cu_par = 1 - up
    so = 4 * up
    ko = 2 * up
    bo = 2 * up

    def w2bsl(off):
        return s.b_W2b[:, off + a0:off + a0 + C], rk("W2b", off + a0, off + a0 + C)

    lnp = s.b_W[:, a0:a0 + C]
    lnpk = rk("W", a0, a0 + C)
    sz = s.b_W[:, 2048 + a0:2048 + a0 + C]
    szk = rk("W", 2048 + a0, 2048 + a0 + C)
    lpp, lppk = st8sl(s, so + 0, C)
    t.emit("vector", lambda e: e.tensor_tensor(lpp, lnp, sz, ALU.subtract),
           reads=lnpk + szk, writes=lppk)
    invp, invpk = st8sl(s, so + 1, C)
    t.emit("scalar", lambda e: e.activation(out=invp, in_=lnp, func=ACT.Exp,
                                            scale=EXPF), reads=lnpk, writes=invpk)
    Pt, Ptk = st8sl(s, so + 2, C)
    t.emit("scalar", lambda e: e.activation(out=Pt, in_=lnp, func=ACT.Exp,
                                            scale=-EXPF), reads=lnpk, writes=Ptk)
    Ppv, Ppvk = st8sl(s, so + 3, C)
    t.emit("scalar", lambda e: e.activation(out=Ppv, in_=lpp, func=ACT.Exp,
                                            scale=-EXPF), reads=lppk, writes=Ppvk)
    pc_col = s.b_st8[:, (so + 2) * 64 + C - 1:(so + 2) * 64 + C]
    pc_keys = [("st8", so + 2)]

    for bdslot, off, sc, sck in [(12 + bo, 0, invp, invpk),
                                 (13 + bo, 1024, invp, invpk)]:
        ap, keys = w2bsl(off)
        for hh in range(2):
            t.emit("vector", lambda e, ap=ap, sc=sc, bdslot=bdslot, hh=hh:
                   e.tensor_tensor(
                       s.b_bd[hh * 64:(hh + 1) * 64,
                              bdslot * 128 + hh * C:bdslot * 128 + (hh + 1) * C],
                       ap[hh * 64:(hh + 1) * 64, :],
                       sc[hh * 64:(hh + 1) * 64, :], ALU.mult),
                   reads=keys + sck, writes=[("bd", bdslot)])
    for slot, off, sc, sck in [(2 + ko, 2048, Ppv, Ppvk),
                                (3 + ko, 3072, Pt, Ptk)]:
        ap, keys = w2bsl(off)
        kta, ktk = ktsl(s, slot, C)
        t.emit("vector", lambda e, kta=kta, ap=ap, sc=sc: e.tensor_tensor(
            kta, ap, sc, ALU.mult), reads=keys + sck, writes=ktk)
    vap, vkeys = w2bsl(4096)

    def trb(srcs):
        bank = s.pbt_i % 2
        s.pbt_i += 1
        p = s.psumb[bank]
        rp = [("psum", 6 + bank)]
        rd = []
        for i, (sap, skeys, hh) in enumerate(srcs):
            t.emit("tensor", lambda e, p=p, i=i, sap=sap, hh=hh: e.transpose(
                p[0:C, i * 64:i * 64 + 64], sap,
                s.b_cstIb[hh * 64:(hh + 1) * 64, hh * 64:(hh + 1) * 64]),
                reads=skeys + [("cstIb",)], writes=rp)
        return p, rp

    kapkt, kapktk = ktsl(s, 2 + ko, C)
    rbkt, rbktk = ktsl(s, 3 + ko, C)
    for bdslot, srcf, srck in [
            (9, lambda hh: kapkt[hh * 64:(hh + 1) * 64, 0:C], kapktk),
            (10, lambda hh: s.b_bd[hh * 64:(hh + 1) * 64,
                                   (12 + bo) * 128 + hh * C:(12 + bo) * 128 + (hh + 1) * C],
             [("bd", 12 + bo)]),
            (11, lambda hh: s.b_bd[hh * 64:(hh + 1) * 64,
                                   (13 + bo) * 128 + hh * C:(13 + bo) * 128 + (hh + 1) * C],
             [("bd", 13 + bo)])]:
        p, rp = trb([(srcf(hh), srck, hh) for hh in range(2)])
        for hh in range(2):
            t.emit("vector", lambda e, p=p, hh=hh, bdslot=bdslot: e.tensor_copy(
                s.b_bd[hh * C:hh * C + C, bdslot * 128 + hh * 64:bdslot * 128 + (hh + 1) * 64],
                p[0:C, hh * 64:(hh + 1) * 64]),
                reads=rp, writes=[("bd", bdslot)])
    p, rp = trb([(vap[hh * 64:(hh + 1) * 64, 0:C], vkeys, hh) for hh in range(2)])
    for hh in range(2):
        t.emit("vector", lambda e, p=p, hh=hh: e.tensor_copy(
            s.b_kt[hh * C:(hh + 1) * C, 7 * 64:8 * 64],
            p[0:C, hh * 64:(hh + 1) * 64]), reads=rp, writes=[("kt", 7)])
    vtv = s.b_kt[0:C2, 7 * 64:8 * 64]
    vtvk = [("kt", 7)]

    def trb_full(sap, skeys):
        bank = s.pbt_i % 2
        s.pbt_i += 1
        p = s.psumb[bank]
        rp = [("psum", 6 + bank)]
        t.emit("tensor", lambda e, p=p, sap=sap: e.transpose(
            p[0:C2, 0:C2], sap, s.b_cstIb[0:C2, 0:C2]),
            reads=skeys + [("cstIb",)], writes=rp)
        return p, rp

    def mmx(lhsT, lk, rhs, rhk, M, N):
        p, rp = pb.take()
        t.emit("tensor", lambda e, p=p: e.matmul(
            p[0:M, 0:N], lhsT, rhs, start=True, stop=True),
            reads=lk + rhk, writes=rp)
        return p, rp

    kbd = s.b_bd[:, (12 + bo) * 128:(12 + bo) * 128 + C2]
    kbdk = [("bd", 12 + bo)]
    bbd = s.b_bd[:, (13 + bo) * 128:(13 + bo) * 128 + C2]
    bbdk = [("bd", 13 + bo)]

    for bdslot, lhsTa, lk, rhsa, rhk, mask, maskk in [
            (0, kbd, kbdk, kapkt, kapktk, s.b_mstr, [("mstr",)]),
            (1, bbd, bbdk, kapkt, kapktk, s.b_mstr, [("mstr",)]),
            (6, kbd, kbdk, rbkt, rbktk, s.b_mtr, [("mtr",)]),
            (7, bbd, bbdk, rbkt, rbktk, s.b_mtr, [("mtr",)])]:
        p, rp = mmx(lhsTa, lk, rhsa, rhk, C2, C)
        for hh in range(2):
            t.emit("vector", lambda e, p=p, hh=hh, bdslot=bdslot, mask=mask:
                   e.tensor_tensor(
                       s.b_bd[hh * C:hh * C + C,
                              bdslot * 128 + hh * C:bdslot * 128 + (hh + 1) * C],
                       p[hh * C:hh * C + C, 0:C], mask[hh * C:hh * C + C, 0:C],
                       ALU.mult),
                   reads=rp + maskk, writes=[("bd", bdslot)])

    La = s.b_bd[0:C2, 1 * 128:1 * 128 + C2]
    Lk = [("bd", 1)]
    Tm = s.b_bd[0:C2, 2 * 128:2 * 128 + C2]
    Tmk = [("bd", 2)]
    t.emit("vector", lambda e: e.tensor_tensor(
        Tm, s.b_cstIb[0:C2, 0:C2], La, ALU.subtract),
        reads=Lk + [("cstIb",)], writes=Tmk)
    Lp = s.b_bd[0:C2, 3 * 128:3 * 128 + C2]
    Lpk = [("bd", 3)]
    t.emit("vector", lambda e: e.tensor_copy(Lp, La), reads=Lk, writes=Lpk)
    LpT = s.b_bd[0:C2, 4 * 128:4 * 128 + C2]
    LpTk = [("bd", 4)]
    pb0, rpb0 = trb_full(Lp, Lpk)
    t.emit("vector", lambda e, pb0=pb0: e.tensor_copy(LpT, pb0[0:C2, 0:C2]),
           reads=rpb0, writes=LpTk)
    e_ = 2
    while e_ < C:
        p2, rp2 = mmx(LpT, LpTk, Lp, Lpk, C2, C2)
        t.emit("vector", lambda e, p2=p2: e.tensor_copy(Lp, p2[0:C2, 0:C2]),
               reads=rp2, writes=Lpk)
        pb3, rpb3 = trb_full(Lp, Lpk)
        t.emit("vector", lambda e, pb3=pb3: e.tensor_copy(LpT, pb3[0:C2, 0:C2]),
               reads=rpb3, writes=LpTk)
        p4, rp4 = mmx(LpT, LpTk, Tm, Tmk, C2, C2)
        t.emit("vector", lambda e, p4=p4: e.scalar_tensor_tensor(
            Tm, p4[0:C2, 0:C2], 1.0, Tm, ALU.mult, ALU.add),
            reads=rp4 + Tmk, writes=Tmk)
        e_ *= 2
    TTS = s.b_bd[0:C2, 5 * 128:5 * 128 + C2]
    TTSk = [("bd", 5)]
    pb5, rpb5 = trb_full(Tm, Tmk)
    t.emit("vector", lambda e, pb5=pb5: e.tensor_copy(TTS, pb5[0:C2, 0:C2]),
           reads=rpb5, writes=TTSk)

    Mkk = s.b_bd[0:C2, 0:C2]
    Mkkk = [("bd", 0)]
    p6, rp6 = mmx(Mkk, Mkkk, vtv, vtvk, C2, 64)
    mkv, mkvk = ktsl(s, 8, 64)
    t.emit("vector", lambda e, p6=p6: e.tensor_copy(
        s.b_kt[0:C2, 8 * 64:9 * 64], p6[0:C2, 0:64]), reads=rp6, writes=mkvk)
    p7, rp7 = mmx(Tm, Tmk, s.b_kt[0:C2, 8 * 64:9 * 64], mkvk, C2, 64)
    t.emit("vector", lambda e, p7=p7: e.tensor_scalar(
        s.b_kt[0:C2, 9 * 64:10 * 64], p7[0:C2, 0:64], -1.0, None, ALU.mult),
        reads=rp7, writes=[("kt", 9)])
    oneg = s.b_kt[0:C2, 9 * 64:10 * 64]
    onegk = [("kt", 9)]

    LRK = s.b_bd[0:C2, 6 * 128:6 * 128 + C2]
    LRKk = [("bd", 6)]
    LRB = s.b_bd[0:C2, 7 * 128:7 * 128 + C2]
    LRBk = [("bd", 7)]
    p8, rp8 = pb.take()
    t.emit("tensor", lambda e, p8=p8: e.matmul(
        p8[0:C2, 0:64], LRK, vtv, start=True, stop=False),
        reads=LRKk + vtvk, writes=rp8)
    t.emit("tensor", lambda e, p8=p8: e.matmul(
        p8[0:C2, 0:64], LRB, oneg, start=False, stop=True),
        reads=LRBk + onegk, writes=rp8)
    yl, ylk = bW2b(s, 5120, 5120 + 64)
    t.emit("vector", lambda e, p8=p8, yl=yl: e.tensor_copy(yl[0:C2, :], p8[0:C2, 0:64]),
           reads=rp8, writes=ylk)
    t.emit("gpsimd", lambda e, yl=yl, j=j, c=c: e.dma_start(
        out=s.s_pack[(c * s.NPB + j) * 128:(c * s.NPB + j) * 128 + C2, 320:384],
        in_=yl[0:C2, :]),
        reads=ylk, writes=[("s_pack", j, c)], dma=True)

    bbtk = s.b_bd[0:C2, 11 * 128:12 * 128]
    bbtkk = [("bd", 11)]
    p9, rp9 = mmx(TTS, TTSk, bbtk, bbtkk, C2, 128)
    x1bk = [("kt", 10), ("kt", 11)]
    t.emit("vector", lambda e, p9=p9: e.tensor_copy(
        s.b_kt[0:C2, 10 * 64:12 * 64], p9[0:C2, 0:128]), reads=rp9, writes=x1bk)
    kaptkbd = s.b_bd[0:C2, 9 * 128:10 * 128]
    kaptkbdk = [("bd", 9)]
    p10, rp10 = mmx(kaptkbd, kaptkbdk, s.b_kt[0:C2, 10 * 64:12 * 64], x1bk, 128, 128)
    imjk2 = [("kt", 12), ("kt", 13)]
    t.emit("vector", lambda e, p10=p10: e.scalar_tensor_tensor(
        s.b_kt[:, 12 * 64:14 * 64], p10[:, 0:128], -1.0, s.b_cstI[:],
        ALU.mult, ALU.add),
        reads=rp10 + [("cstI",)], writes=imjk2)
    imjt = s.b_kt[:, 12 * 64:14 * 64]
    imjtk = imjk2
    t.emit("gpsimd", lambda e, j=j, c=c: e.dma_start(
        out=s.s_pack[(c * s.NPB + j) * 128:(c * s.NPB + j) * 128 + 128, 0:128],
        in_=imjt),
        reads=imjtk, writes=[("s_pack", j, c)], dma=True)

    p11, rp11 = mmx(TTS, TTSk, LRB, LRBk, C2, C2)
    wst, wstk = bdsl(s, 8, C2)
    t.emit("vector", lambda e, p11=p11: e.tensor_copy(
        s.b_bd[0:C2, 8 * 128:8 * 128 + C2], p11[0:C2, 0:C2]), reads=rp11, writes=wstk)
    p12, rp12 = mmx(kaptkbd, kaptkbdk, s.b_bd[0:C2, 8 * 128:8 * 128 + C2], wstk,
                    128, C2)
    rhatk = [("kt", 14), ("kt", 15)]
    for hh in range(2):
        t.emit("vector", lambda e, p12=p12, hh=hh: e.scalar_tensor_tensor(
            s.b_kt[hh * 64:(hh + 1) * 64, 14 * 64 + hh * C:14 * 64 + (hh + 1) * C],
            p12[hh * 64:(hh + 1) * 64, hh * C:(hh + 1) * C], -1.0,
            rbkt[hh * 64:(hh + 1) * 64, 0:C], ALU.mult, ALU.add),
            reads=rp12 + rbktk, writes=rhatk)
    t.emit("gpsimd", lambda e, j=j, c=c: e.dma_start(
        out=s.s_pack[(c * s.NPB + j) * 128:(c * s.NPB + j) * 128 + 128, 128:256],
        in_=s.b_kt[:, 14 * 64:16 * 64]),
        reads=rhatk, writes=[("s_pack", j, c)], dma=True)

    kbtkbd = s.b_bd[0:C2, 10 * 128:11 * 128]
    kbtkbdk = [("bd", 10)]
    p13, rp13 = pb.take()
    t.emit("tensor", lambda e, p13=p13: e.matmul(
        p13[:, 0:64], kbtkbd, vtv, start=True, stop=False),
        reads=kbtkbdk + vtvk, writes=rp13)
    t.emit("tensor", lambda e, p13=p13: e.matmul(
        p13[:, 0:64], bbtk, oneg, start=False, stop=True),
        reads=bbtkk + onegk, writes=rp13)
    dlt, dltk = ktsl(s, 6, 64)
    t.emit("vector", lambda e, p13=p13, dlt=dlt: e.tensor_scalar(
        dlt, p13[:, 0:64], pc_col, None, ALU.mult),
        reads=rp13 + pc_keys, writes=dltk)
    t.emit("gpsimd", lambda e, dlt=dlt, j=j, c=c: e.dma_start(
        out=s.s_pack[(c * s.NPB + j) * 128:(c * s.NPB + j) * 128 + 128, 256:320],
        in_=dlt),
        reads=dltk, writes=[("s_pack", j, c)], dma=True)

    nxt = 1 - cur
    ecE = s.b_ec[:, cur * 128:cur * 128 + 64]
    ecD = s.b_ec[:, cur * 128 + 64:cur * 128 + 128]
    ecEn = s.b_ec[:, nxt * 128:nxt * 128 + 64]
    ecDn = s.b_ec[:, nxt * 128 + 64:nxt * 128 + 128]
    p14, rp14 = mmx(imjt, imjtk, ecE, [("ecE", cur)], 128, 64)
    t.emit("vector", lambda e, p14=p14, ecEn=ecEn: e.tensor_scalar(
        ecEn, p14[:, 0:64], pc_col, None, ALU.mult),
        reads=rp14 + pc_keys, writes=[("ecE", nxt)])
    p15, rp15 = mmx(imjt, imjtk, ecD, [("ecD", cur)], 128, 64)
    t.emit("vector", lambda e, p15=p15, ecDn=ecDn, dlt=dlt: e.scalar_tensor_tensor(
        ecDn, p15[:, 0:64], pc_col, dlt, ALU.mult, ALU.add),
        reads=rp15 + pc_keys + dltk, writes=[("ecD", nxt)])
    return nxt


def ph_rwkv_post(s):
    t = s.t
    NPB = s.NPB; TC = s.TC; C = s.C; C2 = 2 * C; NCH = s.NCH; HD = s.cfg["HD"]
    NCR = s.NCR
    pb = Pb(s)
    t.emit("gpsimd", lambda e: e.collective_compute(
        "AllGather", ALU.bypass, replica_groups=[list(range(NCR))],
        ins=[s.agE_in.opt()], outs=[s.agE_out.opt()]),
        reads=[("agE_in", j) for j in range(NPB)], writes=[("agE_out",)], cc=True)
    t.emit("gpsimd", lambda e: e.collective_compute(
        "AllGather", ALU.bypass, replica_groups=[list(range(NCR))],
        ins=[s.agD_in.opt()], outs=[s.agD_out.opt()]),
        reads=[("agD_in", j) for j in range(NPB)], writes=[("agD_out",)], cc=True)
    t.emit("sync", lambda e: e.dma_start(out=s.b_pfx[:], in_=s.pfxm64),
           reads=[("pfxm64",)], writes=[("pfx",)], dma=True)
    t.emit("vector", lambda e: e.tensor_copy(s.b_bd[0:64, 0:64],
                                             s.b_cstI[0:64, 0:64]),
           reads=[("cstI",)], writes=[("bd", 0)])
    t.emit("vector", lambda e: e.tensor_copy(s.b_bd[64:128, 0:64],
                                             s.b_cstI[64:128, 64:128]),
           reads=[("cstI",)], writes=[("bd", 0)])
    isbs = s.b_bd[:, 0:64]
    isbsk = [("bd", 0)]
    t.emit("vector", lambda e: e.memset(s.b_ec[:], 0.0),
           writes=[("ecE", 0), ("ecE", 1), ("ecD", 0), ("ecD", 1)] +
           [("ecS", jj) for jj in range(NPB)])

    for j in range(NPB):
        if NCR == 1:
            t.emit("vector", lambda e, j=j: e.memset(
                s.b_ec[:, (2 + j) * 64:(3 + j) * 64], 0.0), writes=[("ecS", j)])
            continue
        t.emit("vector", lambda e: e.memset(s.b_ec[:, 0:64], 0.0),
               writes=[("ecE", 0)])
        scur = 0
        for r in range(NCR - 1):
            era = s.b_st8[:, 0:64]
            erk = [("st8", 0)]
            t.emit("sync", lambda e, r=r, j=j, era=era: e.dma_start(
                out=era,
                in_=s.agE_out[(r * NPB + j) * 64:(r * NPB + j) * 64 + 64, :]
                .rearrange("k (h kp) -> h k kp", h=2)),
                reads=[("agE_out",)], writes=erk, dma=True)
            dra = s.b_st8[:, 128:192]
            drk = [("st8", 2)]
            t.emit("sync", lambda e, r=r, j=j, dra=dra: e.dma_start(
                out=dra, in_=s.agD_out[(r * NPB + j) * 128:(r * NPB + j) * 128 + 128, :]),
                reads=[("agD_out",)], writes=drk, dma=True)
            t1a = s.b_st8[:, 192:256]
            t1k = [("st8", 3)]
            t.emit("vector", lambda e, era=era, t1a=t1a: e.tensor_tensor(
                t1a, era, isbs, ALU.subtract), reads=erk + isbsk, writes=t1k)
            eeff = s.b_bd[:, 128:192]
            eeffk = [("bd", 1)]
            t.emit("vector", lambda e, t1a=t1a, eeff=eeff, r=r: e.scalar_tensor_tensor(
                eeff, t1a, s.b_pfx[:, r:r + 1], isbs, ALU.mult, ALU.add),
                reads=t1k + isbsk + [("pfx",)], writes=eeffk)
            deff, deffk = ktsl(s, 14, 64)
            t.emit("vector", lambda e, dra=dra, deff=deff, r=r: e.tensor_scalar(
                deff, dra, s.b_pfx[:, r:r + 1], None, ALU.mult),
                reads=drk + [("pfx",)], writes=deffk)
            last = (r == NCR - 2)
            sTa = s.b_ec[:, scur * 64:scur * 64 + 64]
            if last:
                sTn = s.b_ec[:, (2 + j) * 64:(3 + j) * 64]
                snk = [("ecS", j)]
            else:
                snxt = 1 - scur
                sTn = s.b_ec[:, snxt * 64:snxt * 64 + 64]
                snk = [("ecE", snxt)]
            p, rp = pb.take()
            for hh in range(2):
                t.emit("tensor", lambda e, p=p, hh=hh, eeff=eeff, sTa=sTa: e.matmul(
                    p[0:64, hh * 64:(hh + 1) * 64],
                    eeff[hh * 64:(hh + 1) * 64, 0:64],
                    sTa[hh * 64:(hh + 1) * 64, :], start=True, stop=True),
                    reads=eeffk + [("ecE", scur)], writes=rp)
            stk, stkk = st8sl(s, 4, 64)
            for hh in range(2):
                t.emit("vector", lambda e, p=p, hh=hh, stk=stk: e.tensor_copy(
                    stk[hh * 64:(hh + 1) * 64, :], p[0:64, hh * 64:(hh + 1) * 64]),
                    reads=rp, writes=stkk)
            t.emit("vector", lambda e, stk=stk, sTn=sTn, deff=deff: e.tensor_tensor(
                sTn, stk, deff, ALU.add),
                reads=stkk + deffk, writes=snk)
            if not last:
                scur = 1 - scur
    for j in range(NPB):
        pcc = s.b_st8[:, 512 + j * NCH:512 + (j + 1) * NCH]
        lnpe = s.b_st8[:, 0:NCH]
        lnpek = [("st8", 0)]
        t.emit("sync", lambda e, j=j, lnpe=lnpe: e.dma_start(
            out=lnpe, in_=s.s_lnp[j * 128:(j + 1) * 128, :].rearrange(
                "p (n c) -> p n c", c=C)[:, :, C - 1],
            allow_slow_non_contiguous=True),
            reads=[("s_lnp", j)], writes=lnpek, dma=True)
        t.emit("scalar", lambda e, pcc=pcc, lnpe=lnpe: e.activation(
            out=pcc, in_=lnpe, func=ACT.Exp, scale=-EXPF),
            reads=lnpek, writes=[("st8pc", j)])
    GC = 512 // HD
    for c in range(NCH):
        cg = c % GC
        for j in range(NPB):
            par = j % 2
            kb = par * 512
            imjt = s.b_kt[:, kb:kb + 128]
            imjk = [("kt", 8 * par), ("kt", 8 * par + 1)]
            rhat = s.b_kt[:, kb + 128:kb + 256]
            rhk = [("kt", 8 * par + 2), ("kt", 8 * par + 3)]
            dlt = s.b_kt[:, kb + 256:kb + 320]
            dltk = [("kt", 8 * par + 4)]
            ylc = s.b_kt[:, kb + 320:kb + 384]
            ylck = [("kt", 8 * par + 5)]
            t.emit("sync", lambda e, j=j, c=c, kb=kb: e.dma_start(
                out=s.b_kt[:, kb:kb + 384],
                in_=s.s_pack[(c * NPB + j) * 128:(c * NPB + j) * 128 + 128, :]),
                reads=[("s_pack", j, c)],
                writes=imjk + rhk + dltk + ylck, dma=True)
            sT = s.b_ec[:, (2 + j) * 64:(3 + j) * 64]
            sTk = [("ecS", j)]
            p, rp = pb.take()
            t.emit("tensor", lambda e, p=p, rhat=rhat, sT=sT: e.matmul(
                p[0:C2, 0:64], rhat[:, 0:C2], sT, start=True, stop=True),
                reads=rhk + sTk, writes=rp)
            ya, yak = bW(s, j * 512 + cg * 64, j * 512 + cg * 64 + 64)
            t.emit("vector", lambda e, p=p, ya=ya, ylc=ylc: e.tensor_tensor(
                ya[0:C2, :], p[0:C2, 0:64], ylc[0:C2, :], ALU.add),
                reads=rp + ylck, writes=yak)
            p2, rp2 = pb.take()
            t.emit("tensor", lambda e, p2=p2, imjt=imjt, sT=sT: e.matmul(
                p2[:, 0:64], imjt, sT, start=True, stop=True),
                reads=imjk + sTk, writes=rp2)
            t.emit("vector", lambda e, p2=p2, sT=sT, dlt=dlt, c=c, j=j:
                   e.scalar_tensor_tensor(
                       sT, p2[:, 0:64], s.b_st8[:, 512 + j * NCH + c:512 + j * NCH + c + 1],
                       dlt, ALU.mult, ALU.add),
                   reads=rp2 + [("st8pc", j)] + dltk, writes=sTk)
            if cg == GC - 1 or c == NCH - 1:
                gn_group(s, pb, j, c - cg, cg + 1)

    load_bigw(s, s.wo, "wo")
    XW = s.XW
    for (t0, t1) in segs(TC, 512):
        W = t1 - t0
        for ks in range(NPB):
            t.emit("sync", lambda e, ks=ks, t0=t0, W=W: e.dma_start(
                out=s.b_W2b[:, ks * 512:ks * 512 + W],
                in_=s.s_yg[ks * 128:(ks + 1) * 128, t0:t0 + W]),
                reads=[("s_yg", ks)], writes=rk("W2b", ks * 512, ks * 512 + W),
                dma=True)
        outs = mm_proj(s, pb, lambda ks: bW2b(s, ks * 512, ks * 512 + W), None, W)
        for mj in range(NPB):
            p, rp = outs[mj]
            xa = s.b_x1[:, mj * XW + 1 + t0:mj * XW + 1 + t1]
            xkeys = rk("x1", mj * XW + 1 + t0, mj * XW + 1 + t1)
            t.emit("vector", lambda e, xa=xa, p=p, W=W: e.tensor_tensor(
                xa, xa, p[:, 0:W], ALU.add), reads=rp + xkeys, writes=xkeys)


def gn_group(s, pb, j, c0, gc):
    t = s.t
    C = s.C; C2 = 2 * C; HD = s.cfg["HD"]; TC = s.TC
    W = gc * HD
    par = j % 2
    ybase = j * 512
    gbase = s.NPB * 512 + par * 2048
    tb0 = par * 64
    yk = rk("W", ybase, ybase + 512)

    def tk(n):
        return [("tokg", par, n)]
    sq, sqk = bW(s, gbase, gbase + W)
    t.emit("scalar", lambda e, sq=sq: e.activation(
        out=sq[0:C2, :], in_=s.b_W[0:C2, ybase:ybase + W], func=ACT.Square),
        reads=yk, writes=sqk)
    su = s.b_tok[:, tb0 + 0:tb0 + gc]
    t.emit("vector", lambda e, gc=gc: e.tensor_reduce(
        su[0:C2, :], s.b_W[0:C2, ybase:ybase + W].rearrange("p (g v) -> p g v", v=HD),
        AX.X, ALU.add), reads=yk, writes=tk(0))
    s2 = s.b_tok[:, tb0 + 8:tb0 + 8 + gc]
    t.emit("vector", lambda e, sq=sq, gc=gc: e.tensor_reduce(
        s2[0:C2, :], sq[0:C2, :].rearrange("p (g v) -> p g v", v=HD),
        AX.X, ALU.add), reads=sqk, writes=tk(1))
    mu = s.b_tok[:, tb0 + 16:tb0 + 16 + gc]
    t.emit("vector", lambda e, gc=gc: e.tensor_scalar(
        mu[0:C2, :], su[0:C2, :], 1.0 / HD, None, ALU.mult),
        reads=tk(0), writes=tk(2))
    ex2 = s.b_tok[:, tb0 + 24:tb0 + 24 + gc]
    t.emit("vector", lambda e, gc=gc: e.tensor_scalar(
        ex2[0:C2, :], s2[0:C2, :], 1.0 / HD, None, ALU.mult),
        reads=tk(1), writes=tk(3))
    mu2 = s.b_tok[:, tb0 + 32:tb0 + 32 + gc]
    t.emit("vector", lambda e: e.tensor_tensor(
        mu2[0:C2, :], mu[0:C2, :], mu[0:C2, :], ALU.mult),
        reads=tk(2), writes=tk(4))
    t.emit("vector", lambda e: e.tensor_tensor(
        ex2[0:C2, :], ex2[0:C2, :], mu2[0:C2, :], ALU.subtract),
        reads=tk(3) + tk(4), writes=tk(3))
    t.emit("vector", lambda e: e.tensor_scalar(
        ex2[0:C2, :], ex2[0:C2, :], 0.00064, None, ALU.add),
        reads=tk(3), writes=tk(3))
    t.emit("vector", lambda e: e.reciprocal(mu2[0:C2, :], ex2[0:C2, :]),
           reads=tk(3), writes=tk(4))
    rstd = s.b_tok[:, tb0 + 40:tb0 + 40 + gc]
    t.emit("scalar", lambda e: e.activation(
        out=rstd[0:C2, :], in_=mu2[0:C2, :], func=ACT.Sqrt),
        reads=tk(4), writes=tk(5))
    nt = s.b_tok[:, tb0 + 48:tb0 + 48 + gc]
    t.emit("vector", lambda e: e.tensor_tensor(
        nt[0:C2, :], rstd[0:C2, :], rstd[0:C2, :], ALU.mult),
        reads=tk(5), writes=tk(6))
    t.emit("vector", lambda e: e.tensor_tensor(
        nt[0:C2, :], nt[0:C2, :], ex2[0:C2, :], ALU.mult),
        reads=tk(3) + tk(6), writes=tk(6))
    t.emit("vector", lambda e: e.tensor_scalar(
        nt[0:C2, :], nt[0:C2, :], -0.5, 1.5, ALU.mult, ALU.add),
        reads=tk(6), writes=tk(6))
    t.emit("vector", lambda e: e.tensor_tensor(
        rstd[0:C2, :], rstd[0:C2, :], nt[0:C2, :], ALU.mult),
        reads=tk(6) + tk(5), writes=tk(5))
    ms = s.b_tok[:, tb0 + 56:tb0 + 56 + gc]
    t.emit("vector", lambda e: e.tensor_tensor(
        ms[0:C2, :], mu[0:C2, :], rstd[0:C2, :], ALU.mult),
        reads=tk(2) + tk(5), writes=tk(7))
    t.emit("vector", lambda e: e.tensor_scalar(
        ms[0:C2, :], ms[0:C2, :], -1.0, None, ALU.mult),
        reads=tk(7), writes=tk(7))
    yg3 = s.b_W[0:C2, ybase:ybase + W].rearrange("p (g v) -> p g v", v=HD)
    ygn, ygnk = bW(s, gbase + 512, gbase + 512 + W)
    ygn3 = ygn[0:C2, :].rearrange("p (g v) -> p g v", v=HD)
    rstd3 = rstd[0:C2, :].unsqueeze(2).broadcast_to([C2, gc, HD])
    ms3 = ms[0:C2, :].unsqueeze(2).broadcast_to([C2, gc, HD])
    t.emit("vector", lambda e: e.tensor_tensor(ygn3, yg3, rstd3, ALU.mult),
           reads=yk + tk(5), writes=ygnk)
    t.emit("vector", lambda e: e.tensor_tensor(ygn3, ygn3, ms3, ALU.add),
           reads=ygnk + tk(7), writes=ygnk)
    yb2 = gbase + 1024
    yf, yfk = bW(s, yb2, yb2 + gc * C)
    for cl in range(gc):
        p, rp = pb.take()
        for hh in range(2):
            t.emit("tensor", lambda e, p=p, hh=hh, cl=cl: e.transpose(
                p[0:64, hh * C:(hh + 1) * C],
                ygn[hh * C:hh * C + C, cl * HD:cl * HD + 64],
                s.b_cstI[hh * C:hh * C + C, hh * C:hh * C + C]),
                reads=ygnk + [("cstI",)], writes=rp)
        for hh in range(2):
            t.emit("vector", lambda e, p=p, hh=hh, cl=cl: e.tensor_copy(
                s.b_W[hh * 64:(hh + 1) * 64, yb2 + cl * C:yb2 + (cl + 1) * C],
                p[0:64, hh * C:(hh + 1) * C]),
                reads=rp, writes=yfk)
    WT = gc * C
    t.emit("vector", lambda e, j=j: e.tensor_scalar(
        yf, yf, pvc(s, "vlnw", j), pvc(s, "vlnb", j), ALU.mult, ALU.add),
        reads=yfk + [("pv", "vlnw"), ("pv", "vlnb")], writes=yfk)
    t0tok = c0 * C
    tw0 = par * 512
    t.emit("sync", lambda e, j=j, t0tok=t0tok, WT=WT, tw0=tw0: e.dma_start(
        out=s.b_two[:, tw0:tw0 + WT], in_=s.s_srk[j * 2:(j + 1) * 2, t0tok:t0tok + WT]),
        reads=[("s_srk", j)], writes=[("twog", par)], dma=True)
    psr, rsr = pb.take()
    t.emit("tensor", lambda e, psr=psr, WT=WT, tw0=tw0: e.matmul(
        psr[:, 0:WT], s.hexp2_ap(), s.b_two[0:2, tw0:tw0 + WT], start=True, stop=True),
        reads=[("hexp2c",), ("twog", par)], writes=rsr)
    vt, vtk = bW2b(s, par * 512, par * 512 + WT)
    t.emit("sync", lambda e, j=j, t0tok=t0tok, WT=WT, vt=vt: e.dma_start(
        out=vt, in_=s.s_v[j * 128:(j + 1) * 128, t0tok:t0tok + WT]),
        reads=[("s_v", j)], writes=vtk, dma=True)
    tmp, tmpk = bW(s, gbase + 1536, gbase + 1536 + WT)
    t.emit("vector", lambda e, psr=psr, vt=vt, tmp=tmp, WT=WT: e.tensor_tensor(
        tmp, psr[:, 0:WT], vt, ALU.mult), reads=rsr + vtk, writes=tmpk)
    t.emit("vector", lambda e, tmp=tmp: e.tensor_tensor(
        yf, yf, tmp, ALU.add), reads=tmpk + yfk, writes=yfk)
    gt, gtk = bW2b(s, 1024 + par * 512, 1024 + par * 512 + WT)
    t.emit("sync", lambda e, j=j, t0tok=t0tok, WT=WT, gt=gt: e.dma_start(
        out=gt, in_=s.s_g[j * 128:(j + 1) * 128, t0tok:t0tok + WT]),
        reads=[("s_g", j)], writes=gtk, dma=True)
    ygb, ygbk = bW2b(s, 2048 + par * 512, 2048 + par * 512 + WT)
    t.emit("vector", lambda e, gt=gt, ygb=ygb: e.tensor_tensor(
        ygb, yf, gt, ALU.mult), reads=yfk + gtk, writes=ygbk)
    t.emit("gpsimd", lambda e, j=j, t0tok=t0tok, WT=WT, ygb=ygb: e.dma_start(
        out=s.s_yg[j * 128:(j + 1) * 128, t0tok:t0tok + WT], in_=ygb),
        reads=ygbk, writes=[("s_yg", j)], dma=True)

def ph_ln1(s):
    t = s.t
    NPB = s.NPB; TC = s.TC; XW = s.XW; NCR = s.NCR
    pb = Pb(s)
    ln_channels(s, srcb="x1", dstb="xn0", eps=1e-5)
    for j in range(NPB):
        for (src_c, dst_c) in [(1, 0), (TC - 2, 3)]:
            t.emit("gpsimd", lambda e, j=j, src_c=src_c, dst_c=dst_c: e.dma_start(
                out=s.agH_in[j * 128:(j + 1) * 128, dst_c:dst_c + 3],
                in_=s.b_xn0[:, j * XW + src_c:j * XW + src_c + 3]),
                reads=rk("xn0", j * XW, (j + 1) * XW), writes=[("agH_in", j)],
                dma=True)
    t.emit("gpsimd", lambda e: e.collective_compute(
        "AllGather", ALU.bypass, replica_groups=[list(range(NCR))],
        ins=[s.agH_in.opt()], outs=[s.agH_out.opt()]),
        reads=[("agH_in", j) for j in range(NPB)], writes=[("agH_out",)], cc=True)
    for j in range(NPB):
        t.emit("gpsimd", lambda e, j=j: e.dma_start(
            out=s.s_xn1[j * 128:(j + 1) * 128, 3:TC + 3],
            in_=s.b_xn0[:, j * XW + 1:j * XW + 1 + TC]),
            reads=rk("xn0", j * XW, (j + 1) * XW), writes=[("s_xn1", j)], dma=True)
    for j in range(NPB):
        ha, hk = bW(s, 0, 8)
        t.emit("vector", lambda e, ha=ha: e.memset(ha[:, 0:6], 0.0), writes=hk)
        for r in range(NCR):
            ga, gk = bW(s, 512, 512 + 6)
            t.emit("sync", lambda e, r=r, j=j, ga=ga: e.dma_start(
                out=ga, in_=s.agH_out[r * s.D + j * 128:r * s.D + (j + 1) * 128, :]),
                reads=[("agH_out",)], writes=gk, dma=True)
            t.emit("vector", lambda e, ha=ha, ga=ga, r=r: e.scalar_tensor_tensor(
                ha[:, 0:3], ga[:, 3:6], s.b_msk[:, r:r + 1], ha[:, 0:3], ALU.mult, ALU.add),
                reads=gk + hk + [("mskc",)], writes=hk)
            t.emit("vector", lambda e, ha=ha, ga=ga, r=r: e.scalar_tensor_tensor(
                ha[:, 3:6], ga[:, 0:3], s.b_msk[:, NCR + r:NCR + r + 1], ha[:, 3:6],
                ALU.mult, ALU.add),
                reads=gk + hk + [("mskc",)], writes=hk)
        t.emit("gpsimd", lambda e, j=j, ha=ha: e.dma_start(
            out=s.s_xn1[j * 128:(j + 1) * 128, 0:3], in_=ha[:, 0:3]),
            reads=hk, writes=[("s_xn1", j)], dma=True)
        t.emit("gpsimd", lambda e, j=j, ha=ha: e.dma_start(
            out=s.s_xn1[j * 128:(j + 1) * 128, TC + 3:TC + 6], in_=ha[:, 3:6]),
            reads=hk, writes=[("s_xn1", j)], dma=True)


def ph_mamba_prep(s):
    t = s.t
    NPB = s.NPB; TC = s.TC; C2 = s.cfg["C2"]; NCR = s.NCR; DSL = s.DSL
    NQ = C2 // 128
    DTR = s.cfg["DTR"]
    pb = Pb(s)
    t.emit("gpsimd", lambda e: e.dma_start(
        out=s.b_wt[:, 0:NQ * 96],
        in_=s.xprojt.rearrange("(q p) m -> p q m", p=128)),
        reads=[("xprojt",)], writes=[("wt", 0)], dma=True)
    t.emit("gpsimd", lambda e: e.dma_start(
        out=s.b_wt[:, 4096:4096 + s.D][0:DTR, :], in_=s.dtprojt),
        reads=[("dtprojt",)], writes=[("wt", 4)], dma=True)

    for dire in range(2):
        for q in range(NQ):
            xe, xek = bW2b(s, 0, TC + 6)
            t.emit("sync", lambda e, q=q, xe=xe: e.dma_start(
                out=xe, in_=s.s_xn1[q * 64:(q + 1) * 64, :]
                .unsqueeze(1).broadcast_to([64, 2, TC + 6])),
                reads=[("s_xn1", q // 2)], writes=xek, dma=True)
            for (a, b) in segs(TC, 512):
                W = b - a
                cva, cvk = bW(s, 0, W)
                for jj in range(4):
                    off = (a + jj) if dire == 0 else (a + 6 - jj)
                    if jj == 0:
                        t.emit("vector", lambda e, cva=cva, xe=xe, off=off, q=q, jj=jj, W=W:
                               e.tensor_scalar(cva, xe[:, off:off + W],
                                               pvc(s, "cw1", 4 * q + jj), None, ALU.mult),
                               reads=xek + [("pv", "cw1")], writes=cvk)
                    else:
                        t.emit("vector", lambda e, cva=cva, xe=xe, off=off, q=q, jj=jj, W=W:
                               e.scalar_tensor_tensor(cva, xe[:, off:off + W],
                                                      pvc(s, "cw1", 4 * q + jj), cva,
                                                      ALU.mult, ALU.add),
                               reads=xek + cvk + [("pv", "cw1")], writes=cvk)
                xca, xck = bW2b(s, 4096, 4096 + W)
                t.emit("scalar", lambda e, cva=cva, xca=xca, q=q: e.activation(
                    out=xca, in_=cva, func=ACT.Silu, bias=pvc(s, "cb1", q), scale=1.0),
                    reads=cvk + [("pv", "cb1")], writes=xck)
                t.emit("gpsimd", lambda e, xca=xca, q=q, a=a, W=W, dire=dire: e.dma_start(
                    out=s.s_xc[dire * C2 + q * 128:dire * C2 + (q + 1) * 128, a:a + W],
                    in_=xca),
                    reads=xck, writes=[("s_xc", dire, q)], dma=True)
                pxm, rxm = pb.take()
                t.emit("tensor", lambda e, pxm=pxm, xca=xca, q=q, W=W: e.matmul(
                    pxm[(q % 2) * 64:(q % 2) * 64 + 64, 0:W], s.b_ps64[:, 0:64],
                    xca, start=True, stop=True),
                    reads=[("ps64c",)] + xck, writes=rxm)
                dj = q // 2
                t.emit("vector", lambda e, pxm=pxm, q=q, dj=dj, a=a, W=W: e.tensor_copy(
                    s.b_W2[(q % 2) * 64:(q % 2) * 64 + 64, dj * TC + a:dj * TC + a + W],
                    pxm[(q % 2) * 64:(q % 2) * 64 + 64, 0:W]),
                    reads=rxm, writes=rk("W2", dj * TC + a, dj * TC + a + W))
        NRB = 128 // DSL
        for dj in range(NPB):
            t.emit("sync", lambda e, dj=dj, dire=dire: e.dma_start(
                out=s.a2aM_in.rearrange("(r f x) c -> r f x c", f=4, x=DSL)
                [dj * 128 // DSL:(dj * 128 + 128) // DSL, dire * 2 + 1, :, :],
                in_=s.b_W2[:, dj * TC:(dj + 1) * TC]),
                reads=rk("W2", dj * TC, (dj + 1) * TC),
                writes=[("a2aM_in", dire, dj, "xm")], dma=True)
        for (a, b) in segs(TC, 512):
            W = b - a
            pxd, rxd = pb.take()
            for q in range(NQ):
                xca, xck = bW2b(s, 0, W)
                t.emit("sync", lambda e, q=q, a=a, W=W, dire=dire, xca=xca: e.dma_start(
                    out=xca, in_=s.s_xc[dire * C2 + q * 128:dire * C2 + (q + 1) * 128,
                                        a:a + W]),
                    reads=[("s_xc", dire, q)], writes=xck, dma=True)
                t.emit("tensor", lambda e, pxd=pxd, q=q, xca=xca, W=W: e.matmul(
                    pxd[0:96, 0:W], s.b_wt[:, q * 96:(q + 1) * 96], xca,
                    start=(q == 0), stop=(q == NQ - 1)),
                    reads=[("wt", 0)] + xck, writes=rxd)
            xdb, xdbk = bW(s, 1024, 1024 + W)
            t.emit("vector", lambda e, pxd=pxd, xdb=xdb, W=W: e.tensor_copy(
                xdb[0:96, :], pxd[0:96, 0:W]), reads=rxd, writes=xdbk)
            bca, bck = bW2b(s, 0, 2 * W)
            t.emit("vector", lambda e, xdb=xdb, bca=bca, W=W: e.tensor_copy(
                bca[0:32, 0:W], xdb[64:96, 0:W]), reads=xdbk, writes=bck)
            t.emit("gpsimd", lambda e, bca=bca, a=a, W=W, dire=dire: e.dma_start(
                out=s.agBC_in[dire * 32:dire * 32 + 32, a:a + W], in_=bca[0:32, 0:W]),
                reads=bck, writes=[("agBC_in", dire)], dma=True)
            dtp, dtpk = bW2b(s, 2048, 2048 + W)
            t.emit("vector", lambda e, xdb=xdb, dtp=dtp, W=W: e.tensor_copy(
                dtp[0:DTR, :], xdb[0:DTR, 0:W]), reads=xdbk, writes=dtpk)
            for mj in range(NPB):
                pdt, rdt = pb.take()
                t.emit("tensor", lambda e, pdt=pdt, mj=mj, dtp=dtp, W=W: e.matmul(
                    pdt[:, 0:W], s.b_wt[:, 4096 + mj * 128:4096 + (mj + 1) * 128][0:DTR, :],
                    dtp[0:DTR, :], start=True, stop=True),
                    reads=[("wt", 4)] + dtpk, writes=rdt)
                za, zk = bW(s, 2048, 2048 + W)
                t.emit("vector", lambda e, pdt=pdt, za=za, mj=mj, W=W: e.tensor_scalar(
                    za, pdt[:, 0:W], pvc(s, "vdtb", mj), None, ALU.add),
                    reads=rdt + [("pv", "vdtb")], writes=zk)
                aa, ak = bW(s, 2560, 2560 + W)
                t.emit("scalar", lambda e, za=za, aa=aa, W=W: e.activation(
                    out=aa, in_=za, func=ACT.Abs), reads=zk, writes=ak)
                t.emit("scalar", lambda e, aa=aa, W=W: e.activation(
                    out=aa, in_=aa, func=ACT.Exp, scale=-1.0), reads=ak, writes=ak)
                t.emit("scalar", lambda e, aa=aa, W=W: e.activation(
                    out=aa, in_=aa, func=ACT.Ln, bias=1.0, scale=1.0),
                    reads=ak, writes=ak)
                t.emit("vector", lambda e, za=za, W=W: e.tensor_scalar(
                    za, za, 0.0, None, ALU.max), reads=zk, writes=zk)
                dta, dtk = bW2b(s, 4096, 4096 + W)
                t.emit("vector", lambda e, dta=dta, za=za, aa=aa: e.tensor_tensor(
                    dta, za, aa, ALU.add), reads=zk + ak, writes=dtk)
                t.emit("gpsimd", lambda e, dta=dta, mj=mj, a=a, W=W, dire=dire: e.dma_start(
                    out=s.a2aM_in.rearrange("(r f x) c -> r f x c", f=4, x=DSL)
                    [mj * 128 // DSL:(mj * 128 + 128) // DSL, dire * 2, :, a:a + W],
                    in_=dta),
                    reads=dtk, writes=[("a2aM_in", dire, mj, "dt")], dma=True)
    from concourse.aot_env import allow_ici_all_to_all
    with allow_ici_all_to_all():
        t.emit("gpsimd", lambda e: e.collective_compute(
            "AllToAll", ALU.bypass, replica_groups=[list(range(NCR))],
            ins=[s.a2aM_in.opt()], outs=[s.a2aM_out.opt()]),
            reads=[("a2aM_in", d, j, q) for d in range(2) for j in range(NPB)
                   for q in ("dt", "xm")],
            writes=[("a2aM_out",)], cc=True)
    t.emit("gpsimd", lambda e: e.collective_compute(
        "AllGather", ALU.bypass, replica_groups=[list(range(NCR))],
        ins=[s.agBC_in.opt()], outs=[s.agBC_out.opt()]),
        reads=[("agBC_in", 0), ("agBC_in", 1)], writes=[("agBC_out",)], cc=True)


def revlast(ap):
    lst = ap.ap
    step, cnt = lst[-1]
    ap.ap = lst[:-1] + [[-step, cnt]]
    ap.offset = ap.offset + (cnt - 1) * step
    return ap


def ph_scan(s):
    t = s.t
    TC = s.TC; NCR = s.NCR; DSL = s.DSL; NST = s.cfg["NST"]
    TF = NCR * TC
    SEG = min(2048, TF)
    NSEG = TF // SEG
    NTI = DSL * NST // 128
    pb = Pb(s)
    t.emit("scalar", lambda e: e.activation(
        out=s.b_st8[:, 0:NTI], in_=s.b_alog[:, 0:NTI], func=ACT.Exp),
        reads=[("alogc",)], writes=[("st8", 0)])
    t.emit("vector", lambda e: e.tensor_scalar(
        s.b_st8[:, 0:NTI], s.b_st8[:, 0:NTI], -1.0, None, ALU.mult),
        reads=[("st8", 0)], writes=[("st8", 0)])
    a2v = s.a2aM_out.rearrange("(r f x) c -> r f x c", f=4, x=DSL)
    s.dscr("s_rev", (4 * DSL + 64, TF), BF16)
    for dire in range(2):
        for (g0, g1) in segs(TF, SEG):
            W = g1 - g0
            r0 = g0 // TC
            nrb = max(W // TC, 1)
            dtb, dtbk = bW2b(s, 0, W)
            t.emit("sync", lambda e, dire=dire, r0=r0, nrb=nrb, dtb=dtb: e.dma_start(
                out=dtb[0:DSL, :], in_=a2v[r0:r0 + nrb, dire * 2, 0:DSL, :]),
                reads=[("a2aM_out",)], writes=dtbk, dma=True)
            xmb, xmbk = bW2b(s, SEG, SEG + W)
            t.emit("sync", lambda e, dire=dire, r0=r0, nrb=nrb, xmb=xmb: e.dma_start(
                out=xmb[0:DSL, :], in_=a2v[r0:r0 + nrb, dire * 2 + 1, 0:DSL, :]),
                reads=[("a2aM_out",)], writes=xmbk, dma=True)
            ub = s.b_wt[:, 0:W]
            ubk = [("wt", 0)]
            t.emit("vector", lambda e, ub=ub, dtb=dtb, xmb=xmb: e.tensor_tensor(
                ub[0:DSL, :], dtb[0:DSL, :], xmb[0:DSL, :], ALU.mult),
                reads=dtbk + xmbk, writes=ubk)
            t.emit("gpsimd", lambda e, ub=ub, dire=dire, g0=g0, W=W: e.dma_start(
                out=s.s_u[dire * DSL:dire * DSL + DSL, g0:g0 + W], in_=ub[0:DSL, :]),
                reads=ubk, writes=[("s_u", dire)], dma=True)
            if dire == 1:
                m0 = TF - g0 - W
                rb = s.b_wt[:, SEG:SEG + W]
                rbk = [("wt", 4)]
                t.emit("vector", lambda e, rb=rb, dtb=dtb: e.tensor_copy(
                    rb[0:DSL, :], revlast(dtb[0:DSL, :])), reads=dtbk, writes=rbk)
                t.emit("gpsimd", lambda e, rb=rb, m0=m0, W=W: e.dma_start(
                    out=s.s_rev[0:DSL, m0:m0 + W], in_=rb[0:DSL, :]),
                    reads=rbk, writes=[("s_rev", 0)], dma=True)
                t.emit("vector", lambda e, rb=rb, ub=ub: e.tensor_copy(
                    rb[0:DSL, :], revlast(ub[0:DSL, :])), reads=ubk, writes=rbk)
                t.emit("gpsimd", lambda e, rb=rb, m0=m0, W=W: e.dma_start(
                    out=s.s_rev[DSL:2 * DSL, m0:m0 + W], in_=rb[0:DSL, :]),
                    reads=rbk, writes=[("s_rev", 1)], dma=True)
            ykeys = [("W2y", ti_, blk) for ti_ in range(NTI)
                     for blk in range(g0 // 512, (g0 + W + 511) // 512)]
            if dire == 0:
                t.emit("vector", lambda e, xmb=xmb, g0=g0, W=W: e.tensor_scalar(
                    s.b_W2[0:DSL, g0:g0 + W], xmb[0:DSL, :], pvc(s, "vD")[0:DSL, :], None,
                    ALU.mult),
                    reads=xmbk + [("pv", "vD")], writes=ykeys)
            else:
                t.emit("vector", lambda e, xmb=xmb, g0=g0, W=W: e.scalar_tensor_tensor(
                    s.b_W2[0:DSL, g0:g0 + W], xmb[0:DSL, :], pvc(s, "vD")[0:DSL, :],
                    s.b_W2[0:DSL, g0:g0 + W], ALU.mult, ALU.add),
                    reads=xmbk + [("pv", "vD")] + ykeys,
                    writes=ykeys)
    bcv = s.agBC_out.rearrange("(r u) c -> r u c", u=64)
    for ui, uoff in [(0, 32), (1, 48)]:
        for (g0, g1) in segs(TF, SEG):
            W = g1 - g0
            r0 = g0 // TC
            nrb = max(W // TC, 1)
            m0 = TF - g0 - W
            ca, ck = bW2b(s, 0, W)
            t.emit("sync", lambda e, ca=ca, uoff=uoff, r0=r0, nrb=nrb: e.dma_start(
                out=ca[0:16, :], in_=bcv[r0:r0 + nrb, uoff:uoff + 16, :]
                .rearrange("r u c -> u r c")),
                reads=[("agBC_out",)], writes=ck, dma=True)
            cb, cbk = bW2b(s, SEG, SEG + W)
            t.emit("vector", lambda e, ca=ca, cb=cb: e.tensor_copy(
                cb[0:16, :], revlast(ca[0:16, :])), reads=ck, writes=cbk)
            t.emit("gpsimd", lambda e, cb=cb, m0=m0, W=W, ui=ui: e.dma_start(
                out=s.s_rev[4 * DSL + ui * 16:4 * DSL + (ui + 1) * 16, m0:m0 + W],
                in_=cb[0:16, :]),
                reads=cbk, writes=[("s_rev", 2 + ui)], dma=True)
    su = s.s_u
    srv = s.s_rev
    for dire in range(2):
        t.emit("vector", lambda e, dire=dire: e.memset(s.b_st8[:, 64:64 + NTI], 0.0),
               writes=[("st8c", ti_) for ti_ in range(NTI)])
        for si in range(NSEG):
            g0 = si * SEG
            W = SEG
            r0 = g0 // TC
            nrb = max(W // TC, 1)
            brep, brepk = bW2b(s, 0, W)
            crep, crepk = bW2b(s, SEG, SEG + W)
            if dire == 0:
                for (dst, dstk, uoff) in [(brep, brepk, 0), (crep, crepk, 16)]:
                    for rb in range(nrb):
                        t.emit("sync", lambda e, dst=dst, uoff=uoff, r0=r0, rb=rb:
                               e.dma_start(out=dst[:, rb * TC:(rb + 1) * TC],
                                           in_=bcv[r0 + rb, uoff:uoff + 16, :]
                                           .unsqueeze(0).broadcast_to([8, 16, TC])),
                               reads=[("agBC_out",)], writes=dstk, dma=True)
            else:
                for (dst, dstk, ui) in [(brep, brepk, 0), (crep, crepk, 1)]:
                    t.emit("sync", lambda e, dst=dst, ui=ui, g0=g0, W=W:
                           e.dma_start(out=dst, in_=srv[4 * DSL + ui * 16:
                                                        4 * DSL + (ui + 1) * 16,
                                                        g0:g0 + W]
                                       .unsqueeze(0).broadcast_to([8, 16, W])),
                           reads=[("s_rev", 2 + ui)], writes=dstk, dma=True)
            for ti in range(NTI):
                d0 = ti * 8
                pp = ti % 2
                dtr2 = s.b_wt[:, pp * 4096:pp * 4096 + W]
                dtr2k = [("wt", 4 * pp), ("wt", 4 * pp + 1)]
                urep = s.b_wt[:, pp * 4096 + 2048:pp * 4096 + 2048 + W]
                urepk = [("wt", 4 * pp + 2), ("wt", 4 * pp + 3)]
                if dire == 0:
                    for rb in range(nrb):
                        t.emit("sync", lambda e, r0=r0, rb=rb, d0=d0, dtr2=dtr2:
                               e.dma_start(out=dtr2[:, rb * TC:(rb + 1) * TC],
                                           in_=a2v[r0 + rb, 0, d0:d0 + 8, :]
                                           .unsqueeze(1).broadcast_to([8, 16, TC])),
                               reads=[("a2aM_out",)], writes=dtr2k, dma=True)
                    t.emit("sync", lambda e, g0=g0, d0=d0, urep=urep, W=W:
                           e.dma_start(out=urep, in_=su[d0:d0 + 8, g0:g0 + W]
                                       .unsqueeze(1).broadcast_to([8, 16, W])),
                           reads=[("s_u", 0)], writes=urepk, dma=True)
                else:
                    t.emit("sync", lambda e, g0=g0, d0=d0, dtr2=dtr2, W=W:
                           e.dma_start(out=dtr2, in_=srv[d0:d0 + 8, g0:g0 + W]
                                       .unsqueeze(1).broadcast_to([8, 16, W])),
                           reads=[("s_rev", 0)], writes=dtr2k, dma=True)
                    t.emit("sync", lambda e, g0=g0, d0=d0, urep=urep, W=W:
                           e.dma_start(out=urep, in_=srv[DSL + d0:DSL + d0 + 8, g0:g0 + W]
                                       .unsqueeze(1).broadcast_to([8, 16, W])),
                           reads=[("s_rev", 1)], writes=urepk, dma=True)
                dca, dck = bW(s, pp * 4096, pp * 4096 + W)
                t.emit("scalar", lambda e, dca=dca, dtr2=dtr2, ti=ti: e.activation(
                    out=dca, in_=dtr2, func=ACT.Exp, scale=s.b_st8[:, ti:ti + 1]),
                    reads=dtr2k + [("st8", 0)], writes=dck)
                t.emit("vector", lambda e, urep=urep, brep=brep: e.tensor_tensor(
                    urep, urep, brep, ALU.mult), reads=urepk + brepk, writes=urepk)
                ha, hk = bW(s, pp * 4096 + 2048, pp * 4096 + 2048 + W)
                t.emit("vector", lambda e, ha=ha, dca=dca, urep=urep, ti=ti:
                       e.tensor_tensor_scan(ha, dca, urep,
                                            s.b_st8[:, 64 + ti:64 + ti + 1],
                                            ALU.mult, ALU.add),
                       reads=dck + urepk + [("st8c", ti)], writes=hk)
                t.emit("vector", lambda e, ha=ha, ti=ti, W=W: e.tensor_copy(
                    s.b_st8[:, 64 + ti:64 + ti + 1], ha[:, W - 1:W]),
                    reads=hk, writes=[("st8c", ti)])
                t.emit("vector", lambda e, dtr2=dtr2, ha=ha, crep=crep: e.tensor_tensor(
                    dtr2, ha, crep, ALU.mult), reads=hk + crepk, writes=dtr2k)
                for (w0, w1) in segs(W, 512):
                    pr, rr = pb.take()
                    t.emit("tensor", lambda e, pr=pr, dtr2=dtr2, w0=w0, w1=w1: e.matmul(
                        pr[0:8, 0:w1 - w0], s.b_ps64[:, 64:72], dtr2[:, w0:w1],
                        start=True, stop=True),
                        reads=[("ps64c",)] + dtr2k, writes=rr)
                    wb = (w0 // 512) % 4
                    ysm = s.b_W2b[:, 4096 + wb * 512:4096 + wb * 512 + 512]
                    ysmk = [("ysm", wb)]
                    t.emit("vector", lambda e, pr=pr, ysm=ysm, w0=w0, w1=w1: e.tensor_copy(
                        ysm[0:8, 0:w1 - w0], pr[0:8, 0:w1 - w0]), reads=rr, writes=ysmk)
                    if dire == 0:
                        t.emit("gpsimd", lambda e, ysm=ysm, d0=d0, g0=g0, w0=w0, w1=w1:
                               e.dma_start(out=s.b_W2[d0:d0 + 8, g0 + w0:g0 + w1],
                                           in_=ysm[0:8, 0:w1 - w0], accum_op=ALU.add),
                               reads=ysmk + [("W2y", ti, (g0 + w0) // 512)],
                               writes=[("W2y", ti, (g0 + w0) // 512)], dma=True)
                    else:
                        t.emit("gpsimd", lambda e, ysm=ysm, d0=d0, g0=g0, w0=w0, w1=w1:
                               e.dma_start(out=s.s_ybr[d0:d0 + 8, g0 + w0:g0 + w1],
                                           in_=ysm[0:8, 0:w1 - w0]),
                               reads=ysmk, writes=[("s_ybr", d0, g0 + w0)], dma=True)
    for (g0, g1) in segs(TF, SEG):
        W = g1 - g0
        m0 = TF - g0 - W
        ca, ck = bW2b(s, 0, W)
        t.emit("sync", lambda e, ca=ca, g0=g0, W=W: e.dma_start(
            out=ca[0:DSL, :], in_=s.s_ybr[0:DSL, g0:g0 + W]),
            reads=[("s_ybr", d0_, w0_) for d0_ in range(0, DSL, 8)
                   for w0_ in range(g0, g0 + W, 512)], writes=ck, dma=True)
        ykeys2 = [("W2y", ti_, blk) for ti_ in range(NTI)
                  for blk in range(m0 // 512, (m0 + W + 511) // 512)]
        t.emit("vector", lambda e, ca=ca, m0=m0, W=W: e.tensor_tensor(
            s.b_W2[0:DSL, m0:m0 + W], s.b_W2[0:DSL, m0:m0 + W],
            revlast(ca[0:DSL, :]), ALU.add),
            reads=ck + ykeys2, writes=ykeys2)
    y2 = s.a2aY_in.rearrange("(r x) c -> r x c", x=DSL)
    for r in range(NCR):
        t.emit("gpsimd", lambda e, r=r: e.dma_start(
            out=y2[r, 0:DSL, :], in_=s.b_W2[0:DSL, r * TC:(r + 1) * TC]),
            reads=[("W2y", ti_, blk) for ti_ in range(NTI)
                   for blk in range(r * TC // 512, ((r + 1) * TC + 511) // 512)],
            writes=[("a2aY_in",)], dma=True)
    from concourse.aot_env import allow_ici_all_to_all
    with allow_ici_all_to_all():
        t.emit("gpsimd", lambda e: e.collective_compute(
            "AllToAll", ALU.bypass, replica_groups=[list(range(NCR))],
            ins=[s.a2aY_in.opt()], outs=[s.a2aY_out.opt()]),
            reads=[("a2aY_in",)], writes=[("a2aY_out",)], cc=True)


def ph_ffn1_final(s):
    t = s.t
    NPB = s.NPB; TC = s.TC; XW = s.XW; DQ = s.cfg["DQ"]; D = s.D
    NQB = DQ // 128
    pb = Pb(s)
    for og in range(NQB):
        for wi, wap, wofs in [(0, s.qg, 0), (1, s.qu, 4096)]:
            t.emit("gpsimd", lambda e, wap=wap, wofs=wofs, og=og: e.dma_start(
                out=s.b_wt[:, wofs:wofs + NPB * 128],
                in_=wap[:, og * 128:(og + 1) * 128]
                .rearrange("(ks p) m -> p ks m", p=128)),
                reads=[("qg",) if wofs == 0 else ("qu",)],
                writes=[("wt", 0 if wofs == 0 else 4)], dma=True)
        for (a, b) in segs(TC, 512):
            W = b - a
            pg, rg = pb.take()
            pu, ru = pb.take()
            for ks in range(NPB):
                t.emit("tensor", lambda e, pg=pg, ks=ks, a=a, W=W: e.matmul(
                    pg[:, 0:W], s.b_wt[:, ks * 128:ks * 128 + 128],
                    s.b_xn0[:, ks * XW + 1 + a:ks * XW + 1 + a + W],
                    start=(ks == 0), stop=(ks == NPB - 1)),
                    reads=[("wt", 0)] + rk("xn0", ks * XW, (ks + 1) * XW), writes=rg)
            for ks in range(NPB):
                t.emit("tensor", lambda e, pu=pu, ks=ks, a=a, W=W: e.matmul(
                    pu[:, 0:W], s.b_wt[:, 4096 + ks * 128:4096 + ks * 128 + 128],
                    s.b_xn0[:, ks * XW + 1 + a:ks * XW + 1 + a + W],
                    start=(ks == 0), stop=(ks == NPB - 1)),
                    reads=[("wt", 4)] + rk("xn0", ks * XW, (ks + 1) * XW), writes=ru)
            sg, sgk = bW(s, 0, W)
            t.emit("scalar", lambda e, sg=sg, pg=pg, W=W: e.activation(
                out=sg, in_=pg[:, 0:W], func=ACT.Silu, scale=1.0 / 127.0),
                reads=rg, writes=sgk)
            hq, hqk = bW2b(s, 0, W)
            t.emit("vector", lambda e, hq=hq, sg=sg, pu=pu, W=W: e.tensor_tensor(
                hq, sg, pu[:, 0:W], ALU.mult), reads=sgk + ru, writes=hqk)
            t.emit("gpsimd", lambda e, hq=hq, og=og, a=a, W=W: e.dma_start(
                out=s.s_hq[og * 128:(og + 1) * 128, a:a + W], in_=hq),
                reads=hqk, writes=[("s_hq", og)], dma=True)
    for (a, b) in segs(TC, 512):
        W = b - a
        pouts = [(s.psum[i], [("psum", i)]) for i in range(NPB)]
        for kg in range(NQB):
            t.emit("gpsimd", lambda e, kg=kg: e.dma_start(
                out=s.b_wt[:, (kg % 4) * 1024:(kg % 4) * 1024 + D],
                in_=s.qd[kg * 128:(kg + 1) * 128, :]),
                reads=[("qd",)], writes=[("wt", kg % 4)], dma=True)
            hq, hqk = bW2b(s, (kg % 4) * 512, (kg % 4) * 512 + W)
            t.emit("sync", lambda e, hq=hq, kg=kg, a=a, W=W: e.dma_start(
                out=hq, in_=s.s_hq[kg * 128:(kg + 1) * 128, a:a + W]),
                reads=[("s_hq", kg)], writes=hqk, dma=True)
            for mj in range(NPB):
                p, rp = pouts[mj]
                t.emit("tensor", lambda e, p=p, kg=kg, mj=mj, hq=hq, W=W: e.matmul(
                    p[:, 0:W], s.b_wt[:, (kg % 4) * 1024 + mj * 128:
                                      (kg % 4) * 1024 + mj * 128 + 128],
                    hq, start=(kg == 0), stop=(kg == NQB - 1)),
                    reads=[("wt", kg % 4)] + hqk, writes=rp)
        for mj in range(NPB):
            p, rp = pouts[mj]
            at, atk = bW2b(s, 4096, 4096 + W)
            t.emit("sync", lambda e, at=at, mj=mj, a=a, W=W: e.dma_start(
                out=at, in_=s.a2aY_out[mj * 128:(mj + 1) * 128, a:a + W]),
                reads=[("a2aY_out",)], writes=atk, dma=True)
            oa, ok_ = bW(s, 0, W)
            t.emit("vector", lambda e, oa=oa, p=p, at=at, W=W: e.scalar_tensor_tensor(
                oa, p[:, 0:W], 1.0 / (127.0 * 127.0), at, ALU.mult, ALU.add),
                reads=rp + atk, writes=ok_)
            t.emit("vector", lambda e, oa=oa, mj=mj, a=a, W=W: e.tensor_tensor(
                oa, oa, s.b_x1[:, mj * XW + 1 + a:mj * XW + 1 + a + W], ALU.add),
                reads=ok_ + rk("x1", mj * XW + 1 + a, mj * XW + 1 + b), writes=ok_)
            t.emit("gpsimd", lambda e, oa=oa, mj=mj, a=a, W=W: e.dma_start(
                out=s.out[mj * 128:(mj + 1) * 128, a:a + W], in_=oa),
                reads=ok_, writes=[("outw", mj)], dma=True)
    t.eng["gpsimd"].wait_ge(t.sem[("gpsimd", "d")], t.cnt[("gpsimd", "d")])
    t.eng["sync"].wait_ge(t.sem[("sync", "d")], t.cnt[("sync", "d")])


from concourse.bass_utils import run_bass_kernel_spmd


def kernel(**inputs):
    cfg = full_cfg()
    NCR = cfg["NCORES"]
    TC = cfg["TC"]; D = cfg["D"]
    nc = build(cfg, upto="all")
    in_maps = [host_layout(cfg, inputs, c) for c in range(NCR)]
    res = run_bass_kernel_spmd(nc, in_maps, core_ids=list(range(NCR)))
    out = np.zeros((1, NCR * TC, D), np.float32)
    for c in range(NCR):
        out[0, c * TC:(c + 1) * TC] = np.asarray(res.results[c]["out"],
                                                 np.float32).T
    return out


# version 12 (final)
# speedup vs baseline: 1.0012x; 1.0012x over previous
import numpy as np
from contextlib import ExitStack

import concourse.bass as bass
import concourse.mybir as mybir

F32 = mybir.dt.float32
BF16 = mybir.dt.bfloat16
I32 = mybir.dt.int32
ALU = mybir.AluOpType
ACT = mybir.ActivationFunctionType
AX = mybir.AxisListType





ENGINES = ("sync", "tensor", "vector", "scalar", "gpsimd")


class Trk:
    def __init__(self, nc, ctx):
        self.nc = nc
        self.eng = {e: getattr(nc, e) for e in ENGINES}
        self.sem = {}
        self.cnt = {}
        keys = ([(e, "c") for e in ENGINES] + [(e, "d") for e in ENGINES]
                + [("gpsimd", "x")])
        for k in keys:
            self.sem[k] = ctx.enter_context(nc.semaphore(f"trk_{k[0]}_{k[1]}"))
            self.cnt[k] = 0
        self.last_w = {}
        self.readers = {}
        self.seen = {e: {k: 0 for k in keys} for e in ENGINES}
        self.flushed = {}
        self.nwaits = 0
        self.nrelays = 0
        self.nops = 0

    def _wait(self, e, pk, val):
        if val <= 0 or self.seen[e][pk] >= val:
            return
        self.eng[e].wait_ge(self.sem[pk], val)
        self.seen[e][pk] = val
        self.nwaits += 1

    def _relay_k(self, pk, v):
        fl = self.flushed.setdefault(pk, (0, 0))
        if fl[0] >= v:
            return fl[1]
        pe = pk[0]
        total = self.cnt[pk]
        self.eng[pe].wait_ge(self.sem[pk], total)
        self.seen[pe][pk] = total
        self.cnt[(pe, "c")] += 1
        cv = self.cnt[(pe, "c")]
        self.eng[pe].sem_inc(self.sem[(pe, "c")], 1)
        self.flushed[pk] = (total, cv)
        self.nrelays += 1
        return cv

    def emit(self, e, fn, reads=(), writes=(), dma=False, cc=False, inc_extra=None):
        need = {}

        def add(pk, v):
            need[pk] = max(need.get(pk, 0), v)

        for r in reads:
            lw = self.last_w.get(r)
            if lw:
                add(*lw)
        for w in writes:
            lw = self.last_w.get(w)
            if lw:
                add(*lw)
            for pk, v in self.readers.get(w, {}).items():
                add(pk, v)
        for pk, v in sorted(need.items(), key=lambda kv: str(kv[0])):
            pe, kind = pk
            if kind in ("d", "x"):
                if pe == e:
                    self._wait(e, pk, self.cnt[pk])
                else:
                    cv = self._relay_k(pk, v)
                    self._wait(e, (pe, "c"), cv)
            else:
                self._wait(e, pk, v)
        inst = fn(self.eng[e])
        k = (e, "d") if dma else (("gpsimd", "x") if cc else (e, "c"))
        step = 16 if dma else 1
        self.cnt[k] += step
        inst.then_inc(self.sem[k], step)
        if inc_extra is not None:
            inst.then_inc(*inc_extra)
        myv = self.cnt[k]
        for w in writes:
            self.last_w[w] = (k, myv)
            self.readers[w] = {}
        for r in reads:
            self.readers.setdefault(r, {})[k] = myv
        self.nops += 1
        return inst

    def snapshot(self):
        return dict(self.cnt)

    def wait_snapshot(self, e, snap):
        for pk, v in snap.items():
            pe, kind = pk
            if kind in ("d", "x") and pe != e:
                cv = self._relay_k(pk, v)
                self._wait(e, (pe, "c"), cv)
            else:
                self._wait(e, pk, v)


def full_cfg():
    return dict(TC=2048, D=1024, H=16, HD=64, C=64, DFF=4096, C2=2048,
                DQ=3072, NST=16, DTR=64, DW=64, DA=64, DG=128, NCORES=8)


def tiny_cfg():
    return dict(TC=256, D=256, H=4, HD=64, C=64, DFF=512, C2=512,
                DQ=384, NST=16, DTR=64, DW=64, DA=64, DG=128, NCORES=8)


def tiny8_cfg():
    return dict(TC=128, D=1024, H=16, HD=64, C=64, DFF=512, C2=2048,
                DQ=384, NST=16, DTR=64, DW=64, DA=64, DG=128, NCORES=8)


class St:
    pass


def segs(total, step, start=0):
    out = []
    a = start
    while a < total:
        out.append((a, min(a + step, total)))
        a += step
    return out


def build(cfg, taps=(), upto="all"):
    TC = cfg["TC"]; D = cfg["D"]; NPB = D // 128
    DFF = cfg["DFF"]; NFB = DFF // 128
    XW = TC + 2
    HW = ((XW + 7) // 8) * 8
    nc = bass.Bass("TRN2", target_bir_lowering=False, debug=False)

    s = St()
    s.cfg = cfg; s.nc = nc; s.taps = set(taps); s.upto = upto
    s.NPB = NPB; s.NFB = NFB; s.XW = XW; s.HW = HW
    s.TC = TC; s.D = D; s.DFF = DFF

    def din(name, shape, dt=F32):
        ap = nc.dram_tensor(name, list(shape), dt, kind="ExternalInput").ap()
        setattr(s, name, ap)
        return ap
    s.din = din

    din("xT", (D, XW))
    din("vmix", (128, 6 * NPB))
    din("vw0", (128, NPB)); din("va0", (128, NPB))
    din("vkk", (128, NPB)); din("vka", (128, NPB))
    din("vlnw", (128, NPB)); din("vlnb", (128, NPB)); din("vrk", (128, NPB))
    din("f0w1t", (D, DFF)); din("f0w2t", (DFF, D))
    din("f0cw", (128, 3 * NFB))
    din("f0cb", (128, NFB))
    din("ident", (128, 128))
    din("wr", (D, D)); din("wk", (D, D)); din("wv", (D, D)); din("wo", (D, D))
    din("lw1", (D, cfg["DW"])); din("lw2", (cfg["DW"], D))
    din("la1", (D, cfg["DA"])); din("la2", (cfg["DA"], D))
    din("lg1", (D, cfg["DG"])); din("lg2", (cfg["DG"], D))
    din("hsum2", (128, 2))
    din("hexp2", (2, 128))
    din("rstpat", (128, TC))
    din("pfxm64", (128, cfg["NCORES"]))
    din("mskLR", (128, 2 * cfg["NCORES"]))
    din("psum64", (128, 72))
    DSL = D // cfg["NCORES"]
    s.DSL = DSL
    din("vD", (128, 1))
    din("alog_lanes", (128, max(DSL * 16 // 128, 1)))
    din("cw1", (128, 4 * (cfg["C2"] // 128)))
    din("cb1", (128, cfg["C2"] // 128))
    din("xprojt", (cfg["C2"], 96))
    din("dtprojt", (cfg["DTR"], D))
    din("vdtb", (128, NPB))
    din("qg", (D, cfg["DQ"]), I32)
    din("qu", (D, cfg["DQ"]), I32)
    din("qd", (cfg["DQ"], D), I32)
    C = cfg["C"]
    din("m_stril", (128, C))
    din("m_tril", (128, C))
    s.out = nc.dram_tensor("out", [D, TC], F32, kind="ExternalOutput").ap()

    s.tap_out = {}

    def dtap(name, shape, dt=F32):
        if name in s.taps and name not in s.tap_out:
            s.tap_out[name] = nc.dram_tensor("tap_" + name, list(shape), dt,
                                             kind="ExternalOutput").ap()
        return s.tap_out.get(name)
    s.dtap = dtap

    def dscr(name, shape, dt):
        ap = nc.dram_tensor(name, list(shape), dt).ap()
        setattr(s, name, ap)
        return ap
    s.dscr = dscr

    H = cfg["H"]
    NCH = TC // cfg["C"]
    s.H = H; s.NCH = NCH; s.C = cfg["C"]
    NCR = cfg["NCORES"]
    s.NCR = NCR
    dscr("s_r", (D, TC), BF16); dscr("s_kmod", (D, TC), BF16)
    dscr("s_v", (D, TC), BF16); dscr("s_kap", (D, TC), BF16)
    dscr("s_ab", (D, TC), BF16); dscr("s_sz", (D, TC), BF16)
    dscr("s_g", (D, TC), BF16); dscr("s_lnp", (D, TC), F32)
    dscr("s_srk", (2 * NPB, TC), F32)
    dscr("s_yg", (D, TC), BF16)
    C2_ = cfg["C2"]
    dscr("s_xn1", (D, TC + 6), BF16)
    dscr("s_xc", (2 * C2_, TC), BF16)
    dscr("agH_in", (D, 6), F32)
    dscr("agH_out", (NCR * D, 6), F32)
    DSL_ = D // NCR
    dscr("a2aM_in", (NCR * 4 * DSL_, TC), BF16)
    dscr("a2aM_out", (NCR * 4 * DSL_, TC), BF16)
    dscr("agBC_in", (64, TC), BF16)
    dscr("agBC_out", (NCR * 64, TC), BF16)
    dscr("s_u", (2 * DSL_, NCR * TC), BF16)
    dscr("a2aY_in", (NCR * DSL_, TC), BF16)
    dscr("a2aY_out", (NCR * DSL_, TC), BF16)
    dscr("s_hq", (cfg["DQ"], TC), BF16)
    dscr("s_ybr", (DSL_, NCR * TC), BF16)
    C_ = cfg["C"]
    dscr("s_pack", (NCH * NPB * 128, 384), BF16)
    dscr("agE_in", (NPB * 64, 128), F32)
    dscr("agE_out", (NCR * NPB * 64, 128), F32)
    dscr("agD_in", (NPB * 128, 64), F32)
    dscr("agD_out", (NCR * NPB * 128, 64), F32)
    dscr("s_sh", (DFF, TC), BF16)
    dscr("s_tern2", (DFF, D), BF16)

    with ExitStack() as ctx:
        def sbuf(name, shape, dt):
            t_ = ctx.enter_context(nc.sbuf_tensor(name, list(shape), dt))
            setattr(s, "b_" + name, t_)
            return t_
        s.sbuf = sbuf

        KK0 = NPB * 512
        SQ0 = 2 * NPB * 512
        s.KK0, s.SQ0 = KK0, SQ0
        s.LERP0 = SQ0 + 512
        s.EX0 = SQ0 + 1024
        SEGW = min(2048, cfg['NCORES'] * TC)
        FG0 = min(3, NFB)
        s.FG0 = FG0
        WCOLS = max(FG0 * HW + 3072, SQ0 + 1536, 2048 + TC // 2 + 64,
                    2 * SEGW + 576, 4096, NPB * 512 + 4096)
        sbuf("x1", (128, NPB * XW), BF16)
        sbuf("xn0", (128, NPB * XW), BF16)
        sbuf("W", (128, WCOLS), F32)
        sbuf("W2", (128, max(NPB * XW, NFB * 512)), BF16)
        sbuf("W2b", (128, 6144), BF16)
        sbuf("wt", (128, 8192), BF16)
        s.b_wlr = s.b_wt
        sbuf("two", (2, 2048), F32)
        sbuf("bd", (128, 20 * 128), BF16)
        sbuf("pfx", (128, cfg["NCORES"]), F32)
        sbuf("kt", (128, 16 * 64), BF16)
        sbuf("st8", (128, 12 * 64), F32)
        sbuf("ec", (128, 10 * 64), BF16)
        sbuf("pv", (128, 384), F32)
        sbuf("rows", (1, 5 * 512), F32)
        sbuf("rowsq", (128, XW), BF16)
        sbuf("tok", (128, 128), F32)
        sbuf("cstI", (128, 128), F32)
        sbuf("cstIb", (128, 128), BF16)
        sbuf("onesr", (1, 128), F32)
        sbuf("onesrb", (1, 128), BF16)
        sbuf("rq1", (1, 512), BF16)
        sbuf("onesc", (128, 1), F32)
        sbuf("onescb", (128, 1), BF16)
        sbuf("iq", (128, 512), I32)
        sbuf("hs2", (128, 2), F32)
        sbuf("hx2", (2, 128), F32)
        sbuf("rst", (128, TC), BF16)
        sbuf("mstr", (128, cfg["C"]), F32)
        sbuf("mtr", (128, cfg["C"]), F32)
        sbuf("msk", (128, 2 * cfg["NCORES"]), F32)
        sbuf("ps64", (128, 72), BF16)
        sbuf("alog", (128, 16), F32)
        s.psum = [ctx.enter_context(nc.psum_tensor(f"ps{i}", [128, 512], F32))
                  for i in range(8)]
        s.psumb = [s.psum[6].bitcast(BF16), s.psum[7].bitcast(BF16)]
        s.pbt_i = 0

        t = Trk(nc, ctx)
        s.t = t
        s.block = ctx.enter_context(nc.Block())

        ph_loads(s)
        ph_ln0(s)
        if upto not in ("ln0", "ffn"):
            ph_rwkv_proj(s)
            ph_rwkv_chunks(s)
            ph_rwkv_post(s)
        if upto not in ("ln0", "rwkv"):
            ph_ffn0(s)
        if upto == "all":
            ph_ln1(s)
            ph_mamba_prep(s)
            ph_scan(s)
            ph_ffn1_final(s)
        else:
            ph_finish(s)

    return nc


def rk(buf, c0, c1):
    return [(buf, i) for i in range(c0 // 512, (c1 + 511) // 512)]


class Pb:
    def __init__(self, s, banks=range(8)):
        self.s = s
        self.banks = list(banks)
        self.i = 0

    def take(self):
        b = self.banks[self.i % len(self.banks)]
        self.i += 1
        return self.s.psum[b], [("psum", b)]


def bW(s, c0, c1):
    return s.b_W[:, c0:c1], rk("W", c0, c1)


def bW2(s, c0, c1):
    return s.b_W2[:, c0:c1], rk("W2", c0, c1)


def bW2b(s, c0, c1):
    return s.b_W2b[:, c0:c1], rk("W2b", c0, c1)


def ph_loads(s):
    t = s.t
    NPB = s.NPB; XW = s.XW
    for j in range(NPB):
        t.emit("gpsimd", lambda e, j=j: e.dma_start(
            out=s.b_x1[:, j * XW:(j + 1) * XW],
            in_=s.xT[j * 128:(j + 1) * 128, :]),
            reads=[("xT",)], writes=rk("x1", j * XW, (j + 1) * XW), dma=True)
    col = 0
    s.pvcol = {}
    for name in ["vmix", "vw0", "va0", "vkk", "vka", "vlnw", "vlnb", "vrk",
                 "f0cw", "f0cb", "vdtb", "cw1", "cb1", "vD"]:
        ap = getattr(s, name)
        w = ap.shape[1]
        t.emit("sync", lambda e, ap=ap, col=col, w=w: e.dma_start(
            out=s.b_pv[:, col:col + w], in_=ap), reads=[(name,)],
            writes=[("pv", name)], dma=True)
        s.pvcol[name] = col
        col += w
    s.pv_free = col
    t.emit("sync", lambda e: e.dma_start(out=s.b_cstI[:], in_=s.ident),
           reads=[("ident",)], writes=[("cstI",)], dma=True)
    t.emit("gpsimd", lambda e: e.dma_start(out=s.b_cstIb[:], in_=s.ident),
           reads=[("ident",)], writes=[("cstIb",)], dma=True)
    t.emit("sync", lambda e: e.dma_start(out=s.b_hs2[:], in_=s.hsum2),
           reads=[("hsum2",)], writes=[("hsum2c",)], dma=True)
    t.emit("sync", lambda e: e.dma_start(out=s.b_hx2[:], in_=s.hexp2),
           reads=[("hexp2",)], writes=[("hexp2c",)], dma=True)
    t.emit("gpsimd", lambda e: e.dma_start(out=s.b_rst[:], in_=s.rstpat),
           reads=[("rstpat",)], writes=[("rst",)], dma=True)
    t.emit("sync", lambda e: e.dma_start(out=s.b_mstr[:], in_=s.m_stril),
           reads=[("m_stril",)], writes=[("mstr",)], dma=True)
    t.emit("sync", lambda e: e.dma_start(out=s.b_mtr[:], in_=s.m_tril),
           reads=[("m_tril",)], writes=[("mtr",)], dma=True)
    s.hs2_ap = lambda: s.b_hs2[:, 0:2]
    s.hexp2_ap = lambda: s.b_hx2[0:2, :]
    t.emit("vector", lambda e: e.memset(s.b_tok[:], 0.0), writes=[("tok", c) for c in range(64)])
    t.emit("vector", lambda e: e.memset(s.b_onesr[:], 1.0), writes=[("onesr",)])
    t.emit("vector", lambda e: e.memset(s.b_onesrb[:], 1.0), writes=[("onesrb",)])
    t.emit("vector", lambda e: e.memset(s.b_onesc[:], 1.0), writes=[("onesc",)])
    t.emit("vector", lambda e: e.memset(s.b_onescb[:], 1.0), writes=[("onescb",)])
    if s.upto == "all":
        t.emit("sync", lambda e: e.dma_start(out=s.b_msk[:], in_=s.mskLR),
               reads=[("mskLR",)], writes=[("mskc",)], dma=True)
        t.emit("gpsimd", lambda e: e.dma_start(out=s.b_ps64[:], in_=s.psum64),
               reads=[("psum64",)], writes=[("ps64c",)], dma=True)
        t.emit("sync", lambda e: e.dma_start(out=s.b_alog[:, 0:s.alog_lanes.shape[1]],
                                             in_=s.alog_lanes),
               reads=[("alog_lanes",)], writes=[("alogc",)], dma=True)


def pvc(s, name, j=0):
    c = s.pvcol[name] + j
    return s.b_pv[:, c:c + 1]


def pv_alloc(s, name, w=1):
    s.pvcol[name] = s.pv_free
    s.pv_free += w
    assert s.pv_free <= 384
    return s.pvcol[name]



def len_W(s):
    return s.b_W.shape[1]


def bcast_sbuf(s, pb, row_ap, row_reg, width, c_dst, bf=False, obase=0):
    t = s.t
    p, r = pb.take()
    if bf:
        t.emit("vector", lambda e, row_ap=row_ap, width=width: e.tensor_copy(
            s.b_rq1[0:1, 0:width], row_ap), reads=list(row_reg), writes=[("rq1",)])
        row_ap = s.b_rq1[0:1, 0:width]
        row_reg = [("rq1",)]
        ones, okey = s.b_onesrb[0:1, 0:128], ("onesrb",)
    else:
        ones, okey = s.b_onesr[0:1, 0:128], ("onesr",)
    t.emit("tensor", lambda e, row_ap=row_ap: e.matmul(p[:, 0:width], ones,
                                        row_ap, start=True, stop=True),
           reads=[okey] + list(row_reg), writes=r)
    dst, keys = bW(s, c_dst, c_dst + width)
    t.emit("scalar", lambda e, dst=dst, p=p, width=width: e.copy(dst, p[:, 0:width]),
           reads=r, writes=keys)
    return dst, keys


def bcast_row(s, pb, row_ap, row_reg, width, bf=False, obase=0):
    t = s.t
    p, r = pb.take()
    if bf:
        t.emit("vector", lambda e, row_ap=row_ap, width=width: e.tensor_copy(
            s.b_rq1[0:1, 0:width], row_ap), reads=list(row_reg), writes=[("rq1",)])
        row_ap = s.b_rq1[0:1, 0:width]
        row_reg = [("rq1",)]
        ones, okey = s.b_onesrb[0:1, 0:128], ("onesrb",)
    else:
        ones, okey = s.b_onesr[0:1, 0:128], ("onesr",)
    t.emit("tensor", lambda e, row_ap=row_ap: e.matmul(p[:, 0:width], ones,
                                        row_ap, start=True, stop=True),
           reads=[okey] + list(row_reg), writes=r)
    return p, r


def ph_ln0(s):
    ln_channels(s, srcb="x1", dstb="xn0", eps=1e-5)
    tap_blocks(s, "xn0", s.b_xn0, "xn0", s.NPB, s.XW, BF16)


def ln_channels(s, srcb, dstb, eps):
    t = s.t
    NPB = s.NPB; XW = s.XW; D = s.D
    src = getattr(s, "b_" + srcb)
    dst = getattr(s, "b_" + dstb)
    pb = Pb(s)
    for (t0, t1) in segs(XW, 512):
        W = t1 - t0
        p1, r1 = pb.take()
        p2, r2 = pb.take()
        for j in range(NPB):
            t.emit("scalar", lambda e, j=j: e.activation(
                out=s.b_W[:, j * 512:j * 512 + W],
                in_=src[:, j * XW + t0:j * XW + t1], func=ACT.Square),
                reads=rk(srcb, j * XW + t0, j * XW + t1),
                writes=rk("W", j * 512, j * 512 + W))
        for j in range(NPB):
            t.emit("tensor", lambda e, j=j: e.matmul(
                p1[0:1, 0:W], s.b_onescb[:, 0:1],
                src[:, j * XW + t0:j * XW + t1],
                start=(j == 0), stop=(j == NPB - 1)),
                reads=[("onescb",)] + rk(srcb, j * XW + t0, j * XW + t1), writes=r1)
        for j in range(NPB):
            t.emit("tensor", lambda e, j=j: e.matmul(
                p2[0:1, 0:W], s.b_onesc[:, 0:1],
                s.b_W[:, j * 512:j * 512 + W],
                start=(j == 0), stop=(j == NPB - 1)),
                reads=[("onesc",)] + rk("W", j * 512, j * 512 + W), writes=r2)
        ln_rows(s, p1, r1, p2, r2, t0, W, D, eps)
        p3, r3 = bcast_row(s, pb, s.b_rows[0:1, 512:512 + W], [("rows", 3)], W)
        p4, r4 = bcast_row(s, pb, s.b_rows[0:1, 1024:1024 + W], [("rows", 4)], W)
        for j in range(NPB):
            t.emit("vector", lambda e, j=j: e.tensor_tensor(
                s.b_W[:, j * 512:j * 512 + W],
                src[:, j * XW + t0:j * XW + t1], p3[:, 0:W], ALU.mult),
                reads=rk(srcb, j * XW + t0, j * XW + t1) + r3,
                writes=rk("W", j * 512, j * 512 + W))
            t.emit("vector", lambda e, j=j: e.tensor_tensor(
                dst[:, j * XW + t0:j * XW + t1],
                s.b_W[:, j * 512:j * 512 + W], p4[:, 0:W], ALU.add),
                reads=rk("W", j * 512, j * 512 + W) + r4,
                writes=rk(dstb, j * XW + t0, j * XW + t1))


def ln_rows(s, p_sum, r_sum, p_sum2, r_sum2, t0, W, N, eps):
    t = s.t
    R = s.b_rows
    t1 = t0 + W
    LNS = {0: 0, 3: 1, 4: 2, 5: 3, 6: 4}

    def rr(row):
        return R[0:1, LNS[row] * 512:LNS[row] * 512 + W]

    def er(row):
        return [("rows", row)]

    t.emit("vector", lambda e: e.tensor_scalar(
        rr(0), p_sum[0:1, 0:W], 1.0 / N, None, ALU.mult),
        reads=r_sum, writes=er(0))
    t.emit("vector", lambda e: e.tensor_scalar(
        rr(5), p_sum2[0:1, 0:W], 1.0 / N, None, ALU.mult),
        reads=r_sum2, writes=er(5))
    t.emit("vector", lambda e: e.tensor_tensor(
        rr(6), rr(0), rr(0), ALU.mult),
        reads=er(0), writes=er(6))
    t.emit("vector", lambda e: e.tensor_tensor(
        rr(5), rr(5), rr(6), ALU.subtract),
        reads=er(5) + er(6), writes=er(5))
    t.emit("vector", lambda e: e.tensor_scalar(
        rr(5), rr(5), eps, None, ALU.add),
        reads=er(5), writes=er(5))
    t.emit("vector", lambda e: e.reciprocal(rr(6), rr(5)),
           reads=er(5), writes=er(6))
    t.emit("scalar", lambda e: e.activation(
        out=rr(3), in_=rr(6), func=ACT.Sqrt),
        reads=er(6), writes=er(3))
    t.emit("vector", lambda e: e.tensor_tensor(
        rr(6), rr(3), rr(3), ALU.mult),
        reads=er(3), writes=er(6))
    t.emit("vector", lambda e: e.tensor_tensor(
        rr(6), rr(6), rr(5), ALU.mult),
        reads=er(5) + er(6), writes=er(6))
    t.emit("vector", lambda e: e.tensor_scalar(
        rr(6), rr(6), -0.5, 1.5, ALU.mult, ALU.add),
        reads=er(6), writes=er(6))
    t.emit("vector", lambda e: e.tensor_tensor(
        rr(3), rr(3), rr(6), ALU.mult),
        reads=er(6) + er(3), writes=er(3))
    t.emit("vector", lambda e: e.tensor_tensor(
        rr(4), rr(0), rr(3), ALU.mult),
        reads=er(0) + er(3), writes=er(4))
    t.emit("vector", lambda e: e.tensor_scalar(
        rr(4), rr(4), -1.0, None, ALU.mult),
        reads=er(4), writes=er(4))


def tap_blocks(s, name, buf, bufname, nblk, W, dt):
    if name not in s.taps:
        return
    ap = s.dtap(name, (nblk * 128, W), dt)
    t = s.t
    for j in range(nblk):
        t.emit("gpsimd", lambda e, j=j: e.dma_start(
            out=ap[j * 128:(j + 1) * 128, :], in_=buf[:, j * W:(j + 1) * W]),
            reads=rk(bufname, j * W, (j + 1) * W),
            writes=[("tap" + name, j)], dma=True)


def weight_mean_cols(s, w_ap, wname, K, M, pb, ws_name):
    t = s.t
    KB = K // 128
    first = True
    CW = 2048
    for j in range(KB):
        for (m0, m1) in segs(M, CW):
            Wd = m1 - m0
            a1, k1 = bW(s, 0, Wd)
            t.emit("sync", lambda e, j=j, m0=m0, a1=a1, Wd=Wd: e.dma_start(
                out=a1, in_=w_ap[j * 128:(j + 1) * 128, m0:m0 + Wd]),
                reads=[(wname,)], writes=k1, dma=True)
            c2 = 2048
            a2, k2 = bW(s, c2, c2 + Wd)
            t.emit("scalar", lambda e, a1=a1, a2=a2: e.activation(
                out=a2, in_=a1, func=ACT.Abs), reads=k1, writes=k2)
            t.emit("vector", lambda e, a2=a2: e.tensor_reduce(
                s.b_tok[:, 0:1], a2, AX.X, ALU.add),
                reads=k2, writes=[("tok", 0)])
            if first:
                t.emit("vector", lambda e: e.tensor_copy(s.b_tok[:, 1:2], s.b_tok[:, 0:1]),
                       reads=[("tok", 0)], writes=[("tok", 1)])
                first = False
            else:
                t.emit("vector", lambda e: e.tensor_tensor(
                    s.b_tok[:, 1:2], s.b_tok[:, 1:2], s.b_tok[:, 0:1], ALU.add),
                    reads=[("tok", 0), ("tok", 1)], writes=[("tok", 1)])
    p, r = pb.take()
    t.emit("tensor", lambda e: e.matmul(p[0:1, 0:1], s.b_tok[:, 1:2],
                                        s.b_onesc[:, 0:1], start=True, stop=True),
           reads=[("tok", 1), ("onesc",)], writes=r)
    t.emit("vector", lambda e: e.tensor_scalar(
        s.b_tok[0:1, 2:3], p[0:1, 0:1], 1.0 / (K * M), 1e-5, ALU.mult, ALU.max),
        reads=r, writes=[("tok", 2)])
    p2, r2 = pb.take()
    t.emit("tensor", lambda e: e.matmul(p2[:, 0:1], s.b_onesr[0:1, 0:128],
                                        s.b_tok[0:1, 2:3], start=True, stop=True),
           reads=[("tok", 2), ("onesr",)], writes=r2)
    cw = pv_alloc(s, ws_name, 3)
    s.pvcol[ws_name + "_thr"] = cw + 1
    s.pvcol[ws_name + "_nthr"] = cw + 2
    t.emit("vector", lambda e: e.tensor_copy(s.b_pv[:, cw:cw + 1], p2[:, 0:1]),
           reads=r2, writes=[("pv", ws_name)])
    t.emit("vector", lambda e: e.tensor_scalar(
        s.b_pv[:, cw + 1:cw + 2], p2[:, 0:1], 0.5, None, ALU.mult),
        reads=r2, writes=[("pv", ws_name + "_thr")])
    t.emit("vector", lambda e: e.tensor_scalar(
        s.b_pv[:, cw + 2:cw + 3], p2[:, 0:1], -0.5, None, ALU.mult),
        reads=r2, writes=[("pv", ws_name + "_nthr")])


def tern_block(s, w_ap, wname, j, m0, m1, dst_ap, dst_keys, ws_name, c_ld, c_ge):
    t = s.t
    Wd = m1 - m0
    a1, k1 = bW(s, c_ld, c_ld + Wd)
    t.emit("sync", lambda e, a1=a1: e.dma_start(
        out=a1, in_=w_ap[j * 128:(j + 1) * 128, m0:m1]),
        reads=[(wname,)], writes=k1, dma=True)
    a2, k2 = bW(s, c_ge, c_ge + Wd)
    t.emit("vector", lambda e, a1=a1, a2=a2: e.tensor_scalar(
        a2, a1, pvc(s, ws_name + "_thr"), None, ALU.is_ge),
        reads=k1 + [("pv", ws_name + "_thr")], writes=k2)
    t.emit("vector", lambda e, a1=a1: e.tensor_scalar(
        a1, a1, pvc(s, ws_name + "_nthr"), None, ALU.is_le),
        reads=k1 + [("pv", ws_name + "_nthr")], writes=k1)
    t.emit("vector", lambda e, a1=a1, a2=a2: e.tensor_tensor(
        dst_ap, a2, a1, ALU.subtract),
        reads=k1 + k2, writes=dst_keys)


def quant_rows(s, srcfn, nblk, pb, W_total, srow_row, orow_row, abs_c0,
               oscale=None):
    t = s.t
    for (a, b) in segs(W_total, 512):
        Wd = b - a
        acc, kacc = bW(s, abs_c0 + a, abs_c0 + b)
        ap0, k0 = srcfn(0, a, b)
        t.emit("scalar", lambda e, acc=acc, ap0=ap0: e.activation(
            out=acc, in_=ap0, func=ACT.Abs), reads=k0, writes=kacc)
        for j in range(1, nblk):
            apj, kj = srcfn(j, a, b)
            tmpa, tmpk = bW(s, len_W(s) - 512, len_W(s) - 512 + (b - a))
            t.emit("scalar", lambda e, tmpa=tmpa, apj=apj: e.activation(
                out=tmpa, in_=apj, func=ACT.Abs), reads=kj, writes=tmpk)
            t.emit("vector", lambda e, acc=acc, tmpa=tmpa: e.tensor_tensor(
                acc, acc, tmpa, ALU.max), reads=tmpk + kacc, writes=kacc)
    quant_rows_from_abs(s, pb, W_total, srow_row, orow_row, abs_c0, oscale)


def quant_rows_from_abs(s, pb, W_total, srow_row, orow_row, abs_c0, oscale=None):
    t = s.t
    nb = (W_total + 127) // 128
    assert nb <= 20
    for tb in range(nb):
        a = tb * 128
        Wd = min(128, W_total - a)
        p, r = pb.take()
        src, ksrc = bW(s, abs_c0 + a, abs_c0 + a + Wd)
        t.emit("tensor", lambda e, p=p, src=src, Wd=Wd: e.transpose(
            p[0:Wd, 0:128], src, s.b_cstI[:]),
            reads=ksrc + [("cstI",)], writes=r)
        t.emit("vector", lambda e, p=p, Wd=Wd, tb=tb: e.reduce_max(
            s.b_tok[0:Wd, 20 + tb:21 + tb], p[0:Wd, 0:128], AX.X),
            reads=r, writes=[("tok", 20 + tb)])
    for tb in range(nb):
        t.emit("vector", lambda e, tb=tb: e.tensor_scalar(
            s.b_tok[:, 20 + tb:21 + tb], s.b_tok[:, 20 + tb:21 + tb], 1e-5,
            None, ALU.max),
            reads=[("tok", 20 + tb)], writes=[("tok", 20 + tb)])
        t.emit("vector", lambda e, tb=tb: e.reciprocal(
            s.b_tok[:, 40 + tb:41 + tb], s.b_tok[:, 20 + tb:21 + tb]),
            reads=[("tok", 20 + tb)], writes=[("tok", 40 + tb)])
        t.emit("vector", lambda e, tb=tb: e.tensor_scalar(
            s.b_tok[:, 40 + tb:41 + tb], s.b_tok[:, 40 + tb:41 + tb], 127.0,
            None, ALU.mult),
            reads=[("tok", 40 + tb)], writes=[("tok", 40 + tb)])
        if oscale is not None:
            t.emit("vector", lambda e, tb=tb: e.tensor_scalar(
                s.b_tok[:, 20 + tb:21 + tb], s.b_tok[:, 20 + tb:21 + tb],
                pvc(s, oscale), None, ALU.mult),
                reads=[("tok", 20 + tb), ("pv", oscale)], writes=[("tok", 20 + tb)])
        t.emit("vector", lambda e, tb=tb: e.tensor_scalar(
            s.b_tok[:, 20 + tb:21 + tb], s.b_tok[:, 20 + tb:21 + tb],
            1.0 / 127.0, None, ALU.mult),
            reads=[("tok", 20 + tb)], writes=[("tok", 20 + tb)])
    for (cols, dstrow) in ((40, srow_row), (20, orow_row)):
        for tb in range(nb):
            a = tb * 128
            Wd = min(128, W_total - a)
            t.emit("gpsimd", lambda e, tb=tb, a=a, Wd=Wd, dstrow=dstrow, cols=cols:
                   e.dma_start(
                       out=s.b_rowsq[32 * (dstrow - 1):32 * (dstrow - 1) + 1,
                                     a:a + Wd],
                       in_=s.b_tok[0:Wd, cols + tb:cols + tb + 1]),
                   reads=[("tok", cols + tb)], writes=[("rowsq", dstrow)], dma=True)


def quant_apply(s, srcfn, dstfn, nblk, pb, W_total, srow_row, c_tmp):
    t = s.t
    for (a, b) in segs(W_total, 512):
        Wd = b - a
        ps, rs = bcast_row(s, pb,
                           s.b_rowsq[32 * (srow_row - 1):32 * (srow_row - 1) + 1, a:b],
                           [("rowsq", srow_row)], Wd, bf=True)
        for j in range(nblk):
            sap, skeys = srcfn(j, a, b)
            dap, dkeys = dstfn(j, a, b)
            t1a, t1k = bW(s, c_tmp, c_tmp + Wd)
            t.emit("vector", lambda e, t1a=t1a, sap=sap, ps=ps, Wd=Wd: e.tensor_tensor(
                t1a, sap, ps[:, 0:Wd], ALU.mult), reads=skeys + rs, writes=t1k)
            t.emit("vector", lambda e, t1a=t1a: e.tensor_scalar(
                t1a, t1a, 128.5, None, ALU.add), reads=t1k, writes=t1k)
            t.emit("vector", lambda e, t1a=t1a, Wd=Wd: e.tensor_copy(
                s.b_iq[:, 0:Wd], t1a), reads=t1k, writes=[("iq",)])
            t.emit("vector", lambda e, dap=dap, Wd=Wd: e.tensor_scalar(
                dap, s.b_iq[:, 0:Wd], -128.0, None, ALU.add),
                reads=[("iq",)], writes=dkeys)


def ph_ffn0(s):
    t = s.t
    NPB, NFB = s.NPB, s.NFB
    TC = s.TC; XW = s.XW; HW = s.HW; D = s.D; DFF = s.DFF
    pb = Pb(s)

    weight_mean_cols(s, s.f0w1t, "f0w1t", D, DFF, pb, "ws1")
    weight_mean_cols(s, s.f0w2t, "f0w2t", DFF, D, pb, "ws2")

    quant_rows(s, lambda j, a, b: bxn0(s, j, a, b), NPB, pb, XW,
               srow_row=1, orow_row=2, abs_c0=0)
    quant_apply(s, lambda j, a, b: bxn0(s, j, a, b),
                lambda j, a, b: bW2(s, j * XW + a, j * XW + b),
                NPB, pb, XW, srow_row=1, c_tmp=0)
    tap_blocks(s, "xq1", s.b_W2, "W2", NPB, XW, BF16)

    for j in range(NFB):
        a, k = bW2b(s, 0, D)
        tern_block(s, s.f0w2t, "f0w2t", j, 0, D, a, k, "ws2", c_ld=0, c_ge=1024)
        t.emit("gpsimd", lambda e, j=j, a=a: e.dma_start(
            out=s.s_tern2[j * 128:(j + 1) * 128, :], in_=a),
            reads=k, writes=[("s_tern2", j)], dma=True)

    FG = s.FG0
    FGH = FG * HW
    for fg in range(0, NFB, FG):
        gpar = (fg // FG) % 2
        nfi = min(FG, NFB - fg)
        for j in range(NPB):
            tern_block(s, s.f0w1t, "f0w1t", j, fg * 128, (fg + nfi) * 128,
                       s.b_wt[:, j * 1024 + gpar * 512:j * 1024 + gpar * 512 + nfi * 128],
                       [("wtg", j, gpar), ("wt", j)], "ws1",
                       c_ld=FGH + 1024 + (j % 2) * 512,
                       c_ge=FGH + 2048 + (j % 2) * 512)
        for (a, b) in segs(XW, 512):
            Wd = b - a
            repa, repk = bcast_sbuf(s, pb, s.b_rowsq[32:33, a:b],
                                    [("rowsq", 2)], Wd, FGH, bf=True)
            for fl in range(nfi):
                p, rp = pb.take()
                for j in range(NPB):
                    t.emit("tensor", lambda e, j=j, fl=fl, a=a, Wd=Wd, p=p,
                           gpar=gpar: e.matmul(
                        p[:, 0:Wd],
                        s.b_wt[:, j * 1024 + gpar * 512 + fl * 128:
                               j * 1024 + gpar * 512 + fl * 128 + 128],
                        s.b_W2[:, j * XW + a:j * XW + a + Wd],
                        start=(j == 0), stop=(j == NPB - 1)),
                        reads=[("wtg", j, gpar)] + rk("W2", j * XW + a, j * XW + b),
                        writes=rp)
                ha, hk = bW(s, fl * HW + a, fl * HW + a + Wd)
                t.emit("vector", lambda e, ha=ha, p=p, repa=repa, Wd=Wd:
                       e.scalar_tensor_tensor(ha, p[:, 0:Wd], pvc(s, "ws1"),
                                              repa, ALU.mult, ALU.mult),
                       reads=rp + repk + [("pv", "ws1")], writes=hk)
        for fl in range(nfi):
            fi = fg + fl
            base = fl * HW
            for (a, b) in segs(TC, 512):
                Wd = b - a
                cpar = ((fl * 4) + a // 512) % 2
                hsrc, hkeys = bW(s, base + a, base + a + Wd + 2)
                cva, cvk = bW(s, FGH + cpar * 512, FGH + cpar * 512 + Wd)
                t.emit("vector", lambda e, cva=cva, base=base, a=a, Wd=Wd, fi=fi:
                       e.tensor_scalar(cva, s.b_W[:, base + a:base + a + Wd],
                                       pvc(s, "f0cw", 3 * fi + 0), None, ALU.mult),
                       reads=hkeys + [("pv", "f0cw")], writes=cvk)
                for tap in (1, 2):
                    t.emit("vector", lambda e, cva=cva, base=base, a=a, Wd=Wd,
                           fi=fi, tap=tap: e.scalar_tensor_tensor(
                               cva, s.b_W[:, base + a + tap:base + a + tap + Wd],
                               pvc(s, "f0cw", 3 * fi + tap), cva, ALU.mult, ALU.add),
                           reads=hkeys + cvk + [("pv", "f0cw")], writes=cvk)
                sha, shk = bW2b(s, 2048 + a % 2048, 2048 + a % 2048 + Wd)
                t.emit("scalar", lambda e, sha=sha, cva=cva, fi=fi: e.activation(
                    out=sha, in_=cva, func=ACT.Silu, bias=pvc(s, "f0cb", fi),
                    scale=1.0),
                    reads=cvk + [("pv", "f0cb")], writes=shk)
                t.emit("gpsimd", lambda e, sha=sha, fi=fi, a=a, Wd=Wd: e.dma_start(
                    out=s.s_sh[fi * 128:(fi + 1) * 128, a:a + Wd], in_=sha),
                    reads=shk, writes=[("s_sh", fi)], dma=True)
                acc, kacc = bW2b(s, 4096 + a, 4096 + b)
                if fi == 0:
                    t.emit("scalar", lambda e, acc=acc, sha=sha: e.activation(
                        out=acc, in_=sha, func=ACT.Abs), reads=shk, writes=kacc)
                else:
                    tmpa, tmpk = bW2b(s, (a // 512 % 2) * 512, (a // 512 % 2) * 512 + Wd)
                    t.emit("scalar", lambda e, tmpa=tmpa, sha=sha: e.activation(
                        out=tmpa, in_=sha, func=ACT.Abs), reads=shk, writes=tmpk)
                    t.emit("vector", lambda e, acc=acc, tmpa=tmpa: e.tensor_tensor(
                        acc, acc, tmpa, ALU.max), reads=tmpk + kacc, writes=kacc)

    for (a, b) in segs(TC, 512):
        aw, kw = bW(s, a, b)
        ab2, kb2 = bW2b(s, 4096 + a, 4096 + b)
        t.emit("vector", lambda e, aw=aw, ab2=ab2: e.tensor_copy(aw, ab2),
               reads=kb2, writes=kw)
    quant_rows_from_abs(s, pb, TC, srow_row=3, orow_row=4, abs_c0=0, oscale="ws2")

    for (a, b) in segs(TC, 512):
        Wd = b - a
        srepa, srepk = bcast_sbuf(s, pb, s.b_rowsq[64:65, a:b],
                                  [("rowsq", 3)], Wd, FGH, bf=True)
        for j in range(NFB):
            t.emit("sync", lambda e, j=j, a=a, Wd=Wd: e.dma_start(
                out=s.b_W2b[:, 0:Wd], in_=s.s_sh[j * 128:(j + 1) * 128, a:a + Wd]),
                reads=[("s_sh", j)], writes=rk("W2b", 0, Wd), dma=True)
            t1a, t1k = bW(s, 0, Wd)
            t.emit("vector", lambda e, t1a=t1a, srepa=srepa, Wd=Wd: e.tensor_tensor(
                t1a, s.b_W2b[:, 0:Wd], srepa, ALU.mult),
                reads=rk("W2b", 0, Wd) + srepk, writes=t1k)
            t.emit("vector", lambda e, t1a=t1a: e.tensor_scalar(
                t1a, t1a, 128.5, None, ALU.add), reads=t1k, writes=t1k)
            t.emit("vector", lambda e, t1a=t1a, Wd=Wd: e.tensor_copy(
                s.b_iq[:, 0:Wd], t1a), reads=t1k, writes=[("iq",)])
            t.emit("vector", lambda e, j=j, Wd=Wd: e.tensor_scalar(
                s.b_W2[:, j * 512:j * 512 + Wd], s.b_iq[:, 0:Wd], -128.0,
                None, ALU.add),
                reads=[("iq",)], writes=rk("W2", j * 512, j * 512 + Wd))
        orepa, orepk = bcast_sbuf(s, pb, s.b_rowsq[96:97, a:b],
                                  [("rowsq", 4)], Wd, FGH + 512, bf=True)
        pouts = [(s.psum[i], [("psum", i)]) for i in range(NPB)]
        for j in range(NFB):
            t.emit("sync", lambda e, j=j: e.dma_start(
                out=s.b_wt[:, (j % 8) * 1024:(j % 8) * 1024 + D],
                in_=s.s_tern2[j * 128:(j + 1) * 128, :]),
                reads=[("s_tern2", j)], writes=[("wt", j % 8)], dma=True)
            for oi in range(NPB):
                p, rp = pouts[oi]
                t.emit("tensor", lambda e, j=j, oi=oi, p=p, Wd=Wd: e.matmul(
                    p[:, 0:Wd],
                    s.b_wt[:, (j % 8) * 1024 + oi * 128:(j % 8) * 1024 + oi * 128 + 128],
                    s.b_W2[:, j * 512:j * 512 + Wd],
                    start=(j == 0), stop=(j == NFB - 1)),
                    reads=[("wt", j % 8)] + rk("W2", j * 512, j * 512 + Wd),
                    writes=rp)
        for oi in range(NPB):
            p, rp = pouts[oi]
            t1a, t1k = bW(s, 0, Wd)
            t.emit("vector", lambda e, t1a=t1a, p=p, orepa=orepa, Wd=Wd:
                   e.tensor_tensor(t1a, p[:, 0:Wd], orepa, ALU.mult),
                   reads=rp + orepk, writes=t1k)
            xa = s.b_x1[:, oi * XW + 1 + a:oi * XW + 1 + b]
            xkeys = rk("x1", oi * XW + 1 + a, oi * XW + 1 + b)
            t.emit("vector", lambda e, xa=xa, t1a=t1a: e.tensor_tensor(
                xa, xa, t1a, ALU.add), reads=t1k + xkeys, writes=xkeys)
    tap_blocks(s, "x1f", s.b_x1, "x1", NPB, XW, BF16)


def bxn0(s, j, a, b):
    XW = s.XW
    return s.b_xn0[:, j * XW + a:j * XW + b], rk("xn0", j * XW + a, j * XW + b)


def ph_finish(s):
    t = s.t
    TC = s.TC; XW = s.XW
    for j in range(s.NPB):
        t.emit("gpsimd", lambda e, j=j: e.dma_start(
            out=s.out[j * 128:(j + 1) * 128, :],
            in_=s.b_x1[:, j * XW + 1:j * XW + 1 + TC]),
            reads=rk("x1", j * XW + 1, j * XW + 1 + TC),
            writes=[("outw", j)], dma=True)
    t.eng["gpsimd"].wait_ge(t.sem[("gpsimd", "d")], t.cnt[("gpsimd", "d")])
    t.eng["sync"].wait_ge(t.sem[("sync", "d")], t.cnt[("sync", "d")])


def host_layout(cfg, inputs, core):
    TC = cfg["TC"]; D = cfg["D"]; NPB = D // 128
    NC = cfg["NCORES"]
    x = np.asarray(inputs["L0_attn_x"])[0]
    T = x.shape[0]
    assert T == TC * NC
    t0 = core * TC
    xe = np.zeros((TC + 2, D), np.float32)
    xe[1:TC + 1] = x[t0:t0 + TC]
    if t0 > 0:
        xe[0] = x[t0 - 1]
    if t0 + TC < T:
        xe[TC + 1] = x[t0 + TC]
    m = {}
    m["xT"] = np.ascontiguousarray(xe.T)

    def v128(vec):
        v = np.asarray(vec, np.float32).reshape(-1)
        return np.ascontiguousarray(v.reshape(-1, 128).T)
    m["vmix"] = np.concatenate([v128(inputs["L0_attn_x_" + nm])
                                for nm in ["r", "w", "k", "v", "a", "g"]], axis=1)
    m["vw0"] = v128(inputs["L0_attn_w0"])
    m["va0"] = v128(inputs["L0_attn_a0"])
    m["vkk"] = v128(inputs["L0_attn_k_k"])
    m["vka"] = v128(inputs["L0_attn_k_a"])
    m["vlnw"] = v128(inputs["L0_attn_ln_w"])
    m["vlnb"] = v128(inputs["L0_attn_ln_b"])
    m["vrk"] = v128(inputs["L0_attn_r_k"].reshape(-1))
    m["f0w1t"] = np.ascontiguousarray(np.asarray(inputs["L0_ffn_w1"], np.float32).T)
    m["f0w2t"] = np.ascontiguousarray(np.asarray(inputs["L0_ffn_w2"], np.float32).T)
    cw = np.asarray(inputs["L0_ffn_conv_w"], np.float32)[:, 0, :]
    NFB = cfg["DFF"] // 128
    m["f0cw"] = np.ascontiguousarray(
        cw.reshape(NFB, 128, 3).transpose(1, 0, 2).reshape(128, NFB * 3)
    )
    m["f0cb"] = v128(inputs["L0_ffn_conv_b"])
    m["ident"] = np.eye(128, dtype=np.float32)
    C = cfg["C"]
    m["wr"] = np.asarray(inputs["L0_attn_W_r"], np.float32)
    m["wk"] = np.asarray(inputs["L0_attn_W_k"], np.float32)
    m["wv"] = np.asarray(inputs["L0_attn_W_v"], np.float32)
    m["wo"] = np.asarray(inputs["L0_attn_W_o"], np.float32)
    m["lw1"] = np.asarray(inputs["L0_attn_w1"], np.float32)
    m["lw2"] = np.asarray(inputs["L0_attn_w2"], np.float32)
    m["la1"] = np.asarray(inputs["L0_attn_a1"], np.float32)
    m["la2"] = np.asarray(inputs["L0_attn_a2"], np.float32)
    m["lg1"] = np.asarray(inputs["L0_attn_g1"], np.float32)
    m["lg2"] = np.asarray(inputs["L0_attn_g2"], np.float32)
    hs2 = np.zeros((128, 2), np.float32)
    hs2[0:64, 0] = 1.0
    hs2[64:128, 1] = 1.0
    m["hsum2"] = hs2
    hx2 = np.zeros((2, 128), np.float32)
    hx2[0, 0:64] = 1.0
    hx2[1, 64:128] = 1.0
    m["hexp2"] = hx2
    rst = np.ones((128, TC), np.float32)
    rst[:, ::C] = 0.0
    m["rstpat"] = rst
    pfx = np.zeros((128, NC), np.float32)
    pfx[:, 0:core] = 1.0
    m["pfxm64"] = pfx
    stril = (np.arange(C)[:, None] < np.arange(C)[None, :]).astype(np.float32)
    tril = (np.arange(C)[:, None] <= np.arange(C)[None, :]).astype(np.float32)
    ms_ = np.zeros((128, C), np.float32)
    ms_[0:C] = stril
    ms_[C:2 * C] = stril
    m["m_stril"] = ms_
    mt_ = np.zeros((128, C), np.float32)
    mt_[0:C] = tril
    mt_[C:2 * C] = tril
    m["m_tril"] = mt_
    C2 = cfg["C2"]; NQ = C2 // 128
    DSL = D // NC
    cw = np.asarray(inputs["L1_attn_conv_w"], np.float32)[:, 0, :]
    m["cw1"] = np.ascontiguousarray(
        cw.reshape(NQ, 128, 4).transpose(1, 0, 2).reshape(128, NQ * 4))
    m["cb1"] = v128(inputs["L1_attn_conv_b"])
    m["xprojt"] = np.ascontiguousarray(np.asarray(inputs["L1_attn_x_proj_w"], np.float32).T)
    m["dtprojt"] = np.ascontiguousarray(np.asarray(inputs["L1_attn_dt_proj_w"], np.float32).T)
    m["vdtb"] = v128(inputs["L1_attn_dt_proj_b"])
    al = np.asarray(inputs["L1_attn_A_log"], np.float32)[core * DSL:(core + 1) * DSL]
    NTI = max(DSL * 16 // 128, 1)
    m["alog_lanes"] = np.ascontiguousarray(
        np.stack([al[8 * ti:8 * ti + 8, :].reshape(-1) for ti in range(NTI)], axis=1))
    vD = np.zeros((128, 1), np.float32)
    vD[0:DSL, 0] = np.asarray(inputs["L1_attn_D"], np.float32)[core * DSL:(core + 1) * DSL]
    m["vD"] = vD
    msk = np.zeros((128, 2 * NC), np.float32)
    if core - 1 >= 0:
        msk[:, core - 1] = 1.0
    if core + 1 < NC:
        msk[:, NC + core + 1] = 1.0
    m["mskLR"] = msk
    ps = np.zeros((128, 72), np.float32)
    for p in range(128):
        ps[p, p // 2] = 0.5
        ps[p, 64 + p // 16] = 1.0
    m["psum64"] = ps
    m["qg"] = np.ascontiguousarray(np.asarray(inputs["L1_ffn_w_gate"]).T).astype(np.int32)
    m["qu"] = np.ascontiguousarray(np.asarray(inputs["L1_ffn_w_up"]).T).astype(np.int32)
    m["qd"] = np.ascontiguousarray(np.asarray(inputs["L1_ffn_w_down"]).T).astype(np.int32)
    return m


def run_tiny_test():
    pass


def lerp_d(s, j, t0, W, c_d):
    t = s.t
    XW = s.XW
    da, dk = bW(s, c_d, c_d + W)
    t.emit("vector", lambda e, da=da: e.tensor_tensor(
        da, s.b_xn0[:, j * XW + t0:j * XW + t0 + W],
        s.b_xn0[:, j * XW + t0 + 1:j * XW + t0 + 1 + W], ALU.subtract),
        reads=rk("xn0", j * XW + t0, j * XW + t0 + 1 + W), writes=dk)
    return da, dk


def lerp_apply(s, j, t0, W, mix_idx, da, dk, dst_ap, dst_keys):
    t = s.t
    XW = s.XW
    t.emit("vector", lambda e: e.scalar_tensor_tensor(
        dst_ap, da, pvc(s, "vmix", mix_idx * s.NPB + j),
        s.b_xn0[:, j * XW + t0 + 1:j * XW + t0 + 1 + W], ALU.mult, ALU.add),
        reads=dk + rk("xn0", j * XW + t0 + 1, j * XW + t0 + 1 + W) + [("pv", "vmix")],
        writes=dst_keys)


def load_bigw(s, w_ap, wname):
    t = s.t
    D = s.D
    NPB = s.NPB
    t.emit("gpsimd", lambda e: e.dma_start(
        out=s.b_wt[:, 0:NPB * D],
        in_=w_ap.rearrange("(ks p) m -> p ks m", p=128)),
        reads=[(wname,)],
        writes=[("wt", j) for j in range(8)] +
        [("wlr", n) for n in ("lw1", "la1", "lg1", "lw2", "la2", "lg2")], dma=True)


def mm_proj(s, pb, rhsfn, out_cb, width, kdim=None, wslot=None):
    t = s.t
    D = s.D
    NPB = s.NPB
    outs = []
    for mj in range(NPB):
        p, rp = pb.take()
        for ks in range(NPB):
            ap, keys = rhsfn(ks)
            t.emit("tensor", lambda e, p=p, mj=mj, ks=ks, ap=ap: e.matmul(
                p[:, 0:width], s.b_wt[:, ks * D + mj * 128:ks * D + mj * 128 + 128],
                ap, start=(ks == 0), stop=(ks == NPB - 1)),
                reads=[("wt", ks)] + keys, writes=rp)
        outs.append((p, rp))
    return outs


def ph_rwkv_proj(s):
    t = s.t
    NPB = s.NPB
    TC = s.TC; XW = s.XW; D = s.D
    H = s.H
    pb = Pb(s)
    DW, DA, DG = s.cfg["DW"], s.cfg["DA"], s.cfg["DG"]
    lr = [("lw1", 0, (D, DW), "ks"), ("la1", 512, (D, DA), "ks"),
          ("lg1", 1024, (D, DG), "ks"), ("lw2", 2048, (DW, D), "flat"),
          ("la2", 3072, (DA, D), "flat"), ("lg2", 4096, (DG, D), "flat")]
    s.lroff = {}
    for nm, off, shape, kind in lr:
        ap = getattr(s, nm)
        s.lroff[nm] = off
        if kind == "ks":
            t.emit("gpsimd", lambda e, ap=ap, off=off, w=shape[1]: e.dma_start(
                out=s.b_wlr[:, off:off + NPB * w],
                in_=ap.rearrange("(ks p) m -> p ks m", p=128)),
                reads=[(nm,)], writes=[("wlr", nm)], dma=True)
        else:
            t.emit("gpsimd", lambda e, ap=ap, off=off: e.dma_start(
                out=s.b_wlr[:, off:off + shape[1]][0:shape[0], :], in_=ap),
                reads=[(nm,)], writes=[("wlr", nm)], dma=True)
    pv_alloc(s, "omka", NPB)
    for j in range(NPB):
        t.emit("vector", lambda e, j=j: e.tensor_scalar(
            pvc(s, "omka", j), pvc(s, "vka", j), -1.0, 1.0, ALU.mult, ALU.add),
            reads=[("pv", "vka")], writes=[("pv", "omka")])

    MIX = {"r": 0, "w": 1, "k": 2, "v": 3, "a": 4, "g": 5}

    def lowrank(mix, w1n, wdim, w2n, bias, sink):
        for (t0, t1) in segs(TC, 512):
            W = t1 - t0
            p1, r1 = pb.take()
            for j in range(NPB):
                da, dk = lerp_d(s, j, t0, W, s.LERP0)
                xa, xk = bW2b(s, 0, W)
                lerp_apply(s, j, t0, W, mix, da, dk, xa, xk)
                t.emit("tensor", lambda e, p1=p1, j=j, xa=xa, W=W: e.matmul(
                    p1[0:wdim, 0:W],
                    s.b_wlr[:, s.lroff[w1n] + j * wdim:s.lroff[w1n] + (j + 1) * wdim],
                    xa, start=(j == 0), stop=(j == NPB - 1)),
                    reads=[("wlr", w1n)] + xk, writes=r1)
            lowa, lowk = bW2b(s, 1024, 1024 + W)
            fn1 = ACT.Tanh if w1n == "lw1" else (ACT.Sigmoid if w1n == "lg1" else ACT.Copy)
            t.emit("scalar", lambda e, lowa=lowa, p1=p1, W=W, fn1=fn1: e.activation(
                out=lowa[0:wdim, :], in_=p1[0:wdim, 0:W], func=fn1),
                reads=r1, writes=lowk)
            for mj in range(NPB):
                p2, r2 = pb.take()
                t.emit("tensor", lambda e, p2=p2, mj=mj, lowa=lowa, W=W: e.matmul(
                    p2[:, 0:W],
                    s.b_wlr[:, s.lroff[w2n] + mj * 128:s.lroff[w2n] + (mj + 1) * 128][0:wdim, :],
                    lowa[0:wdim, :], start=True, stop=True),
                    reads=[("wlr", w2n)] + lowk, writes=r2)
                sink(mj, t0, W, p2, r2)

    def sink_a(mj, t0, W, p2, r2):
        dst, dkeys = bW2(s, mj * TC + t0, mj * TC + t0 + W)
        t.emit("scalar", lambda e, dst=dst, p2=p2, W=W, mj=mj: e.activation(
            out=dst, in_=p2[:, 0:W], func=ACT.Sigmoid, bias=pvc(s, "va0", mj),
            scale=1.0),
            reads=r2 + [("pv", "va0")], writes=dkeys)

    def sink_sz(mj, t0, W, p2, r2):
        dst, dkeys = bW2b(s, 2048, 2048 + W)
        t.emit("scalar", lambda e, dst=dst, p2=p2, W=W, mj=mj: e.activation(
            out=dst, in_=p2[:, 0:W], func=ACT.Sigmoid, bias=pvc(s, "vw0", mj),
            scale=1.0),
            reads=r2 + [("pv", "vw0")], writes=dkeys)
        t.emit("gpsimd", lambda e, dst=dst, mj=mj, t0=t0, W=W: e.dma_start(
            out=s.s_sz[mj * 128:(mj + 1) * 128, t0:t0 + W], in_=dst),
            reads=dkeys, writes=[("s_sz", mj)], dma=True)

    def sink_g(mj, t0, W, p2, r2):
        dst, dkeys = bW2b(s, 2048, 2048 + W)
        t.emit("vector", lambda e, dst=dst, p2=p2, W=W: e.tensor_copy(
            dst, p2[:, 0:W]), reads=r2, writes=dkeys)
        t.emit("gpsimd", lambda e, dst=dst, mj=mj, t0=t0, W=W: e.dma_start(
            out=s.s_g[mj * 128:(mj + 1) * 128, t0:t0 + W], in_=dst),
            reads=dkeys, writes=[("s_g", mj)], dma=True)

    lowrank(MIX["a"], "la1", s.cfg["DA"], "la2", "va0", sink_a)
    lowrank(MIX["w"], "lw1", s.cfg["DW"], "lw2", "vw0", sink_sz)
    lowrank(MIX["g"], "lg1", s.cfg["DG"], "lg2", None, sink_g)

    load_bigw(s, s.wk, "wk")
    for (t0, t1) in segs(TC, 512):
        W = t1 - t0
        for j in range(NPB):
            da, dk = lerp_d(s, j, t0, W, s.LERP0)
            xa, xk_ = bW2b(s, j * 512, j * 512 + W)
            lerp_apply(s, j, t0, W, MIX["k"], da, dk, xa, xk_)
        outs = mm_proj(s, pb, lambda ks: bW2b(s, ks * 512, ks * 512 + W), None, W)
        for mj in range(NPB):
            p, rp = outs[mj]
            ka_, kk_ = bW(s, mj * 512, mj * 512 + W)
            t.emit("vector", lambda e, ka_=ka_, p=p, W=W: e.tensor_copy(
                ka_, p[:, 0:W]), reads=rp, writes=kk_)
        for mj in range(NPB):
            ka_, kk_ = bW(s, mj * 512, mj * 512 + W)
            kka, kkk = bW(s, s.KK0 + mj * 512, s.KK0 + mj * 512 + W)
            t.emit("vector", lambda e, kka=kka, ka_=ka_, mj=mj: e.tensor_scalar(
                kka, ka_, pvc(s, "vkk", mj), None, ALU.mult),
                reads=kk_ + [("pv", "vkk")], writes=kkk)
            sqa, sqk = bW(s, s.SQ0, s.SQ0 + W)
            t.emit("scalar", lambda e, sqa=sqa, kka=kka: e.activation(
                out=sqa, in_=kka, func=ACT.Square), reads=kkk, writes=sqk)
            pss, rss = pb.take()
            t.emit("tensor", lambda e, pss=pss, sqa=sqa, W=W: e.matmul(
                pss[0:2, 0:W], s.b_hsum2[:, 0:2] if False else s.b_pv[:, 0:0] if False else s.hs2_ap(), sqa,
                start=True, stop=True),
                reads=[("hsum2c",)] + sqk, writes=rss)
            tw = s.b_two
            t.emit("vector", lambda e, pss=pss, W=W: e.tensor_scalar(
                tw[:, 0:W], pss[0:2, 0:W], 1e-24, None, ALU.max),
                reads=rss, writes=[("two", 0)])
            t.emit("vector", lambda e, W=W: e.reciprocal(tw[:, 512:512 + W], tw[:, 0:W]),
                   reads=[("two", 0)], writes=[("two", 1)])
            t.emit("scalar", lambda e, W=W: e.activation(
                out=tw[:, 1024:1024 + W], in_=tw[:, 512:512 + W], func=ACT.Sqrt),
                reads=[("two", 1)], writes=[("two", 2)])
            t.emit("vector", lambda e, W=W: e.tensor_tensor(
                tw[:, 1536:1536 + W], tw[:, 1024:1024 + W], tw[:, 1024:1024 + W],
                ALU.mult), reads=[("two", 2)], writes=[("two", 3)])
            t.emit("vector", lambda e, W=W: e.tensor_tensor(
                tw[:, 1536:1536 + W], tw[:, 1536:1536 + W], tw[:, 0:W], ALU.mult),
                reads=[("two", 0), ("two", 3)], writes=[("two", 3)])
            t.emit("vector", lambda e, W=W: e.tensor_scalar(
                tw[:, 1536:1536 + W], tw[:, 1536:1536 + W], -0.5, 1.5, ALU.mult,
                ALU.add), reads=[("two", 3)], writes=[("two", 3)])
            t.emit("vector", lambda e, W=W: e.tensor_tensor(
                tw[:, 1024:1024 + W], tw[:, 1024:1024 + W], tw[:, 1536:1536 + W],
                ALU.mult), reads=[("two", 2), ("two", 3)], writes=[("two", 2)])
            pbr, rbr = pb.take()
            t.emit("tensor", lambda e, pbr=pbr, W=W: e.matmul(
                pbr[:, 0:W], s.hexp2_ap(), tw[0:2, 1024:1024 + W],
                start=True, stop=True),
                reads=[("hexp2c",), ("two", 2)], writes=rbr)
            kapa, kapk = bW2b(s, 0, W)
            t.emit("vector", lambda e, kapa=kapa, kka=kka, pbr=pbr, W=W:
                   e.tensor_tensor(kapa, kka, pbr[:, 0:W], ALU.mult),
                   reads=kkk + rbr, writes=kapk)
            t.emit("gpsimd", lambda e, kapa=kapa, mj=mj, t0=t0, W=W: e.dma_start(
                out=s.s_kap[mj * 128:(mj + 1) * 128, t0:t0 + W], in_=kapa),
                reads=kapk, writes=[("s_kap", mj)], dma=True)
            aba, abk = bW2b(s, 1024, 1024 + W)
            t.emit("vector", lambda e, aba=aba, kapa=kapa, mj=mj, t0=t0, W=W:
                   e.tensor_tensor(aba, kapa, s.b_W2[:, mj * TC + t0:mj * TC + t0 + W],
                                   ALU.mult),
                   reads=kapk + rk("W2", mj * TC + t0, mj * TC + t0 + W), writes=abk)
            t.emit("gpsimd", lambda e, aba=aba, mj=mj, t0=t0, W=W: e.dma_start(
                out=s.s_ab[mj * 128:(mj + 1) * 128, t0:t0 + W], in_=aba),
                reads=abk, writes=[("s_ab", mj)], dma=True)
            ma, mk = bW(s, s.SQ0, s.SQ0 + W)
            t.emit("vector", lambda e, ma=ma, mj=mj, t0=t0, W=W: e.tensor_scalar(
                ma, s.b_W2[:, mj * TC + t0:mj * TC + t0 + W], pvc(s, "vka", mj),
                pvc(s, "omka", mj), ALU.mult, ALU.add),
                reads=rk("W2", mj * TC + t0, mj * TC + t0 + W) +
                [("pv", "vka"), ("pv", "omka")], writes=mk)
            kma, kmk = bW2b(s, 0, W)
            t.emit("vector", lambda e, kma=kma, ka_=ka_, ma=ma: e.tensor_tensor(
                kma, ka_, ma, ALU.mult), reads=kk_ + mk, writes=kmk)
            t.emit("gpsimd", lambda e, kma=kma, mj=mj, t0=t0, W=W: e.dma_start(
                out=s.s_kmod[mj * 128:(mj + 1) * 128, t0:t0 + W], in_=kma),
                reads=kmk, writes=[("s_kmod", mj)], dma=True)

    load_bigw(s, s.wr, "wr")
    for (t0, t1) in segs(TC, 512):
        W = t1 - t0
        for j in range(NPB):
            da, dk = lerp_d(s, j, t0, W, s.LERP0)
            xa, xk_ = bW2b(s, j * 512, j * 512 + W)
            lerp_apply(s, j, t0, W, MIX["r"], da, dk, xa, xk_)
        outs = mm_proj(s, pb, lambda ks: bW2b(s, ks * 512, ks * 512 + W), None, W)
        for mj in range(NPB):
            p, rp = outs[mj]
            ra, rk_ = bW(s, mj * 512, mj * 512 + W)
            t.emit("vector", lambda e, ra=ra, p=p, W=W: e.tensor_copy(
                ra, p[:, 0:W]), reads=rp, writes=rk_)
            rb, rbk = bW2b(s, 0, W)
            t.emit("vector", lambda e, rb=rb, ra=ra: e.tensor_copy(rb, ra),
                   reads=rk_, writes=rbk)
            t.emit("gpsimd", lambda e, rb=rb, mj=mj, t0=t0, W=W: e.dma_start(
                out=s.s_r[mj * 128:(mj + 1) * 128, t0:t0 + W], in_=rb),
                reads=rbk, writes=[("s_r", mj)], dma=True)
        for mj in range(NPB):
            ra, rk_ = bW(s, mj * 512, mj * 512 + W)
            kma, kmk = bW2b(s, 1024, 1024 + W)
            t.emit("sync", lambda e, kma=kma, mj=mj, t0=t0, W=W: e.dma_start(
                out=kma, in_=s.s_kmod[mj * 128:(mj + 1) * 128, t0:t0 + W]),
                reads=[("s_kmod", mj)], writes=kmk, dma=True)
            rka, rkk = bW(s, s.SQ0, s.SQ0 + W)
            t.emit("vector", lambda e, rka=rka, ra=ra, kma=kma, mj=mj:
                   e.scalar_tensor_tensor(rka, ra, pvc(s, "vrk", mj), kma,
                                          ALU.mult, ALU.mult),
                   reads=rk_ + kmk + [("pv", "vrk")], writes=rkk)
            pss, rss = pb.take()
            t.emit("tensor", lambda e, pss=pss, rka=rka, W=W: e.matmul(
                pss[0:2, 0:W], s.hs2_ap(), rka, start=True, stop=True),
                reads=[("hsum2c",)] + rkk, writes=rss)
            t.emit("vector", lambda e, pss=pss, W=W: e.tensor_copy(
                s.b_two[:, 0:W], pss[0:2, 0:W]), reads=rss, writes=[("two", 0)])
            t.emit("gpsimd", lambda e, mj=mj, t0=t0, W=W: e.dma_start(
                out=s.s_srk[mj * 2:(mj + 1) * 2, t0:t0 + W], in_=s.b_two[:, 0:W]),
                reads=[("two", 0)], writes=[("s_srk", mj)], dma=True)

    load_bigw(s, s.wv, "wv")
    for (t0, t1) in segs(TC, 512):
        W = t1 - t0
        for j in range(NPB):
            da, dk = lerp_d(s, j, t0, W, s.LERP0)
            xa, xk_ = bW2b(s, j * 512, j * 512 + W)
            lerp_apply(s, j, t0, W, MIX["v"], da, dk, xa, xk_)
        outs = mm_proj(s, pb, lambda ks: bW2b(s, ks * 512, ks * 512 + W), None, W)
        for mj in range(NPB):
            p, rp = outs[mj]
            va, vk = bW2b(s, 0, W)
            t.emit("vector", lambda e, va=va, p=p, W=W: e.tensor_copy(
                va, p[:, 0:W]), reads=rp, writes=vk)
            t.emit("gpsimd", lambda e, va=va, mj=mj, t0=t0, W=W: e.dma_start(
                out=s.s_v[mj * 128:(mj + 1) * 128, t0:t0 + W], in_=va),
                reads=vk, writes=[("s_v", mj)], dma=True)

    for j in range(NPB):
        sza, szk = bW2b(s, 0, TC)
        t.emit("sync", lambda e, sza=sza, j=j: e.dma_start(
            out=sza, in_=s.s_sz[j * 128:(j + 1) * 128, :]),
            reads=[("s_sz", j)], writes=szk, dma=True)
        lpa, lpk = bW(s, 0, TC)
        t.emit("vector", lambda e, lpa=lpa, sza=sza: e.tensor_tensor_scan(
            lpa, s.b_rst[:], sza, 0.0, ALU.mult, ALU.add),
            reads=szk + [("rst",)], writes=lpk)
        t.emit("gpsimd", lambda e, lpa=lpa, j=j: e.dma_start(
            out=s.s_lnp[j * 128:(j + 1) * 128, :], in_=lpa),
            reads=lpk, writes=[("s_lnp", j)], dma=True)


def bdsl(s, i, w=128):
    return s.b_bd[:, i * 128:i * 128 + w], [("bd", i)]


def ktsl(s, i, w=64):
    return s.b_kt[:, i * 64:i * 64 + w], [("kt", i)]


def st8sl(s, i, w=64):
    return s.b_st8[:, i * 64:i * 64 + w], [("st8", i)]


EXPF = 0.60653065971263342


def ph_rwkv_chunks(s):
    t = s.t
    NPB = s.NPB; TC = s.TC; C = s.C
    s.cu_par = 0
    pb = Pb(s, range(6))
    t.emit("vector", lambda e: e.memset(s.b_bd[:], 0.0),
           writes=[("bd", i) for i in range(20)])
    t.emit("vector", lambda e: e.memset(s.b_kt[:], 0.0),
           writes=[("kt", i) for i in range(16)])
    HHALF = TC // 2
    for j in range(NPB):
        t.emit("vector", lambda e: e.tensor_copy(s.b_ec[0:64, 0:64],
                                                 s.b_cstI[0:64, 0:64]),
               reads=[("cstI",)], writes=[("ecE", 0)])
        t.emit("vector", lambda e: e.tensor_copy(s.b_ec[64:128, 0:64],
                                                 s.b_cstI[64:128, 64:128]),
               reads=[("cstI",)], writes=[("ecE", 0)])
        t.emit("vector", lambda e: e.memset(s.b_ec[:, 64:128], 0.0),
               writes=[("ecD", 0)])
        cur = 0
        for half in range(2):
            h0 = half * HHALF
            for nm, off in [("s_kmod", 0), ("s_ab", 1024), ("s_kap", 2048),
                            ("s_r", 3072), ("s_v", 4096)]:
                ap = getattr(s, nm)
                t.emit("sync", lambda e, ap=ap, off=off, j=j, h0=h0: e.dma_start(
                    out=s.b_W2b[:, off:off + HHALF],
                    in_=ap[j * 128:(j + 1) * 128, h0:h0 + HHALF]),
                    reads=[(nm, j)], writes=rk("W2b", off, off + HHALF), dma=True)
            t.emit("sync", lambda e, j=j, h0=h0: e.dma_start(
                out=s.b_W[:, 0:HHALF],
                in_=s.s_lnp[j * 128:(j + 1) * 128, h0:h0 + HHALF]),
                reads=[("s_lnp", j)], writes=rk("W", 0, HHALF), dma=True)
            t.emit("gpsimd", lambda e, j=j, h0=h0: e.dma_start(
                out=s.b_W[:, 2048:2048 + HHALF],
                in_=s.s_sz[j * 128:(j + 1) * 128, h0:h0 + HHALF]),
                reads=[("s_sz", j)], writes=rk("W", 2048, 2048 + HHALF), dma=True)
            for cl in range(HHALF // C):
                c = half * (HHALF // C) + cl
                cur = chunk_unit(s, pb, j, c, cl, cur)
        ecE = s.b_ec[:, cur * 128:cur * 128 + 64]
        ecD = s.b_ec[:, cur * 128 + 64:cur * 128 + 128]
        ef, efk = st8sl(s, 9, 64)
        t.emit("vector", lambda e, ecE=ecE, ef=ef: e.tensor_copy(ef, ecE),
               reads=[("ecE", cur)], writes=efk)
        p, rp = pb.take()
        t.emit("tensor", lambda e, p=p, ef=ef: e.transpose(
            p[0:64, 0:128], ef, s.b_cstI[:]),
            reads=efk + [("cstI",)], writes=rp)
        ea, ek = bW(s, s.EX0, s.EX0 + 128)
        t.emit("vector", lambda e, p=p, ea=ea: e.tensor_copy(ea[0:64, :], p[0:64, 0:128]),
               reads=rp, writes=ek)
        t.emit("gpsimd", lambda e, ea=ea, j=j: e.dma_start(
            out=s.agE_in[j * 64:(j + 1) * 64, :], in_=ea[0:64, :]),
            reads=ek, writes=[("agE_in", j)], dma=True)
        t.emit("gpsimd", lambda e, ecD=ecD, j=j: e.dma_start(
            out=s.agD_in[j * 128:(j + 1) * 128, :], in_=ecD),
            reads=[("ecD", cur)], writes=[("agD_in", j)], dma=True)


def chunk_unit(s, pb, j, c, cl, cur):
    t = s.t
    C = s.C; C2 = 2 * C; HD = s.cfg["HD"]
    a0 = cl * C
    up = s.cu_par
    s.cu_par = 1 - up
    so = 4 * up
    ko = 2 * up
    bo = 2 * up

    def w2bsl(off):
        return s.b_W2b[:, off + a0:off + a0 + C], rk("W2b", off + a0, off + a0 + C)

    lnp = s.b_W[:, a0:a0 + C]
    lnpk = rk("W", a0, a0 + C)
    sz = s.b_W[:, 2048 + a0:2048 + a0 + C]
    szk = rk("W", 2048 + a0, 2048 + a0 + C)
    lpp, lppk = st8sl(s, so + 0, C)
    t.emit("vector", lambda e: e.tensor_tensor(lpp, lnp, sz, ALU.subtract),
           reads=lnpk + szk, writes=lppk)
    invp, invpk = st8sl(s, so + 1, C)
    t.emit("scalar", lambda e: e.activation(out=invp, in_=lnp, func=ACT.Exp,
                                            scale=EXPF), reads=lnpk, writes=invpk)
    Pt, Ptk = st8sl(s, so + 2, C)
    t.emit("scalar", lambda e: e.activation(out=Pt, in_=lnp, func=ACT.Exp,
                                            scale=-EXPF), reads=lnpk, writes=Ptk)
    Ppv, Ppvk = st8sl(s, so + 3, C)
    t.emit("scalar", lambda e: e.activation(out=Ppv, in_=lpp, func=ACT.Exp,
                                            scale=-EXPF), reads=lppk, writes=Ppvk)
    pc_col = s.b_st8[:, (so + 2) * 64 + C - 1:(so + 2) * 64 + C]
    pc_keys = [("st8", so + 2)]

    for bdslot, off, sc, sck in [(12 + bo, 0, invp, invpk),
                                 (13 + bo, 1024, invp, invpk)]:
        ap, keys = w2bsl(off)
        for hh in range(2):
            t.emit("vector", lambda e, ap=ap, sc=sc, bdslot=bdslot, hh=hh:
                   e.tensor_tensor(
                       s.b_bd[hh * 64:(hh + 1) * 64,
                              bdslot * 128 + hh * C:bdslot * 128 + (hh + 1) * C],
                       ap[hh * 64:(hh + 1) * 64, :],
                       sc[hh * 64:(hh + 1) * 64, :], ALU.mult),
                   reads=keys + sck, writes=[("bd", bdslot)])
    for slot, off, sc, sck in [(2 + ko, 2048, Ppv, Ppvk),
                                (3 + ko, 3072, Pt, Ptk)]:
        ap, keys = w2bsl(off)
        kta, ktk = ktsl(s, slot, C)
        t.emit("vector", lambda e, kta=kta, ap=ap, sc=sc: e.tensor_tensor(
            kta, ap, sc, ALU.mult), reads=keys + sck, writes=ktk)
    vap, vkeys = w2bsl(4096)

    def trb(srcs):
        bank = s.pbt_i % 2
        s.pbt_i += 1
        p = s.psumb[bank]
        rp = [("psum", 6 + bank)]
        rd = []
        for i, (sap, skeys, hh) in enumerate(srcs):
            t.emit("tensor", lambda e, p=p, i=i, sap=sap, hh=hh: e.transpose(
                p[0:C, i * 64:i * 64 + 64], sap,
                s.b_cstIb[hh * 64:(hh + 1) * 64, hh * 64:(hh + 1) * 64]),
                reads=skeys + [("cstIb",)], writes=rp)
        return p, rp

    kapkt, kapktk = ktsl(s, 2 + ko, C)
    rbkt, rbktk = ktsl(s, 3 + ko, C)
    for bdslot, srcf, srck in [
            (9, lambda hh: kapkt[hh * 64:(hh + 1) * 64, 0:C], kapktk),
            (10, lambda hh: s.b_bd[hh * 64:(hh + 1) * 64,
                                   (12 + bo) * 128 + hh * C:(12 + bo) * 128 + (hh + 1) * C],
             [("bd", 12 + bo)]),
            (11, lambda hh: s.b_bd[hh * 64:(hh + 1) * 64,
                                   (13 + bo) * 128 + hh * C:(13 + bo) * 128 + (hh + 1) * C],
             [("bd", 13 + bo)])]:
        p, rp = trb([(srcf(hh), srck, hh) for hh in range(2)])
        for hh in range(2):
            t.emit("vector", lambda e, p=p, hh=hh, bdslot=bdslot: e.tensor_copy(
                s.b_bd[hh * C:hh * C + C, bdslot * 128 + hh * 64:bdslot * 128 + (hh + 1) * 64],
                p[0:C, hh * 64:(hh + 1) * 64]),
                reads=rp, writes=[("bd", bdslot)])
    p, rp = trb([(vap[hh * 64:(hh + 1) * 64, 0:C], vkeys, hh) for hh in range(2)])
    for hh in range(2):
        t.emit("vector", lambda e, p=p, hh=hh: e.tensor_copy(
            s.b_kt[hh * C:(hh + 1) * C, 7 * 64:8 * 64],
            p[0:C, hh * 64:(hh + 1) * 64]), reads=rp, writes=[("kt", 7)])
    vtv = s.b_kt[0:C2, 7 * 64:8 * 64]
    vtvk = [("kt", 7)]

    def trb_full(sap, skeys):
        bank = s.pbt_i % 2
        s.pbt_i += 1
        p = s.psumb[bank]
        rp = [("psum", 6 + bank)]
        t.emit("tensor", lambda e, p=p, sap=sap: e.transpose(
            p[0:C2, 0:C2], sap, s.b_cstIb[0:C2, 0:C2]),
            reads=skeys + [("cstIb",)], writes=rp)
        return p, rp

    def mmx(lhsT, lk, rhs, rhk, M, N):
        p, rp = pb.take()
        t.emit("tensor", lambda e, p=p: e.matmul(
            p[0:M, 0:N], lhsT, rhs, start=True, stop=True),
            reads=lk + rhk, writes=rp)
        return p, rp

    kbd = s.b_bd[:, (12 + bo) * 128:(12 + bo) * 128 + C2]
    kbdk = [("bd", 12 + bo)]
    bbd = s.b_bd[:, (13 + bo) * 128:(13 + bo) * 128 + C2]
    bbdk = [("bd", 13 + bo)]

    for bdslot, lhsTa, lk, rhsa, rhk, mask, maskk in [
            (0, kbd, kbdk, kapkt, kapktk, s.b_mstr, [("mstr",)]),
            (1, bbd, bbdk, kapkt, kapktk, s.b_mstr, [("mstr",)]),
            (6, kbd, kbdk, rbkt, rbktk, s.b_mtr, [("mtr",)]),
            (7, bbd, bbdk, rbkt, rbktk, s.b_mtr, [("mtr",)])]:
        p, rp = mmx(lhsTa, lk, rhsa, rhk, C2, C)
        for hh in range(2):
            t.emit("vector", lambda e, p=p, hh=hh, bdslot=bdslot, mask=mask:
                   e.tensor_tensor(
                       s.b_bd[hh * C:hh * C + C,
                              bdslot * 128 + hh * C:bdslot * 128 + (hh + 1) * C],
                       p[hh * C:hh * C + C, 0:C], mask[hh * C:hh * C + C, 0:C],
                       ALU.mult),
                   reads=rp + maskk, writes=[("bd", bdslot)])

    La = s.b_bd[0:C2, 1 * 128:1 * 128 + C2]
    Lk = [("bd", 1)]
    tso = {2: 2, 3: 3, 4: 4, 5: 5} if up == 0 else {2: 16, 3: 17, 4: 18, 5: 19}
    Tm = s.b_bd[0:C2, tso[2] * 128:tso[2] * 128 + C2]
    Tmk = [("bd", tso[2])]
    t.emit("vector", lambda e: e.tensor_tensor(
        Tm, s.b_cstIb[0:C2, 0:C2], La, ALU.subtract),
        reads=Lk + [("cstIb",)], writes=Tmk)
    Lp = s.b_bd[0:C2, tso[3] * 128:tso[3] * 128 + C2]
    Lpk = [("bd", tso[3])]
    t.emit("vector", lambda e: e.tensor_copy(Lp, La), reads=Lk, writes=Lpk)
    LpT = s.b_bd[0:C2, tso[4] * 128:tso[4] * 128 + C2]
    LpTk = [("bd", tso[4])]
    pb0, rpb0 = trb_full(Lp, Lpk)
    t.emit("vector", lambda e, pb0=pb0: e.tensor_copy(LpT, pb0[0:C2, 0:C2]),
           reads=rpb0, writes=LpTk)
    e_ = 2
    while e_ < C:
        p2, rp2 = mmx(LpT, LpTk, Lp, Lpk, C2, C2)
        t.emit("vector", lambda e, p2=p2: e.tensor_copy(Lp, p2[0:C2, 0:C2]),
               reads=rp2, writes=Lpk)
        pb3, rpb3 = trb_full(Lp, Lpk)
        t.emit("vector", lambda e, pb3=pb3: e.tensor_copy(LpT, pb3[0:C2, 0:C2]),
               reads=rpb3, writes=LpTk)
        p4, rp4 = mmx(LpT, LpTk, Tm, Tmk, C2, C2)
        t.emit("vector", lambda e, p4=p4: e.scalar_tensor_tensor(
            Tm, p4[0:C2, 0:C2], 1.0, Tm, ALU.mult, ALU.add),
            reads=rp4 + Tmk, writes=Tmk)
        e_ *= 2
    TTS = s.b_bd[0:C2, tso[5] * 128:tso[5] * 128 + C2]
    TTSk = [("bd", tso[5])]
    pb5, rpb5 = trb_full(Tm, Tmk)
    t.emit("vector", lambda e, pb5=pb5: e.tensor_copy(TTS, pb5[0:C2, 0:C2]),
           reads=rpb5, writes=TTSk)

    Mkk = s.b_bd[0:C2, 0:C2]
    Mkkk = [("bd", 0)]
    p6, rp6 = mmx(Mkk, Mkkk, vtv, vtvk, C2, 64)
    mkv, mkvk = ktsl(s, 8, 64)
    t.emit("vector", lambda e, p6=p6: e.tensor_copy(
        s.b_kt[0:C2, 8 * 64:9 * 64], p6[0:C2, 0:64]), reads=rp6, writes=mkvk)
    p7, rp7 = mmx(Tm, Tmk, s.b_kt[0:C2, 8 * 64:9 * 64], mkvk, C2, 64)
    t.emit("vector", lambda e, p7=p7: e.tensor_scalar(
        s.b_kt[0:C2, 9 * 64:10 * 64], p7[0:C2, 0:64], -1.0, None, ALU.mult),
        reads=rp7, writes=[("kt", 9)])
    oneg = s.b_kt[0:C2, 9 * 64:10 * 64]
    onegk = [("kt", 9)]

    LRK = s.b_bd[0:C2, 6 * 128:6 * 128 + C2]
    LRKk = [("bd", 6)]
    LRB = s.b_bd[0:C2, 7 * 128:7 * 128 + C2]
    LRBk = [("bd", 7)]
    p8, rp8 = pb.take()
    t.emit("tensor", lambda e, p8=p8: e.matmul(
        p8[0:C2, 0:64], LRK, vtv, start=True, stop=False),
        reads=LRKk + vtvk, writes=rp8)
    t.emit("tensor", lambda e, p8=p8: e.matmul(
        p8[0:C2, 0:64], LRB, oneg, start=False, stop=True),
        reads=LRBk + onegk, writes=rp8)
    yl, ylk = bW2b(s, 5120, 5120 + 64)
    t.emit("vector", lambda e, p8=p8, yl=yl: e.tensor_copy(yl[0:C2, :], p8[0:C2, 0:64]),
           reads=rp8, writes=ylk)
    t.emit("gpsimd", lambda e, yl=yl, j=j, c=c: e.dma_start(
        out=s.s_pack[(c * s.NPB + j) * 128:(c * s.NPB + j) * 128 + C2, 320:384],
        in_=yl[0:C2, :]),
        reads=ylk, writes=[("s_pack", j, c)], dma=True)

    bbtk = s.b_bd[0:C2, 11 * 128:12 * 128]
    bbtkk = [("bd", 11)]
    p9, rp9 = mmx(TTS, TTSk, bbtk, bbtkk, C2, 128)
    x1bk = [("kt", 10), ("kt", 11)]
    t.emit("vector", lambda e, p9=p9: e.tensor_copy(
        s.b_kt[0:C2, 10 * 64:12 * 64], p9[0:C2, 0:128]), reads=rp9, writes=x1bk)
    kaptkbd = s.b_bd[0:C2, 9 * 128:10 * 128]
    kaptkbdk = [("bd", 9)]
    p10, rp10 = mmx(kaptkbd, kaptkbdk, s.b_kt[0:C2, 10 * 64:12 * 64], x1bk, 128, 128)
    imjk2 = [("kt", 12), ("kt", 13)]
    t.emit("vector", lambda e, p10=p10: e.scalar_tensor_tensor(
        s.b_kt[:, 12 * 64:14 * 64], p10[:, 0:128], -1.0, s.b_cstI[:],
        ALU.mult, ALU.add),
        reads=rp10 + [("cstI",)], writes=imjk2)
    imjt = s.b_kt[:, 12 * 64:14 * 64]
    imjtk = imjk2
    t.emit("gpsimd", lambda e, j=j, c=c: e.dma_start(
        out=s.s_pack[(c * s.NPB + j) * 128:(c * s.NPB + j) * 128 + 128, 0:128],
        in_=imjt),
        reads=imjtk, writes=[("s_pack", j, c)], dma=True)

    p11, rp11 = mmx(TTS, TTSk, LRB, LRBk, C2, C2)
    wst, wstk = bdsl(s, 8, C2)
    t.emit("vector", lambda e, p11=p11: e.tensor_copy(
        s.b_bd[0:C2, 8 * 128:8 * 128 + C2], p11[0:C2, 0:C2]), reads=rp11, writes=wstk)
    p12, rp12 = mmx(kaptkbd, kaptkbdk, s.b_bd[0:C2, 8 * 128:8 * 128 + C2], wstk,
                    128, C2)
    rhatk = [("kt", 14), ("kt", 15)]
    for hh in range(2):
        t.emit("vector", lambda e, p12=p12, hh=hh: e.scalar_tensor_tensor(
            s.b_kt[hh * 64:(hh + 1) * 64, 14 * 64 + hh * C:14 * 64 + (hh + 1) * C],
            p12[hh * 64:(hh + 1) * 64, hh * C:(hh + 1) * C], -1.0,
            rbkt[hh * 64:(hh + 1) * 64, 0:C], ALU.mult, ALU.add),
            reads=rp12 + rbktk, writes=rhatk)
    t.emit("gpsimd", lambda e, j=j, c=c: e.dma_start(
        out=s.s_pack[(c * s.NPB + j) * 128:(c * s.NPB + j) * 128 + 128, 128:256],
        in_=s.b_kt[:, 14 * 64:16 * 64]),
        reads=rhatk, writes=[("s_pack", j, c)], dma=True)

    kbtkbd = s.b_bd[0:C2, 10 * 128:11 * 128]
    kbtkbdk = [("bd", 10)]
    p13, rp13 = pb.take()
    t.emit("tensor", lambda e, p13=p13: e.matmul(
        p13[:, 0:64], kbtkbd, vtv, start=True, stop=False),
        reads=kbtkbdk + vtvk, writes=rp13)
    t.emit("tensor", lambda e, p13=p13: e.matmul(
        p13[:, 0:64], bbtk, oneg, start=False, stop=True),
        reads=bbtkk + onegk, writes=rp13)
    dlt, dltk = ktsl(s, 6, 64)
    t.emit("vector", lambda e, p13=p13, dlt=dlt: e.tensor_scalar(
        dlt, p13[:, 0:64], pc_col, None, ALU.mult),
        reads=rp13 + pc_keys, writes=dltk)
    t.emit("gpsimd", lambda e, dlt=dlt, j=j, c=c: e.dma_start(
        out=s.s_pack[(c * s.NPB + j) * 128:(c * s.NPB + j) * 128 + 128, 256:320],
        in_=dlt),
        reads=dltk, writes=[("s_pack", j, c)], dma=True)

    nxt = 1 - cur
    ecE = s.b_ec[:, cur * 128:cur * 128 + 64]
    ecD = s.b_ec[:, cur * 128 + 64:cur * 128 + 128]
    ecEn = s.b_ec[:, nxt * 128:nxt * 128 + 64]
    ecDn = s.b_ec[:, nxt * 128 + 64:nxt * 128 + 128]
    p14, rp14 = mmx(imjt, imjtk, ecE, [("ecE", cur)], 128, 64)
    t.emit("vector", lambda e, p14=p14, ecEn=ecEn: e.tensor_scalar(
        ecEn, p14[:, 0:64], pc_col, None, ALU.mult),
        reads=rp14 + pc_keys, writes=[("ecE", nxt)])
    p15, rp15 = mmx(imjt, imjtk, ecD, [("ecD", cur)], 128, 64)
    t.emit("vector", lambda e, p15=p15, ecDn=ecDn, dlt=dlt: e.scalar_tensor_tensor(
        ecDn, p15[:, 0:64], pc_col, dlt, ALU.mult, ALU.add),
        reads=rp15 + pc_keys + dltk, writes=[("ecD", nxt)])
    return nxt


def ph_rwkv_post(s):
    t = s.t
    NPB = s.NPB; TC = s.TC; C = s.C; C2 = 2 * C; NCH = s.NCH; HD = s.cfg["HD"]
    NCR = s.NCR
    pb = Pb(s)
    t.emit("gpsimd", lambda e: e.collective_compute(
        "AllGather", ALU.bypass, replica_groups=[list(range(NCR))],
        ins=[s.agE_in.opt()], outs=[s.agE_out.opt()]),
        reads=[("agE_in", j) for j in range(NPB)], writes=[("agE_out",)], cc=True)
    t.emit("gpsimd", lambda e: e.collective_compute(
        "AllGather", ALU.bypass, replica_groups=[list(range(NCR))],
        ins=[s.agD_in.opt()], outs=[s.agD_out.opt()]),
        reads=[("agD_in", j) for j in range(NPB)], writes=[("agD_out",)], cc=True)
    t.emit("sync", lambda e: e.dma_start(out=s.b_pfx[:], in_=s.pfxm64),
           reads=[("pfxm64",)], writes=[("pfx",)], dma=True)
    t.emit("vector", lambda e: e.tensor_copy(s.b_bd[0:64, 0:64],
                                             s.b_cstI[0:64, 0:64]),
           reads=[("cstI",)], writes=[("bd", 0)])
    t.emit("vector", lambda e: e.tensor_copy(s.b_bd[64:128, 0:64],
                                             s.b_cstI[64:128, 64:128]),
           reads=[("cstI",)], writes=[("bd", 0)])
    isbs = s.b_bd[:, 0:64]
    isbsk = [("bd", 0)]
    t.emit("vector", lambda e: e.memset(s.b_ec[:], 0.0),
           writes=[("ecE", 0), ("ecE", 1), ("ecD", 0), ("ecD", 1)] +
           [("ecS", jj) for jj in range(NPB)])

    for j in range(NPB):
        if NCR == 1:
            t.emit("vector", lambda e, j=j: e.memset(
                s.b_ec[:, (2 + j) * 64:(3 + j) * 64], 0.0), writes=[("ecS", j)])
            continue
        t.emit("vector", lambda e: e.memset(s.b_ec[:, 0:64], 0.0),
               writes=[("ecE", 0)])
        scur = 0
        for r in range(NCR - 1):
            era = s.b_st8[:, 0:64]
            erk = [("st8", 0)]
            t.emit("sync", lambda e, r=r, j=j, era=era: e.dma_start(
                out=era,
                in_=s.agE_out[(r * NPB + j) * 64:(r * NPB + j) * 64 + 64, :]
                .rearrange("k (h kp) -> h k kp", h=2)),
                reads=[("agE_out",)], writes=erk, dma=True)
            dra = s.b_st8[:, 128:192]
            drk = [("st8", 2)]
            t.emit("sync", lambda e, r=r, j=j, dra=dra: e.dma_start(
                out=dra, in_=s.agD_out[(r * NPB + j) * 128:(r * NPB + j) * 128 + 128, :]),
                reads=[("agD_out",)], writes=drk, dma=True)
            t1a = s.b_st8[:, 192:256]
            t1k = [("st8", 3)]
            t.emit("vector", lambda e, era=era, t1a=t1a: e.tensor_tensor(
                t1a, era, isbs, ALU.subtract), reads=erk + isbsk, writes=t1k)
            eeff = s.b_bd[:, 128:192]
            eeffk = [("bd", 1)]
            t.emit("vector", lambda e, t1a=t1a, eeff=eeff, r=r: e.scalar_tensor_tensor(
                eeff, t1a, s.b_pfx[:, r:r + 1], isbs, ALU.mult, ALU.add),
                reads=t1k + isbsk + [("pfx",)], writes=eeffk)
            deff, deffk = ktsl(s, 14, 64)
            t.emit("vector", lambda e, dra=dra, deff=deff, r=r: e.tensor_scalar(
                deff, dra, s.b_pfx[:, r:r + 1], None, ALU.mult),
                reads=drk + [("pfx",)], writes=deffk)
            last = (r == NCR - 2)
            sTa = s.b_ec[:, scur * 64:scur * 64 + 64]
            if last:
                sTn = s.b_ec[:, (2 + j) * 64:(3 + j) * 64]
                snk = [("ecS", j)]
            else:
                snxt = 1 - scur
                sTn = s.b_ec[:, snxt * 64:snxt * 64 + 64]
                snk = [("ecE", snxt)]
            p, rp = pb.take()
            for hh in range(2):
                t.emit("tensor", lambda e, p=p, hh=hh, eeff=eeff, sTa=sTa: e.matmul(
                    p[0:64, hh * 64:(hh + 1) * 64],
                    eeff[hh * 64:(hh + 1) * 64, 0:64],
                    sTa[hh * 64:(hh + 1) * 64, :], start=True, stop=True),
                    reads=eeffk + [("ecE", scur)], writes=rp)
            stk, stkk = st8sl(s, 4, 64)
            for hh in range(2):
                t.emit("vector", lambda e, p=p, hh=hh, stk=stk: e.tensor_copy(
                    stk[hh * 64:(hh + 1) * 64, :], p[0:64, hh * 64:(hh + 1) * 64]),
                    reads=rp, writes=stkk)
            t.emit("vector", lambda e, stk=stk, sTn=sTn, deff=deff: e.tensor_tensor(
                sTn, stk, deff, ALU.add),
                reads=stkk + deffk, writes=snk)
            if not last:
                scur = 1 - scur
    for j in range(NPB):
        pcc = s.b_st8[:, 512 + j * NCH:512 + (j + 1) * NCH]
        lnpe = s.b_st8[:, 0:NCH]
        lnpek = [("st8", 0)]
        t.emit("sync", lambda e, j=j, lnpe=lnpe: e.dma_start(
            out=lnpe, in_=s.s_lnp[j * 128:(j + 1) * 128, :].rearrange(
                "p (n c) -> p n c", c=C)[:, :, C - 1],
            allow_slow_non_contiguous=True),
            reads=[("s_lnp", j)], writes=lnpek, dma=True)
        t.emit("scalar", lambda e, pcc=pcc, lnpe=lnpe: e.activation(
            out=pcc, in_=lnpe, func=ACT.Exp, scale=-EXPF),
            reads=lnpek, writes=[("st8pc", j)])
    GC = 512 // HD
    for c in range(NCH):
        cg = c % GC
        for j in range(NPB):
            par = j % 2
            kb = par * 512
            imjt = s.b_kt[:, kb:kb + 128]
            imjk = [("kt", 8 * par), ("kt", 8 * par + 1)]
            rhat = s.b_kt[:, kb + 128:kb + 256]
            rhk = [("kt", 8 * par + 2), ("kt", 8 * par + 3)]
            dlt = s.b_kt[:, kb + 256:kb + 320]
            dltk = [("kt", 8 * par + 4)]
            ylc = s.b_kt[:, kb + 320:kb + 384]
            ylck = [("kt", 8 * par + 5)]
            t.emit("sync", lambda e, j=j, c=c, kb=kb: e.dma_start(
                out=s.b_kt[:, kb:kb + 384],
                in_=s.s_pack[(c * NPB + j) * 128:(c * NPB + j) * 128 + 128, :]),
                reads=[("s_pack", j, c)],
                writes=imjk + rhk + dltk + ylck, dma=True)
            sT = s.b_ec[:, (2 + j) * 64:(3 + j) * 64]
            sTk = [("ecS", j)]
            p, rp = pb.take()
            t.emit("tensor", lambda e, p=p, rhat=rhat, sT=sT: e.matmul(
                p[0:C2, 0:64], rhat[:, 0:C2], sT, start=True, stop=True),
                reads=rhk + sTk, writes=rp)
            ya, yak = bW(s, j * 512 + cg * 64, j * 512 + cg * 64 + 64)
            t.emit("vector", lambda e, p=p, ya=ya, ylc=ylc: e.tensor_tensor(
                ya[0:C2, :], p[0:C2, 0:64], ylc[0:C2, :], ALU.add),
                reads=rp + ylck, writes=yak)
            p2, rp2 = pb.take()
            t.emit("tensor", lambda e, p2=p2, imjt=imjt, sT=sT: e.matmul(
                p2[:, 0:64], imjt, sT, start=True, stop=True),
                reads=imjk + sTk, writes=rp2)
            t.emit("vector", lambda e, p2=p2, sT=sT, dlt=dlt, c=c, j=j:
                   e.scalar_tensor_tensor(
                       sT, p2[:, 0:64], s.b_st8[:, 512 + j * NCH + c:512 + j * NCH + c + 1],
                       dlt, ALU.mult, ALU.add),
                   reads=rp2 + [("st8pc", j)] + dltk, writes=sTk)
            if cg == GC - 1 or c == NCH - 1:
                gn_group(s, pb, j, c - cg, cg + 1)

    load_bigw(s, s.wo, "wo")
    XW = s.XW
    for (t0, t1) in segs(TC, 512):
        W = t1 - t0
        for ks in range(NPB):
            t.emit("sync", lambda e, ks=ks, t0=t0, W=W: e.dma_start(
                out=s.b_W2b[:, ks * 512:ks * 512 + W],
                in_=s.s_yg[ks * 128:(ks + 1) * 128, t0:t0 + W]),
                reads=[("s_yg", ks)], writes=rk("W2b", ks * 512, ks * 512 + W),
                dma=True)
        outs = mm_proj(s, pb, lambda ks: bW2b(s, ks * 512, ks * 512 + W), None, W)
        for mj in range(NPB):
            p, rp = outs[mj]
            xa = s.b_x1[:, mj * XW + 1 + t0:mj * XW + 1 + t1]
            xkeys = rk("x1", mj * XW + 1 + t0, mj * XW + 1 + t1)
            t.emit("vector", lambda e, xa=xa, p=p, W=W: e.tensor_tensor(
                xa, xa, p[:, 0:W], ALU.add), reads=rp + xkeys, writes=xkeys)


def gn_group(s, pb, j, c0, gc):
    t = s.t
    C = s.C; C2 = 2 * C; HD = s.cfg["HD"]; TC = s.TC
    W = gc * HD
    par = j % 2
    ybase = j * 512
    gbase = s.NPB * 512 + par * 2048
    tb0 = par * 64
    yk = rk("W", ybase, ybase + 512)

    def tk(n):
        return [("tokg", par, n)]
    sq, sqk = bW(s, gbase, gbase + W)
    t.emit("scalar", lambda e, sq=sq: e.activation(
        out=sq[0:C2, :], in_=s.b_W[0:C2, ybase:ybase + W], func=ACT.Square),
        reads=yk, writes=sqk)
    su = s.b_tok[:, tb0 + 0:tb0 + gc]
    t.emit("vector", lambda e, gc=gc: e.tensor_reduce(
        su[0:C2, :], s.b_W[0:C2, ybase:ybase + W].rearrange("p (g v) -> p g v", v=HD),
        AX.X, ALU.add), reads=yk, writes=tk(0))
    s2 = s.b_tok[:, tb0 + 8:tb0 + 8 + gc]
    t.emit("vector", lambda e, sq=sq, gc=gc: e.tensor_reduce(
        s2[0:C2, :], sq[0:C2, :].rearrange("p (g v) -> p g v", v=HD),
        AX.X, ALU.add), reads=sqk, writes=tk(1))
    mu = s.b_tok[:, tb0 + 16:tb0 + 16 + gc]
    t.emit("vector", lambda e, gc=gc: e.tensor_scalar(
        mu[0:C2, :], su[0:C2, :], 1.0 / HD, None, ALU.mult),
        reads=tk(0), writes=tk(2))
    ex2 = s.b_tok[:, tb0 + 24:tb0 + 24 + gc]
    t.emit("vector", lambda e, gc=gc: e.tensor_scalar(
        ex2[0:C2, :], s2[0:C2, :], 1.0 / HD, None, ALU.mult),
        reads=tk(1), writes=tk(3))
    mu2 = s.b_tok[:, tb0 + 32:tb0 + 32 + gc]
    t.emit("vector", lambda e: e.tensor_tensor(
        mu2[0:C2, :], mu[0:C2, :], mu[0:C2, :], ALU.mult),
        reads=tk(2), writes=tk(4))
    t.emit("vector", lambda e: e.tensor_tensor(
        ex2[0:C2, :], ex2[0:C2, :], mu2[0:C2, :], ALU.subtract),
        reads=tk(3) + tk(4), writes=tk(3))
    t.emit("vector", lambda e: e.tensor_scalar(
        ex2[0:C2, :], ex2[0:C2, :], 0.00064, None, ALU.add),
        reads=tk(3), writes=tk(3))
    t.emit("vector", lambda e: e.reciprocal(mu2[0:C2, :], ex2[0:C2, :]),
           reads=tk(3), writes=tk(4))
    rstd = s.b_tok[:, tb0 + 40:tb0 + 40 + gc]
    t.emit("scalar", lambda e: e.activation(
        out=rstd[0:C2, :], in_=mu2[0:C2, :], func=ACT.Sqrt),
        reads=tk(4), writes=tk(5))
    nt = s.b_tok[:, tb0 + 48:tb0 + 48 + gc]
    t.emit("vector", lambda e: e.tensor_tensor(
        nt[0:C2, :], rstd[0:C2, :], rstd[0:C2, :], ALU.mult),
        reads=tk(5), writes=tk(6))
    t.emit("vector", lambda e: e.tensor_tensor(
        nt[0:C2, :], nt[0:C2, :], ex2[0:C2, :], ALU.mult),
        reads=tk(3) + tk(6), writes=tk(6))
    t.emit("vector", lambda e: e.tensor_scalar(
        nt[0:C2, :], nt[0:C2, :], -0.5, 1.5, ALU.mult, ALU.add),
        reads=tk(6), writes=tk(6))
    t.emit("vector", lambda e: e.tensor_tensor(
        rstd[0:C2, :], rstd[0:C2, :], nt[0:C2, :], ALU.mult),
        reads=tk(6) + tk(5), writes=tk(5))
    ms = s.b_tok[:, tb0 + 56:tb0 + 56 + gc]
    t.emit("vector", lambda e: e.tensor_tensor(
        ms[0:C2, :], mu[0:C2, :], rstd[0:C2, :], ALU.mult),
        reads=tk(2) + tk(5), writes=tk(7))
    t.emit("vector", lambda e: e.tensor_scalar(
        ms[0:C2, :], ms[0:C2, :], -1.0, None, ALU.mult),
        reads=tk(7), writes=tk(7))
    yg3 = s.b_W[0:C2, ybase:ybase + W].rearrange("p (g v) -> p g v", v=HD)
    ygn, ygnk = bW(s, gbase + 512, gbase + 512 + W)
    ygn3 = ygn[0:C2, :].rearrange("p (g v) -> p g v", v=HD)
    rstd3 = rstd[0:C2, :].unsqueeze(2).broadcast_to([C2, gc, HD])
    ms3 = ms[0:C2, :].unsqueeze(2).broadcast_to([C2, gc, HD])
    t.emit("vector", lambda e: e.tensor_tensor(ygn3, yg3, rstd3, ALU.mult),
           reads=yk + tk(5), writes=ygnk)
    t.emit("vector", lambda e: e.tensor_tensor(ygn3, ygn3, ms3, ALU.add),
           reads=ygnk + tk(7), writes=ygnk)
    yb2 = gbase + 1024
    yf, yfk = bW(s, yb2, yb2 + gc * C)
    for cl in range(gc):
        p, rp = pb.take()
        for hh in range(2):
            t.emit("tensor", lambda e, p=p, hh=hh, cl=cl: e.transpose(
                p[0:64, hh * C:(hh + 1) * C],
                ygn[hh * C:hh * C + C, cl * HD:cl * HD + 64],
                s.b_cstI[hh * C:hh * C + C, hh * C:hh * C + C]),
                reads=ygnk + [("cstI",)], writes=rp)
        for hh in range(2):
            t.emit("vector", lambda e, p=p, hh=hh, cl=cl: e.tensor_copy(
                s.b_W[hh * 64:(hh + 1) * 64, yb2 + cl * C:yb2 + (cl + 1) * C],
                p[0:64, hh * C:(hh + 1) * C]),
                reads=rp, writes=yfk)
    WT = gc * C
    t.emit("vector", lambda e, j=j: e.tensor_scalar(
        yf, yf, pvc(s, "vlnw", j), pvc(s, "vlnb", j), ALU.mult, ALU.add),
        reads=yfk + [("pv", "vlnw"), ("pv", "vlnb")], writes=yfk)
    t0tok = c0 * C
    tw0 = par * 512
    t.emit("sync", lambda e, j=j, t0tok=t0tok, WT=WT, tw0=tw0: e.dma_start(
        out=s.b_two[:, tw0:tw0 + WT], in_=s.s_srk[j * 2:(j + 1) * 2, t0tok:t0tok + WT]),
        reads=[("s_srk", j)], writes=[("twog", par)], dma=True)
    psr, rsr = pb.take()
    t.emit("tensor", lambda e, psr=psr, WT=WT, tw0=tw0: e.matmul(
        psr[:, 0:WT], s.hexp2_ap(), s.b_two[0:2, tw0:tw0 + WT], start=True, stop=True),
        reads=[("hexp2c",), ("twog", par)], writes=rsr)
    vt, vtk = bW2b(s, par * 512, par * 512 + WT)
    t.emit("sync", lambda e, j=j, t0tok=t0tok, WT=WT, vt=vt: e.dma_start(
        out=vt, in_=s.s_v[j * 128:(j + 1) * 128, t0tok:t0tok + WT]),
        reads=[("s_v", j)], writes=vtk, dma=True)
    tmp, tmpk = bW(s, gbase + 1536, gbase + 1536 + WT)
    t.emit("vector", lambda e, psr=psr, vt=vt, tmp=tmp, WT=WT: e.tensor_tensor(
        tmp, psr[:, 0:WT], vt, ALU.mult), reads=rsr + vtk, writes=tmpk)
    t.emit("vector", lambda e, tmp=tmp: e.tensor_tensor(
        yf, yf, tmp, ALU.add), reads=tmpk + yfk, writes=yfk)
    gt, gtk = bW2b(s, 1024 + par * 512, 1024 + par * 512 + WT)
    t.emit("sync", lambda e, j=j, t0tok=t0tok, WT=WT, gt=gt: e.dma_start(
        out=gt, in_=s.s_g[j * 128:(j + 1) * 128, t0tok:t0tok + WT]),
        reads=[("s_g", j)], writes=gtk, dma=True)
    ygb, ygbk = bW2b(s, 2048 + par * 512, 2048 + par * 512 + WT)
    t.emit("vector", lambda e, gt=gt, ygb=ygb: e.tensor_tensor(
        ygb, yf, gt, ALU.mult), reads=yfk + gtk, writes=ygbk)
    t.emit("gpsimd", lambda e, j=j, t0tok=t0tok, WT=WT, ygb=ygb: e.dma_start(
        out=s.s_yg[j * 128:(j + 1) * 128, t0tok:t0tok + WT], in_=ygb),
        reads=ygbk, writes=[("s_yg", j)], dma=True)

def ph_ln1(s):
    t = s.t
    NPB = s.NPB; TC = s.TC; XW = s.XW; NCR = s.NCR
    pb = Pb(s)
    ln_channels(s, srcb="x1", dstb="xn0", eps=1e-5)
    for j in range(NPB):
        for (src_c, dst_c) in [(1, 0), (TC - 2, 3)]:
            t.emit("gpsimd", lambda e, j=j, src_c=src_c, dst_c=dst_c: e.dma_start(
                out=s.agH_in[j * 128:(j + 1) * 128, dst_c:dst_c + 3],
                in_=s.b_xn0[:, j * XW + src_c:j * XW + src_c + 3]),
                reads=rk("xn0", j * XW, (j + 1) * XW), writes=[("agH_in", j)],
                dma=True)
    t.emit("gpsimd", lambda e: e.collective_compute(
        "AllGather", ALU.bypass, replica_groups=[list(range(NCR))],
        ins=[s.agH_in.opt()], outs=[s.agH_out.opt()]),
        reads=[("agH_in", j) for j in range(NPB)], writes=[("agH_out",)], cc=True)
    for j in range(NPB):
        t.emit("gpsimd", lambda e, j=j: e.dma_start(
            out=s.s_xn1[j * 128:(j + 1) * 128, 3:TC + 3],
            in_=s.b_xn0[:, j * XW + 1:j * XW + 1 + TC]),
            reads=rk("xn0", j * XW, (j + 1) * XW), writes=[("s_xn1", j)], dma=True)
    for j in range(NPB):
        ha, hk = bW(s, 0, 8)
        t.emit("vector", lambda e, ha=ha: e.memset(ha[:, 0:6], 0.0), writes=hk)
        for r in range(NCR):
            ga, gk = bW(s, 512, 512 + 6)
            t.emit("sync", lambda e, r=r, j=j, ga=ga: e.dma_start(
                out=ga, in_=s.agH_out[r * s.D + j * 128:r * s.D + (j + 1) * 128, :]),
                reads=[("agH_out",)], writes=gk, dma=True)
            t.emit("vector", lambda e, ha=ha, ga=ga, r=r: e.scalar_tensor_tensor(
                ha[:, 0:3], ga[:, 3:6], s.b_msk[:, r:r + 1], ha[:, 0:3], ALU.mult, ALU.add),
                reads=gk + hk + [("mskc",)], writes=hk)
            t.emit("vector", lambda e, ha=ha, ga=ga, r=r: e.scalar_tensor_tensor(
                ha[:, 3:6], ga[:, 0:3], s.b_msk[:, NCR + r:NCR + r + 1], ha[:, 3:6],
                ALU.mult, ALU.add),
                reads=gk + hk + [("mskc",)], writes=hk)
        t.emit("gpsimd", lambda e, j=j, ha=ha: e.dma_start(
            out=s.s_xn1[j * 128:(j + 1) * 128, 0:3], in_=ha[:, 0:3]),
            reads=hk, writes=[("s_xn1", j)], dma=True)
        t.emit("gpsimd", lambda e, j=j, ha=ha: e.dma_start(
            out=s.s_xn1[j * 128:(j + 1) * 128, TC + 3:TC + 6], in_=ha[:, 3:6]),
            reads=hk, writes=[("s_xn1", j)], dma=True)


def ph_mamba_prep(s):
    t = s.t
    NPB = s.NPB; TC = s.TC; C2 = s.cfg["C2"]; NCR = s.NCR; DSL = s.DSL
    NQ = C2 // 128
    DTR = s.cfg["DTR"]
    pb = Pb(s)
    t.emit("gpsimd", lambda e: e.dma_start(
        out=s.b_wt[:, 0:NQ * 96],
        in_=s.xprojt.rearrange("(q p) m -> p q m", p=128)),
        reads=[("xprojt",)], writes=[("wt", 0)], dma=True)
    t.emit("gpsimd", lambda e: e.dma_start(
        out=s.b_wt[:, 4096:4096 + s.D][0:DTR, :], in_=s.dtprojt),
        reads=[("dtprojt",)], writes=[("wt", 4)], dma=True)

    for dire in range(2):
        for q in range(NQ):
            xe, xek = bW2b(s, 0, TC + 6)
            t.emit("sync", lambda e, q=q, xe=xe: e.dma_start(
                out=xe, in_=s.s_xn1[q * 64:(q + 1) * 64, :]
                .unsqueeze(1).broadcast_to([64, 2, TC + 6])),
                reads=[("s_xn1", q // 2)], writes=xek, dma=True)
            for (a, b) in segs(TC, 512):
                W = b - a
                cva, cvk = bW(s, 0, W)
                for jj in range(4):
                    off = (a + jj) if dire == 0 else (a + 6 - jj)
                    if jj == 0:
                        t.emit("vector", lambda e, cva=cva, xe=xe, off=off, q=q, jj=jj, W=W:
                               e.tensor_scalar(cva, xe[:, off:off + W],
                                               pvc(s, "cw1", 4 * q + jj), None, ALU.mult),
                               reads=xek + [("pv", "cw1")], writes=cvk)
                    else:
                        t.emit("vector", lambda e, cva=cva, xe=xe, off=off, q=q, jj=jj, W=W:
                               e.scalar_tensor_tensor(cva, xe[:, off:off + W],
                                                      pvc(s, "cw1", 4 * q + jj), cva,
                                                      ALU.mult, ALU.add),
                               reads=xek + cvk + [("pv", "cw1")], writes=cvk)
                xca, xck = bW2b(s, 4096, 4096 + W)
                t.emit("scalar", lambda e, cva=cva, xca=xca, q=q: e.activation(
                    out=xca, in_=cva, func=ACT.Silu, bias=pvc(s, "cb1", q), scale=1.0),
                    reads=cvk + [("pv", "cb1")], writes=xck)
                t.emit("gpsimd", lambda e, xca=xca, q=q, a=a, W=W, dire=dire: e.dma_start(
                    out=s.s_xc[dire * C2 + q * 128:dire * C2 + (q + 1) * 128, a:a + W],
                    in_=xca),
                    reads=xck, writes=[("s_xc", dire, q)], dma=True)
                pxm, rxm = pb.take()
                t.emit("tensor", lambda e, pxm=pxm, xca=xca, q=q, W=W: e.matmul(
                    pxm[(q % 2) * 64:(q % 2) * 64 + 64, 0:W], s.b_ps64[:, 0:64],
                    xca, start=True, stop=True),
                    reads=[("ps64c",)] + xck, writes=rxm)
                dj = q // 2
                t.emit("vector", lambda e, pxm=pxm, q=q, dj=dj, a=a, W=W: e.tensor_copy(
                    s.b_W2[(q % 2) * 64:(q % 2) * 64 + 64, dj * TC + a:dj * TC + a + W],
                    pxm[(q % 2) * 64:(q % 2) * 64 + 64, 0:W]),
                    reads=rxm, writes=rk("W2", dj * TC + a, dj * TC + a + W))
        NRB = 128 // DSL
        for dj in range(NPB):
            t.emit("sync", lambda e, dj=dj, dire=dire: e.dma_start(
                out=s.a2aM_in.rearrange("(r f x) c -> r f x c", f=4, x=DSL)
                [dj * 128 // DSL:(dj * 128 + 128) // DSL, dire * 2 + 1, :, :],
                in_=s.b_W2[:, dj * TC:(dj + 1) * TC]),
                reads=rk("W2", dj * TC, (dj + 1) * TC),
                writes=[("a2aM_in", dire, dj, "xm")], dma=True)
        for (a, b) in segs(TC, 512):
            W = b - a
            pxd, rxd = pb.take()
            for q in range(NQ):
                xca, xck = bW2b(s, 0, W)
                t.emit("sync", lambda e, q=q, a=a, W=W, dire=dire, xca=xca: e.dma_start(
                    out=xca, in_=s.s_xc[dire * C2 + q * 128:dire * C2 + (q + 1) * 128,
                                        a:a + W]),
                    reads=[("s_xc", dire, q)], writes=xck, dma=True)
                t.emit("tensor", lambda e, pxd=pxd, q=q, xca=xca, W=W: e.matmul(
                    pxd[0:96, 0:W], s.b_wt[:, q * 96:(q + 1) * 96], xca,
                    start=(q == 0), stop=(q == NQ - 1)),
                    reads=[("wt", 0)] + xck, writes=rxd)
            xdb, xdbk = bW(s, 1024, 1024 + W)
            t.emit("vector", lambda e, pxd=pxd, xdb=xdb, W=W: e.tensor_copy(
                xdb[0:96, :], pxd[0:96, 0:W]), reads=rxd, writes=xdbk)
            bca, bck = bW2b(s, 0, 2 * W)
            t.emit("vector", lambda e, xdb=xdb, bca=bca, W=W: e.tensor_copy(
                bca[0:32, 0:W], xdb[64:96, 0:W]), reads=xdbk, writes=bck)
            t.emit("gpsimd", lambda e, bca=bca, a=a, W=W, dire=dire: e.dma_start(
                out=s.agBC_in[dire * 32:dire * 32 + 32, a:a + W], in_=bca[0:32, 0:W]),
                reads=bck, writes=[("agBC_in", dire)], dma=True)
            dtp, dtpk = bW2b(s, 2048, 2048 + W)
            t.emit("vector", lambda e, xdb=xdb, dtp=dtp, W=W: e.tensor_copy(
                dtp[0:DTR, :], xdb[0:DTR, 0:W]), reads=xdbk, writes=dtpk)
            for mj in range(NPB):
                pdt, rdt = pb.take()
                t.emit("tensor", lambda e, pdt=pdt, mj=mj, dtp=dtp, W=W: e.matmul(
                    pdt[:, 0:W], s.b_wt[:, 4096 + mj * 128:4096 + (mj + 1) * 128][0:DTR, :],
                    dtp[0:DTR, :], start=True, stop=True),
                    reads=[("wt", 4)] + dtpk, writes=rdt)
                za, zk = bW(s, 2048, 2048 + W)
                t.emit("vector", lambda e, pdt=pdt, za=za, mj=mj, W=W: e.tensor_scalar(
                    za, pdt[:, 0:W], pvc(s, "vdtb", mj), None, ALU.add),
                    reads=rdt + [("pv", "vdtb")], writes=zk)
                aa, ak = bW(s, 2560, 2560 + W)
                t.emit("scalar", lambda e, za=za, aa=aa, W=W: e.activation(
                    out=aa, in_=za, func=ACT.Abs), reads=zk, writes=ak)
                t.emit("scalar", lambda e, aa=aa, W=W: e.activation(
                    out=aa, in_=aa, func=ACT.Exp, scale=-1.0), reads=ak, writes=ak)
                t.emit("scalar", lambda e, aa=aa, W=W: e.activation(
                    out=aa, in_=aa, func=ACT.Ln, bias=1.0, scale=1.0),
                    reads=ak, writes=ak)
                t.emit("vector", lambda e, za=za, W=W: e.tensor_scalar(
                    za, za, 0.0, None, ALU.max), reads=zk, writes=zk)
                dta, dtk = bW2b(s, 4096, 4096 + W)
                t.emit("vector", lambda e, dta=dta, za=za, aa=aa: e.tensor_tensor(
                    dta, za, aa, ALU.add), reads=zk + ak, writes=dtk)
                t.emit("gpsimd", lambda e, dta=dta, mj=mj, a=a, W=W, dire=dire: e.dma_start(
                    out=s.a2aM_in.rearrange("(r f x) c -> r f x c", f=4, x=DSL)
                    [mj * 128 // DSL:(mj * 128 + 128) // DSL, dire * 2, :, a:a + W],
                    in_=dta),
                    reads=dtk, writes=[("a2aM_in", dire, mj, "dt")], dma=True)
    from concourse.aot_env import allow_ici_all_to_all
    with allow_ici_all_to_all():
        t.emit("gpsimd", lambda e: e.collective_compute(
            "AllToAll", ALU.bypass, replica_groups=[list(range(NCR))],
            ins=[s.a2aM_in.opt()], outs=[s.a2aM_out.opt()]),
            reads=[("a2aM_in", d, j, q) for d in range(2) for j in range(NPB)
                   for q in ("dt", "xm")],
            writes=[("a2aM_out",)], cc=True)
    t.emit("gpsimd", lambda e: e.collective_compute(
        "AllGather", ALU.bypass, replica_groups=[list(range(NCR))],
        ins=[s.agBC_in.opt()], outs=[s.agBC_out.opt()]),
        reads=[("agBC_in", 0), ("agBC_in", 1)], writes=[("agBC_out",)], cc=True)


def revlast(ap):
    lst = ap.ap
    step, cnt = lst[-1]
    ap.ap = lst[:-1] + [[-step, cnt]]
    ap.offset = ap.offset + (cnt - 1) * step
    return ap


def ph_scan(s):
    t = s.t
    TC = s.TC; NCR = s.NCR; DSL = s.DSL; NST = s.cfg["NST"]
    TF = NCR * TC
    SEG = min(2048, TF)
    NSEG = TF // SEG
    NTI = DSL * NST // 128
    pb = Pb(s)
    t.emit("scalar", lambda e: e.activation(
        out=s.b_st8[:, 0:NTI], in_=s.b_alog[:, 0:NTI], func=ACT.Exp),
        reads=[("alogc",)], writes=[("st8", 0)])
    t.emit("vector", lambda e: e.tensor_scalar(
        s.b_st8[:, 0:NTI], s.b_st8[:, 0:NTI], -1.0, None, ALU.mult),
        reads=[("st8", 0)], writes=[("st8", 0)])
    a2v = s.a2aM_out.rearrange("(r f x) c -> r f x c", f=4, x=DSL)
    s.dscr("s_rev", (4 * DSL + 64, TF), BF16)
    for dire in range(2):
        for (g0, g1) in segs(TF, SEG):
            W = g1 - g0
            r0 = g0 // TC
            nrb = max(W // TC, 1)
            dtb, dtbk = bW2b(s, 0, W)
            t.emit("sync", lambda e, dire=dire, r0=r0, nrb=nrb, dtb=dtb: e.dma_start(
                out=dtb[0:DSL, :], in_=a2v[r0:r0 + nrb, dire * 2, 0:DSL, :]),
                reads=[("a2aM_out",)], writes=dtbk, dma=True)
            xmb, xmbk = bW2b(s, SEG, SEG + W)
            t.emit("sync", lambda e, dire=dire, r0=r0, nrb=nrb, xmb=xmb: e.dma_start(
                out=xmb[0:DSL, :], in_=a2v[r0:r0 + nrb, dire * 2 + 1, 0:DSL, :]),
                reads=[("a2aM_out",)], writes=xmbk, dma=True)
            ub = s.b_wt[:, 0:W]
            ubk = [("wt", 0)]
            t.emit("vector", lambda e, ub=ub, dtb=dtb, xmb=xmb: e.tensor_tensor(
                ub[0:DSL, :], dtb[0:DSL, :], xmb[0:DSL, :], ALU.mult),
                reads=dtbk + xmbk, writes=ubk)
            t.emit("gpsimd", lambda e, ub=ub, dire=dire, g0=g0, W=W: e.dma_start(
                out=s.s_u[dire * DSL:dire * DSL + DSL, g0:g0 + W], in_=ub[0:DSL, :]),
                reads=ubk, writes=[("s_u", dire)], dma=True)
            if dire == 1:
                m0 = TF - g0 - W
                rb = s.b_wt[:, SEG:SEG + W]
                rbk = [("wt", 4)]
                t.emit("vector", lambda e, rb=rb, dtb=dtb: e.tensor_copy(
                    rb[0:DSL, :], revlast(dtb[0:DSL, :])), reads=dtbk, writes=rbk)
                t.emit("gpsimd", lambda e, rb=rb, m0=m0, W=W: e.dma_start(
                    out=s.s_rev[0:DSL, m0:m0 + W], in_=rb[0:DSL, :]),
                    reads=rbk, writes=[("s_rev", 0)], dma=True)
                t.emit("vector", lambda e, rb=rb, ub=ub: e.tensor_copy(
                    rb[0:DSL, :], revlast(ub[0:DSL, :])), reads=ubk, writes=rbk)
                t.emit("gpsimd", lambda e, rb=rb, m0=m0, W=W: e.dma_start(
                    out=s.s_rev[DSL:2 * DSL, m0:m0 + W], in_=rb[0:DSL, :]),
                    reads=rbk, writes=[("s_rev", 1)], dma=True)
            ykeys = [("W2y", ti_, blk) for ti_ in range(NTI)
                     for blk in range(g0 // 512, (g0 + W + 511) // 512)]
            if dire == 0:
                t.emit("vector", lambda e, xmb=xmb, g0=g0, W=W: e.tensor_scalar(
                    s.b_W2[0:DSL, g0:g0 + W], xmb[0:DSL, :], pvc(s, "vD")[0:DSL, :], None,
                    ALU.mult),
                    reads=xmbk + [("pv", "vD")], writes=ykeys)
            else:
                t.emit("vector", lambda e, xmb=xmb, g0=g0, W=W: e.scalar_tensor_tensor(
                    s.b_W2[0:DSL, g0:g0 + W], xmb[0:DSL, :], pvc(s, "vD")[0:DSL, :],
                    s.b_W2[0:DSL, g0:g0 + W], ALU.mult, ALU.add),
                    reads=xmbk + [("pv", "vD")] + ykeys,
                    writes=ykeys)
    bcv = s.agBC_out.rearrange("(r u) c -> r u c", u=64)
    for ui, uoff in [(0, 32), (1, 48)]:
        for (g0, g1) in segs(TF, SEG):
            W = g1 - g0
            r0 = g0 // TC
            nrb = max(W // TC, 1)
            m0 = TF - g0 - W
            ca, ck = bW2b(s, 0, W)
            t.emit("sync", lambda e, ca=ca, uoff=uoff, r0=r0, nrb=nrb: e.dma_start(
                out=ca[0:16, :], in_=bcv[r0:r0 + nrb, uoff:uoff + 16, :]
                .rearrange("r u c -> u r c")),
                reads=[("agBC_out",)], writes=ck, dma=True)
            cb, cbk = bW2b(s, SEG, SEG + W)
            t.emit("vector", lambda e, ca=ca, cb=cb: e.tensor_copy(
                cb[0:16, :], revlast(ca[0:16, :])), reads=ck, writes=cbk)
            t.emit("gpsimd", lambda e, cb=cb, m0=m0, W=W, ui=ui: e.dma_start(
                out=s.s_rev[4 * DSL + ui * 16:4 * DSL + (ui + 1) * 16, m0:m0 + W],
                in_=cb[0:16, :]),
                reads=cbk, writes=[("s_rev", 2 + ui)], dma=True)
    su = s.s_u
    srv = s.s_rev
    for dire in range(2):
        t.emit("vector", lambda e, dire=dire: e.memset(s.b_st8[:, 64:64 + NTI], 0.0),
               writes=[("st8c", ti_) for ti_ in range(NTI)])
        for si in range(NSEG):
            g0 = si * SEG
            W = SEG
            r0 = g0 // TC
            nrb = max(W // TC, 1)
            brep, brepk = bW2b(s, 0, W)
            crep, crepk = bW2b(s, SEG, SEG + W)
            if dire == 0:
                for (dst, dstk, uoff) in [(brep, brepk, 0), (crep, crepk, 16)]:
                    for rb in range(nrb):
                        t.emit("sync", lambda e, dst=dst, uoff=uoff, r0=r0, rb=rb:
                               e.dma_start(out=dst[:, rb * TC:(rb + 1) * TC],
                                           in_=bcv[r0 + rb, uoff:uoff + 16, :]
                                           .unsqueeze(0).broadcast_to([8, 16, TC])),
                               reads=[("agBC_out",)], writes=dstk, dma=True)
            else:
                for (dst, dstk, ui) in [(brep, brepk, 0), (crep, crepk, 1)]:
                    t.emit("sync", lambda e, dst=dst, ui=ui, g0=g0, W=W:
                           e.dma_start(out=dst, in_=srv[4 * DSL + ui * 16:
                                                        4 * DSL + (ui + 1) * 16,
                                                        g0:g0 + W]
                                       .unsqueeze(0).broadcast_to([8, 16, W])),
                           reads=[("s_rev", 2 + ui)], writes=dstk, dma=True)
            for ti in range(NTI):
                d0 = ti * 8
                pp = ti % 2
                dtr2 = s.b_wt[:, pp * 4096:pp * 4096 + W]
                dtr2k = [("wt", 4 * pp), ("wt", 4 * pp + 1)]
                urep = s.b_wt[:, pp * 4096 + 2048:pp * 4096 + 2048 + W]
                urepk = [("wt", 4 * pp + 2), ("wt", 4 * pp + 3)]
                if dire == 0:
                    for rb in range(nrb):
                        t.emit("sync", lambda e, r0=r0, rb=rb, d0=d0, dtr2=dtr2:
                               e.dma_start(out=dtr2[:, rb * TC:(rb + 1) * TC],
                                           in_=a2v[r0 + rb, 0, d0:d0 + 8, :]
                                           .unsqueeze(1).broadcast_to([8, 16, TC])),
                               reads=[("a2aM_out",)], writes=dtr2k, dma=True)
                    t.emit("sync", lambda e, g0=g0, d0=d0, urep=urep, W=W:
                           e.dma_start(out=urep, in_=su[d0:d0 + 8, g0:g0 + W]
                                       .unsqueeze(1).broadcast_to([8, 16, W])),
                           reads=[("s_u", 0)], writes=urepk, dma=True)
                else:
                    t.emit("sync", lambda e, g0=g0, d0=d0, dtr2=dtr2, W=W:
                           e.dma_start(out=dtr2, in_=srv[d0:d0 + 8, g0:g0 + W]
                                       .unsqueeze(1).broadcast_to([8, 16, W])),
                           reads=[("s_rev", 0)], writes=dtr2k, dma=True)
                    t.emit("sync", lambda e, g0=g0, d0=d0, urep=urep, W=W:
                           e.dma_start(out=urep, in_=srv[DSL + d0:DSL + d0 + 8, g0:g0 + W]
                                       .unsqueeze(1).broadcast_to([8, 16, W])),
                           reads=[("s_rev", 1)], writes=urepk, dma=True)
                dca, dck = bW(s, pp * 4096, pp * 4096 + W)
                t.emit("scalar", lambda e, dca=dca, dtr2=dtr2, ti=ti: e.activation(
                    out=dca, in_=dtr2, func=ACT.Exp, scale=s.b_st8[:, ti:ti + 1]),
                    reads=dtr2k + [("st8", 0)], writes=dck)
                t.emit("vector", lambda e, urep=urep, brep=brep: e.tensor_tensor(
                    urep, urep, brep, ALU.mult), reads=urepk + brepk, writes=urepk)
                ha, hk = bW(s, pp * 4096 + 2048, pp * 4096 + 2048 + W)
                t.emit("vector", lambda e, ha=ha, dca=dca, urep=urep, ti=ti:
                       e.tensor_tensor_scan(ha, dca, urep,
                                            s.b_st8[:, 64 + ti:64 + ti + 1],
                                            ALU.mult, ALU.add),
                       reads=dck + urepk + [("st8c", ti)], writes=hk)
                t.emit("vector", lambda e, ha=ha, ti=ti, W=W: e.tensor_copy(
                    s.b_st8[:, 64 + ti:64 + ti + 1], ha[:, W - 1:W]),
                    reads=hk, writes=[("st8c", ti)])
                t.emit("vector", lambda e, dtr2=dtr2, ha=ha, crep=crep: e.tensor_tensor(
                    dtr2, ha, crep, ALU.mult), reads=hk + crepk, writes=dtr2k)
                for (w0, w1) in segs(W, 512):
                    pr, rr = pb.take()
                    t.emit("tensor", lambda e, pr=pr, dtr2=dtr2, w0=w0, w1=w1: e.matmul(
                        pr[0:8, 0:w1 - w0], s.b_ps64[:, 64:72], dtr2[:, w0:w1],
                        start=True, stop=True),
                        reads=[("ps64c",)] + dtr2k, writes=rr)
                    wb = (w0 // 512) % 4
                    ysm = s.b_W2b[:, 4096 + wb * 512:4096 + wb * 512 + 512]
                    ysmk = [("ysm", wb)]
                    t.emit("scalar", lambda e, pr=pr, ysm=ysm, w0=w0, w1=w1: e.copy(
                        ysm[0:8, 0:w1 - w0], pr[0:8, 0:w1 - w0]), reads=rr, writes=ysmk)
                    if dire == 0:
                        t.emit("gpsimd", lambda e, ysm=ysm, d0=d0, g0=g0, w0=w0, w1=w1:
                               e.dma_start(out=s.b_W2[d0:d0 + 8, g0 + w0:g0 + w1],
                                           in_=ysm[0:8, 0:w1 - w0], accum_op=ALU.add),
                               reads=ysmk + [("W2y", ti, (g0 + w0) // 512)],
                               writes=[("W2y", ti, (g0 + w0) // 512)], dma=True)
                    else:
                        t.emit("gpsimd", lambda e, ysm=ysm, d0=d0, g0=g0, w0=w0, w1=w1:
                               e.dma_start(out=s.s_ybr[d0:d0 + 8, g0 + w0:g0 + w1],
                                           in_=ysm[0:8, 0:w1 - w0]),
                               reads=ysmk, writes=[("s_ybr", d0, g0 + w0)], dma=True)
    for (g0, g1) in segs(TF, SEG):
        W = g1 - g0
        m0 = TF - g0 - W
        ca, ck = bW2b(s, 0, W)
        t.emit("sync", lambda e, ca=ca, g0=g0, W=W: e.dma_start(
            out=ca[0:DSL, :], in_=s.s_ybr[0:DSL, g0:g0 + W]),
            reads=[("s_ybr", d0_, w0_) for d0_ in range(0, DSL, 8)
                   for w0_ in range(g0, g0 + W, 512)], writes=ck, dma=True)
        ykeys2 = [("W2y", ti_, blk) for ti_ in range(NTI)
                  for blk in range(m0 // 512, (m0 + W + 511) // 512)]
        t.emit("vector", lambda e, ca=ca, m0=m0, W=W: e.tensor_tensor(
            s.b_W2[0:DSL, m0:m0 + W], s.b_W2[0:DSL, m0:m0 + W],
            revlast(ca[0:DSL, :]), ALU.add),
            reads=ck + ykeys2, writes=ykeys2)
    y2 = s.a2aY_in.rearrange("(r x) c -> r x c", x=DSL)
    for r in range(NCR):
        t.emit("gpsimd", lambda e, r=r: e.dma_start(
            out=y2[r, 0:DSL, :], in_=s.b_W2[0:DSL, r * TC:(r + 1) * TC]),
            reads=[("W2y", ti_, blk) for ti_ in range(NTI)
                   for blk in range(r * TC // 512, ((r + 1) * TC + 511) // 512)],
            writes=[("a2aY_in",)], dma=True)
    from concourse.aot_env import allow_ici_all_to_all
    with allow_ici_all_to_all():
        t.emit("gpsimd", lambda e: e.collective_compute(
            "AllToAll", ALU.bypass, replica_groups=[list(range(NCR))],
            ins=[s.a2aY_in.opt()], outs=[s.a2aY_out.opt()]),
            reads=[("a2aY_in",)], writes=[("a2aY_out",)], cc=True)


def ph_ffn1_final(s):
    t = s.t
    NPB = s.NPB; TC = s.TC; XW = s.XW; DQ = s.cfg["DQ"]; D = s.D
    NQB = DQ // 128
    pb = Pb(s)
    for og in range(NQB):
        for wi, wap, wofs in [(0, s.qg, 0), (1, s.qu, 4096)]:
            t.emit("gpsimd", lambda e, wap=wap, wofs=wofs, og=og: e.dma_start(
                out=s.b_wt[:, wofs:wofs + NPB * 128],
                in_=wap[:, og * 128:(og + 1) * 128]
                .rearrange("(ks p) m -> p ks m", p=128)),
                reads=[("qg",) if wofs == 0 else ("qu",)],
                writes=[("wt", 0 if wofs == 0 else 4)], dma=True)
        for (a, b) in segs(TC, 512):
            W = b - a
            pg, rg = pb.take()
            pu, ru = pb.take()
            for ks in range(NPB):
                t.emit("tensor", lambda e, pg=pg, ks=ks, a=a, W=W: e.matmul(
                    pg[:, 0:W], s.b_wt[:, ks * 128:ks * 128 + 128],
                    s.b_xn0[:, ks * XW + 1 + a:ks * XW + 1 + a + W],
                    start=(ks == 0), stop=(ks == NPB - 1)),
                    reads=[("wt", 0)] + rk("xn0", ks * XW, (ks + 1) * XW), writes=rg)
            for ks in range(NPB):
                t.emit("tensor", lambda e, pu=pu, ks=ks, a=a, W=W: e.matmul(
                    pu[:, 0:W], s.b_wt[:, 4096 + ks * 128:4096 + ks * 128 + 128],
                    s.b_xn0[:, ks * XW + 1 + a:ks * XW + 1 + a + W],
                    start=(ks == 0), stop=(ks == NPB - 1)),
                    reads=[("wt", 4)] + rk("xn0", ks * XW, (ks + 1) * XW), writes=ru)
            sg, sgk = bW(s, 0, W)
            t.emit("scalar", lambda e, sg=sg, pg=pg, W=W: e.activation(
                out=sg, in_=pg[:, 0:W], func=ACT.Silu, scale=1.0 / 127.0),
                reads=rg, writes=sgk)
            hq, hqk = bW2b(s, 0, W)
            t.emit("vector", lambda e, hq=hq, sg=sg, pu=pu, W=W: e.tensor_tensor(
                hq, sg, pu[:, 0:W], ALU.mult), reads=sgk + ru, writes=hqk)
            t.emit("gpsimd", lambda e, hq=hq, og=og, a=a, W=W: e.dma_start(
                out=s.s_hq[og * 128:(og + 1) * 128, a:a + W], in_=hq),
                reads=hqk, writes=[("s_hq", og)], dma=True)
    for (a, b) in segs(TC, 512):
        W = b - a
        pouts = [(s.psum[i], [("psum", i)]) for i in range(NPB)]
        for kg in range(NQB):
            t.emit("gpsimd", lambda e, kg=kg: e.dma_start(
                out=s.b_wt[:, (kg % 4) * 1024:(kg % 4) * 1024 + D],
                in_=s.qd[kg * 128:(kg + 1) * 128, :]),
                reads=[("qd",)], writes=[("wt", kg % 4)], dma=True)
            hq, hqk = bW2b(s, (kg % 4) * 512, (kg % 4) * 512 + W)
            t.emit("sync", lambda e, hq=hq, kg=kg, a=a, W=W: e.dma_start(
                out=hq, in_=s.s_hq[kg * 128:(kg + 1) * 128, a:a + W]),
                reads=[("s_hq", kg)], writes=hqk, dma=True)
            for mj in range(NPB):
                p, rp = pouts[mj]
                t.emit("tensor", lambda e, p=p, kg=kg, mj=mj, hq=hq, W=W: e.matmul(
                    p[:, 0:W], s.b_wt[:, (kg % 4) * 1024 + mj * 128:
                                      (kg % 4) * 1024 + mj * 128 + 128],
                    hq, start=(kg == 0), stop=(kg == NQB - 1)),
                    reads=[("wt", kg % 4)] + hqk, writes=rp)
        for mj in range(NPB):
            p, rp = pouts[mj]
            at, atk = bW2b(s, 4096, 4096 + W)
            t.emit("sync", lambda e, at=at, mj=mj, a=a, W=W: e.dma_start(
                out=at, in_=s.a2aY_out[mj * 128:(mj + 1) * 128, a:a + W]),
                reads=[("a2aY_out",)], writes=atk, dma=True)
            oa, ok_ = bW(s, 0, W)
            t.emit("vector", lambda e, oa=oa, p=p, at=at, W=W: e.scalar_tensor_tensor(
                oa, p[:, 0:W], 1.0 / (127.0 * 127.0), at, ALU.mult, ALU.add),
                reads=rp + atk, writes=ok_)
            t.emit("vector", lambda e, oa=oa, mj=mj, a=a, W=W: e.tensor_tensor(
                oa, oa, s.b_x1[:, mj * XW + 1 + a:mj * XW + 1 + a + W], ALU.add),
                reads=ok_ + rk("x1", mj * XW + 1 + a, mj * XW + 1 + b), writes=ok_)
            t.emit("gpsimd", lambda e, oa=oa, mj=mj, a=a, W=W: e.dma_start(
                out=s.out[mj * 128:(mj + 1) * 128, a:a + W], in_=oa),
                reads=ok_, writes=[("outw", mj)], dma=True)
    t.eng["gpsimd"].wait_ge(t.sem[("gpsimd", "d")], t.cnt[("gpsimd", "d")])
    t.eng["sync"].wait_ge(t.sem[("sync", "d")], t.cnt[("sync", "d")])


from concourse.bass_utils import run_bass_kernel_spmd


def kernel(**inputs):
    cfg = full_cfg()
    NCR = cfg["NCORES"]
    TC = cfg["TC"]; D = cfg["D"]
    nc = build(cfg, upto="all")
    in_maps = [host_layout(cfg, inputs, c) for c in range(NCR)]
    res = run_bass_kernel_spmd(nc, in_maps, core_ids=list(range(NCR)))
    out = np.zeros((1, NCR * TC, D), np.float32)
    for c in range(NCR):
        out[0, c * TC:(c + 1) * TC] = np.asarray(res.results[c]["out"],
                                                 np.float32).T
    return out
